# version 08
# baseline (speedup 1.0000x reference)
import numpy as np
from contextlib import ExitStack

import concourse.bass as bass
import concourse.mybir as mybir
from concourse.bass_utils import run_bass_kernel_spmd

F32 = mybir.dt.float32
BF16 = mybir.dt.bfloat16
AF = mybir.ActivationFunctionType
ALU = mybir.AluOpType
AX = mybir.AxisListType

T = 4096
D = 1024
H = 4096
NEXP = 9
N16 = 16
BT = 1024
NBLK = T // BT
NTT = BT // 128
NQ = 4
QH = H // NQ
TC = 32
NCH = BT // TC
GRP = 2
RING = 8
ENGINES = ("sync", "gpsimd", "tensor", "vector", "scalar")

K1DT = 0.5 / (0.0495 + 1e-8)
K2DT = 0.0495 / 0.5
CDT = 0.0505


class SemT:
    def __init__(self, h):
        self.h = h
        self.n = 0


class Prog:
    def __init__(self):
        self.events = []

    def set_selfsems(self, m):
        self._selfsems = m

    def emit(self, engine, fn, waits=(), incs=()):
        waits = list(waits)
        incs = list(incs)
        ent = getattr(self, "_selfsems", {}).get(engine)
        if ent is not None:
            gen, do_wait = ent
            assert not incs
            incs = [(gen, 1)]
            if do_wait:
                waits.append((gen, gen.n))
        post = []
        for s, k in incs:
            s.n += k
            post.append((s, s.n))
        assert len(incs) <= 1
        self.events.append((engine, fn, waits, incs))
        return post

    def replay(self, engine, eng):
        for (e, fn, waits, incs) in self.events:
            if e != engine:
                continue
            for (s, v) in waits:
                eng.wait_ge(s.h, v)
            inst = fn(eng)
            for (s, k) in incs:
                if inst is None:
                    eng.sem_inc(s.h, k)
                else:
                    inst.then_inc(s.h, k)


def after(tok):
    return [(s, v) for (s, v) in tok]


def maxwaits(toks):
    best = {}
    for (s, v) in toks:
        if id(s) not in best or best[id(s)][1] < v:
            best[id(s)] = (s, v)
    return list(best.values())


def merge(gens):
    clocks = [0.0] * len(gens)
    alive = [True] * len(gens)
    while any(alive):
        i = min((j for j in range(len(gens)) if alive[j]), key=lambda j: clocks[j])
        try:
            c = next(gens[i])
            clocks[i] += (c or 0.0)
        except StopIteration:
            alive[i] = False


def drain(gen):
    for _ in gen:
        pass


def build_program(phases=("ffn1", "ssm", "ffn2")):
    nc = bass.Bass("TRN2", target_bir_lowering=False)

    xin = nc.dram_tensor("x", [T, D], F32, kind="ExternalInput").ap()
    Wdt = nc.dram_tensor("Wdt", [D, D], F32, kind="ExternalInput").ap()
    bdt = nc.dram_tensor("bdt", [1, D], F32, kind="ExternalInput").ap()
    WB = nc.dram_tensor("WB", [N16, D], F32, kind="ExternalInput").ap()
    WC = nc.dram_tensor("WC", [N16, D], F32, kind="ExternalInput").ap()
    A_log = nc.dram_tensor("A_log", [D, N16], F32, kind="ExternalInput").ap()
    Dp = nc.dram_tensor("Dp", [1, D], F32, kind="ExternalInput").ap()
    nw = nc.dram_tensor("nw", [1, D], F32, kind="ExternalInput").ap()
    rw = nc.dram_tensor("rw", [8, D], F32, kind="ExternalInput").ap()
    W1 = nc.dram_tensor("W1", [NEXP, D, H], F32, kind="ExternalInput").ap()
    W2 = nc.dram_tensor("W2", [NEXP, H, D], F32, kind="ExternalInput").ap()
    b1 = nc.dram_tensor("b1", [NEXP, H], F32, kind="ExternalInput").ap()
    b2 = nc.dram_tensor("b2", [NEXP, D], F32, kind="ExternalInput").ap()
    ident_in = nc.dram_tensor("ident", [128, 128], F32, kind="ExternalInput").ap()

    out = nc.dram_tensor("out", [T, D], F32, kind="ExternalOutput").ap()
    xr1 = nc.dram_tensor("xr1", [T, D], F32, kind="Internal").ap()
    xr2 = nc.dram_tensor("xr2", [T, D], F32, kind="Internal").ap()
    bounce = nc.dram_tensor("bounce", [2, N16 * TC], BF16, kind="Internal").ap()
    w1bf = nc.dram_tensor("w1bf", [NEXP, NQ, 128, 8, QH], BF16, kind="Internal").ap()
    w2bf = nc.dram_tensor("w2bf", [NEXP, NQ, 128, 8, D], BF16, kind="Internal").ap()
    wdtd = nc.dram_tensor("wdtd", [8, 128, 8, 128], BF16, kind="Internal").ap()
    zcd = nc.dram_tensor("zcd", [NCH, 128, TC, 128], BF16, kind="Internal").ap()

    prog = Prog()
    ctx = ExitStack()

    def sbuf(name, shape, dt):
        return ctx.enter_context(nc.sbuf_tensor(name, shape, dt))

    def psum(name, shape, dt=F32):
        return ctx.enter_context(nc.psum_tensor(name, shape, dt))

    def sem(name):
        return SemT(ctx.enter_context(nc.semaphore(name)))

    with ctx:
        identf = sbuf("identf", [128, 128], F32)
        ones_row = sbuf("ones_row", [1, 128], F32)
        ones_bf = sbuf("ones_bf", [1, 128], BF16)
        o400 = sbuf("o400", [128, 128], BF16)
        one_col = sbuf("one_col", [128, 1], F32)
        tanh_bias = sbuf("tanh_bias", [128, 1], F32)
        rwTf = sbuf("rwTf", [128, 8, 8], F32)
        nwT = sbuf("nwT", [128, 8], F32)
        eb1T = sbuf("eb1T", [128, 32, NEXP], F32)
        eb2x = sbuf("eb2x", [NEXP, D], BF16)
        a1t = sbuf("a1t", [128, 128], F32)
        dpT = sbuf("dpT", [128, 8], F32)
        bdtT = sbuf("bdtT", [128, 8], F32)

        s_pe = sem("s_pe")
        s_v = sem("s_v")
        s_sc = sem("s_sc")
        s_ld = sem("s_ld")
        s_ldx = [sem("s_ldx0"), sem("s_ldx1")]
        s_sxt = sem("s_sxt")
        s_ldw = [sem("s_ldw0"), sem("s_ldw1"), sem("s_ldw2"), sem("s_ldw3")]
        s_wdt = sem("s_wdt")
        s_wdl = sem("s_wdl")
        s_zcs = [sem("s_zcs0"), sem("s_zcs1")]
        s_zcl = [sem("s_zcl0"), sem("s_zcl1")]
        s_st = [sem("s_st0"), sem("s_st1")]
        s_sst = sem("s_sst")
        s_bnc = sem("s_bnc")
        s_pw = [sem(f"s_pw{i}") for i in range(4)]
        s_ps = [sem(f"s_ps{i}") for i in range(4)]
        prog.set_selfsems({"vector": (s_v, True), "scalar": (s_sc, True),
                           "tensor": (s_pe, False)})

        ph = [psum("ph0", [128, 512]), psum("ph1", [128, 512])]
        po = [psum("po0", [128, 512]), psum("po1", [128, 512])]
        pns = [psum("pns0", [128, 512]), psum("pns1", [128, 512])]
        pmisc = psum("pmisc", [128, 512])
        pk1 = psum("pk1", [128, 512])
        pr8 = po[0][:, 0:8]
        pt9 = po[1][0:9, 0:128]

        p0 = ExitStack()
        stg = p0.enter_context(nc.sbuf_tensor("p0stage", [128, 1024], F32))
        stgw = p0.enter_context(nc.sbuf_tensor("p0stagew", [16, 1024], F32))
        stgb = p0.enter_context(nc.sbuf_tensor("p0stageb", [128, 1024], BF16))
        stgb1 = p0.enter_context(nc.sbuf_tensor("p0stgb1", [9, 4096], F32))

        t_id = prog.emit("sync", lambda e: e.dma_start(out=identf[:], in_=ident_in[:, :]),
                         waits=[(s_ld, s_ld.n)], incs=[(s_ld, 16)])
        prog.emit("vector", lambda e: e.memset(ones_row[:], 1.0))
        prog.emit("vector", lambda e: e.memset(ones_bf[:], 1.0))
        prog.emit("vector", lambda e: e.memset(o400[:], 1.0 / 400.0))
        prog.emit("vector", lambda e: e.memset(one_col[:], 1.0))
        t_ms = prog.emit("vector", lambda e: e.memset(tanh_bias[:], -CDT * K1DT))

        def row_to_dmajor(src_row, dst):
            tl = prog.emit("sync", lambda e: e.dma_start(
                out=stgw[0:8, 0:128],
                in_=src_row.rearrange("o (c p) -> (o c) p", c=8)),
                waits=[(s_ld, s_ld.n), (s_pe, s_pe.n)], incs=[(s_ld, 16)])
            tp = prog.emit("tensor", lambda e: e.transpose(
                pmisc[:, 0:8], stgw[0:8, 0:128], identf[0:8, 0:8]),
                waits=after(tl) + after(t_id) + [(s_v, s_v.n)])
            return prog.emit("vector", lambda e: e.tensor_copy(out=dst, in_=pmisc[:, 0:8]),
                             waits=after(tp))

        tok_nw = row_to_dmajor(nw, nwT[:, :])
        tok_dp = row_to_dmajor(Dp, dpT[:, :])
        tok_bdt = row_to_dmajor(bdt, bdtT[:, :])

        tl = prog.emit("sync", lambda e: e.dma_start(
            out=a1t[:].rearrange("p (c n) -> p c n", c=8),
            in_=A_log.rearrange("(c p) n -> p c n", p=128)),
            waits=[(s_ld, s_ld.n)], incs=[(s_ld, 16)])
        tsa = prog.emit("scalar", lambda e: e.activation(
            out=a1t[:], in_=a1t[:], func=AF.Tanh, scale=0.1),
            waits=after(tl))
        tsb = prog.emit("scalar", lambda e: e.activation(
            out=a1t[:], in_=a1t[:], func=AF.Exp, scale=10.0),
            waits=after(tsa))
        tok_a1 = prog.emit("vector", lambda e: e.tensor_scalar_mul(a1t[:], a1t[:], -1.0),
                           waits=after(tsb))

        tok_rwT = []
        tl = prog.emit("sync", lambda e: e.dma_start(out=stgw[0:8, :], in_=rw[:, :]),
                       waits=[(s_ld, s_ld.n), (s_pe, s_pe.n)], incs=[(s_ld, 16)])
        for cd in range(8):
            tp = prog.emit("tensor", lambda e, cd=cd: e.transpose(
                pmisc[:, 0:8], stgw[0:8, cd * 128:(cd + 1) * 128], identf[0:8, 0:8]),
                waits=after(tl) + after(tok_nw) + [(s_v, s_v.n)])
            tok_rwT.append(prog.emit("vector", lambda e, cd=cd: e.tensor_scalar(
                out=rwTf[:, cd, :], in0=pmisc[:, 0:8],
                scalar1=nwT[:, cd:cd + 1], scalar2=None, op0=ALU.mult),
                waits=after(tp)))

        tl_b1 = prog.emit("sync", lambda e: e.dma_start(out=stgb1[:, :], in_=b1[:, :]),
                          waits=[(s_ld, s_ld.n), (s_pe, s_pe.n)], incs=[(s_ld, 16)])
        tok_eb1 = []
        for ch in range(32):
            tp = prog.emit("tensor", lambda e, ch=ch: e.transpose(
                pmisc[:, 0:9], stgb1[0:9, ch * 128:(ch + 1) * 128], identf[0:9, 0:9]),
                waits=after(tl_b1) + [(s_v, s_v.n)])
            tok_eb1.append(prog.emit("vector", lambda e, ch=ch: e.tensor_copy(
                out=eb1T[:, ch, :], in_=pmisc[:, 0:9]),
                waits=after(tp)))
        tl_b2 = prog.emit("sync", lambda e: e.dma_start(out=stgw[0:9, :], in_=b2[:, :]),
                          waits=[(s_ld, s_ld.n), (s_pe, s_pe.n), (s_v, s_v.n)],
                          incs=[(s_ld, 16)])
        tok_eb2 = prog.emit("vector", lambda e: e.tensor_copy(out=eb2x[:, :], in_=stgw[0:9, :]),
                            waits=after(tl_b2))

        wdt_st_tok = []
        for co in range(8):
            tl = prog.emit("sync", lambda e, co=co: e.dma_start(
                out=stg[:], in_=Wdt[co * 128:(co + 1) * 128, :]),
                waits=[(s_ld, s_ld.n), (s_v, s_v.n), (s_pe, s_pe.n)],
                incs=[(s_ld, 16)])
            tcps = []
            for cd in range(8):
                tp = prog.emit("tensor", lambda e, cd=cd: e.transpose(
                    pmisc[:, 0:128], stg[:, cd * 128:(cd + 1) * 128], identf[:]),
                    waits=after(tl) + [(s_v, s_v.n)])
                tcps.append(prog.emit("vector", lambda e, cd=cd: e.tensor_copy(
                    out=stgb[:, cd * 128:(cd + 1) * 128], in_=pmisc[:, 0:128]),
                    waits=after(tp) + (wdt_st_tok if cd == 0 else [])))
            wdt_st_tok = after(prog.emit("gpsimd", lambda e, co=co: e.dma_start(
                out=wdtd[co].rearrange("p c d -> p (c d)"), in_=stgb[:]),
                waits=after(tcps[-1]), incs=[(s_wdt, 16)]))
        wdt_done = s_wdt.n

        pstg = [p0.enter_context(nc.sbuf_tensor(f"pstg{i}", [128, 4, 1024], F32)) for i in range(4)]
        pcv = [p0.enter_context(nc.sbuf_tensor(f"pcv{i}", [128, 4, 1024], BF16)) for i in range(4)]
        last_conv = {}
        last_store = {}
        for sl in range(NEXP):
            for q in range(NQ):
                for g4 in range(2):
                    chn = g4 % 2
                    wld = prog.emit("sync", lambda e, sl=sl, q=q, g4=g4, chn=chn: e.dma_start(
                        out=pstg[chn][:],
                        in_=W1[sl][g4 * 512:(g4 + 1) * 512, q * QH:(q + 1) * QH]
                            .rearrange("(a p) h -> p a h", p=128)),
                        waits=last_conv.get(chn, []), incs=[(s_pw[chn], 16)])
                    w_ = after(wld) + last_store.get(chn, [])
                    if sl == 0 and q == 0 and g4 == 0:
                        w_ += after(tok_nw)
                    tcv = None
                    for a in range(4):
                        tcv = prog.emit("vector", lambda e, g4=g4, a=a, chn=chn: e.tensor_scalar(
                            out=pcv[chn][:, a, :], in0=pstg[chn][:, a, :],
                            scalar1=nwT[:, g4 * 4 + a:g4 * 4 + a + 1],
                            scalar2=None, op0=ALU.mult),
                            waits=w_ if a == 0 else ())
                    last_conv[chn] = after(tcv)
                    tst = prog.emit("gpsimd", lambda e, sl=sl, q=q, g4=g4, chn=chn: e.dma_start(
                        out=w1bf[sl, q, :, g4 * 4:(g4 + 1) * 4, :], in_=pcv[chn][:]),
                        waits=after(tcv), incs=[(s_ps[chn], 16)])
                    last_store[chn] = after(tst)
                for g4 in range(2):
                    chn = 2 + g4 % 2
                    wld = prog.emit("sync", lambda e, sl=sl, q=q, g4=g4, chn=chn: e.dma_start(
                        out=pstg[chn][:],
                        in_=W2[sl][q * QH + g4 * 512:q * QH + (g4 + 1) * 512, :]
                            .rearrange("(a p) h -> p a h", p=128)),
                        waits=last_conv.get(chn, []), incs=[(s_pw[chn], 16)])
                    tcv = prog.emit("scalar", lambda e, chn=chn: e.activation(
                        out=pcv[chn][:].rearrange("p a h -> p (a h)"),
                        in_=pstg[chn][:].rearrange("p a h -> p (a h)"), func=AF.Copy),
                        waits=after(wld) + last_store.get(chn, []))
                    last_conv[chn] = after(tcv)
                    tst = prog.emit("gpsimd", lambda e, sl=sl, q=q, g4=g4, chn=chn: e.dma_start(
                        out=w2bf[sl, q, :, g4 * 4:(g4 + 1) * 4, :], in_=pcv[chn][:]),
                        waits=after(tcv), incs=[(s_ps[chn], 16)])
                    last_store[chn] = after(tst)

        allc = (after(tok_a1) + after(tok_dp) + after(tok_bdt) + after(tok_eb2)
                + [(s_wdt, wdt_done)]
                + [(s_ps[i], s_ps[i].n) for i in range(4)]
                + [(s_pw[i], s_pw[i].n) for i in range(4)]
                + [(s_v, s_v.n), (s_sc, s_sc.n), (s_pe, s_pe.n), (s_ld, s_ld.n)])
        for t_ in tok_rwT + tok_eb1:
            allc += after(t_)
        const_waits = maxwaits(allc)
        for _eng in ENGINES:
            prog.emit(_eng, lambda e: None, waits=list(const_waits))
        p0.close()

        nrmT = sbuf("nrmT", [128, 8, BT], BF16)
        nrmTf = sbuf("nrmTf", [128, 8, 128], F32)
        accb = sbuf("accb", [128, NTT, D], BF16)
        g = sbuf("gq", [128, 8, BT], BF16)
        w1q = sbuf("w1q", [128, 8, QH], BF16)
        w2q = sbuf("w2q", [128, 8, D], BF16)
        xt = [sbuf("xt0", [128, D], F32), sbuf("xt1", [128, D], F32)]
        nrm_s = sbuf("nrm_s", [128, D], F32)
        stats = sbuf("stats", [128, 12], F32)
        logit = sbuf("logit", [128, NTT, 8], F32)
        mx8 = sbuf("mx8", [128, NTT, 8], F32)
        s12 = sbuf("s12", [128, NTT, 2], F32)
        wts = sbuf("wts", [128, NTT, 4], F32)
        msk1 = sbuf("msk1", [128, NTT, 8], F32)
        msk2 = sbuf("msk2", [128, NTT, 8], F32)
        comb = sbuf("comb", [128, NTT, NEXP], F32)
        combT = sbuf("combT", [NEXP, BT], BF16)

        fst = {
            "xt_last": {},
            "evh": {},
            "evo": {},
            "pe_q": {},
            "qc": 0,
            "w1tok": None, "w2tok": None,
            "lnv": {},
        }

        def ffn_block(src, dst, blk, dep=()):
            dep = list(dep)
            base = blk * BT
            tok_tcp = None
            tok_tlg = None
            for tt in range(NTT):
                xb = xt[tt % 2]
                row0 = base + tt * 128
                w_ = list(fst["xt_last"].get(tt % 2, [])) + dep
                tld = prog.emit("gpsimd", lambda e, xb=xb, row0=row0: e.dma_start(
                    out=xb[:], in_=src[row0:row0 + 128, :]),
                    waits=w_, incs=[(s_ldx[tt % 2], 16)])
                tsum = prog.emit("vector", lambda e, xb=xb: e.reduce_sum(
                    out=stats[:, 0:1], in_=xb[:], axis=AX.X),
                    waits=after(tld))
                tsq = prog.emit("scalar", lambda e, xb=xb: e.activation(
                    out=nrm_s[:], in_=xb[:], func=AF.Square,
                    accum_out=stats[:, 1:2]),
                    waits=after(tld) + [(s_v, s_v.n)])
                tv = prog.emit("vector", lambda e: e.tensor_scalar_mul(
                    stats[:, 2:3], stats[:, 0:1], 1.0 / D), waits=after(tsum))
                tv = prog.emit("vector", lambda e: e.tensor_scalar_mul(
                    stats[:, 3:4], stats[:, 1:2], 1.0 / D), waits=after(tsq))
                tv = prog.emit("vector", lambda e: e.tensor_mul(
                    stats[:, 4:5], stats[:, 2:3], stats[:, 2:3]))
                tv = prog.emit("vector", lambda e: e.tensor_sub(
                    stats[:, 5:6], stats[:, 3:4], stats[:, 4:5]))
                tv = prog.emit("vector", lambda e: e.tensor_scalar_add(
                    stats[:, 6:7], stats[:, 5:6], 1e-5))
                tv = prog.emit("vector", lambda e: e.reciprocal(
                    out=stats[:, 7:8], in_=stats[:, 6:7]))
                tv = prog.emit("vector", lambda e: e.tensor_mul(
                    stats[:, 8:9], stats[:, 5:6], stats[:, 7:8]))
                tv = prog.emit("vector", lambda e: e.tensor_scalar_add(
                    stats[:, 9:10], stats[:, 8:9], 1e-8))
                tv9 = prog.emit("vector", lambda e: e.tensor_mul(
                    stats[:, 10:11], stats[:, 6:7], stats[:, 9:10]))
                tl1 = prog.emit("scalar", lambda e: e.activation(
                    out=stats[:, 11:12], in_=stats[:, 10:11], func=AF.Ln),
                    waits=after(tv9))
                tl2 = prog.emit("scalar", lambda e: e.activation(
                    out=stats[:, 10:11], in_=stats[:, 11:12], func=AF.Exp,
                    scale=-0.5))
                tnrm = prog.emit("vector", lambda e, xb=xb: e.scalar_tensor_tensor(
                    out=nrm_s[:], in0=xb[:], scalar=stats[:, 2:3],
                    in1=stats[:, 10:11].to_broadcast([128, D]),
                    op0=ALU.subtract, op1=ALU.mult),
                    waits=after(tl2))
                fst["xt_last"][tt % 2] = after(tnrm) + after(tsq)
                ttr = {}
                for cdx in range(8):
                    pb = po[cdx // 4]
                    w_ = after(tnrm)
                    if cdx == 0:
                        for par in (0, 1):
                            if par in fst["evo"]:
                                w_.append((s_v, fst["evo"][par]))
                        if tok_tlg is not None:
                            w_ += after(tok_tlg)
                    ttr[cdx] = prog.emit("tensor", lambda e, cdx=cdx, pb=pb: e.transpose(
                        pb[:, (cdx % 4) * 128:(cdx % 4 + 1) * 128],
                        nrm_s[:, cdx * 128:(cdx + 1) * 128], identf[:]),
                        waits=w_)
                tok_tcp = prog.emit("vector", lambda e, tt=tt: e.tensor_copy(
                    out=nrmT[:, 0:4, tt * 128:(tt + 1) * 128],
                    in_=po[0][:].rearrange("p (c f) -> p c f", c=4)),
                    waits=after(ttr[3]))
                tok_tcp2 = prog.emit("vector", lambda e, tt=tt: e.tensor_copy(
                    out=nrmT[:, 4:8, tt * 128:(tt + 1) * 128],
                    in_=po[1][:].rearrange("p (c f) -> p c f", c=4)),
                    waits=after(ttr[7]))
                tcpf = prog.emit("vector", lambda e: e.tensor_copy(
                    out=nrmTf[:, 0:4, :],
                    in_=po[0][:].rearrange("p (c f) -> p c f", c=4)))
                tcpf2 = prog.emit("vector", lambda e: e.tensor_copy(
                    out=nrmTf[:, 4:8, :],
                    in_=po[1][:].rearrange("p (c f) -> p c f", c=4)))
                trt = None
                for cd in range(8):
                    w_ = after(tcpf2) if cd == 0 else ()
                    trt = prog.emit("tensor", lambda e, cd=cd: e.matmul(
                        pr8, nrmTf[:, cd, :], rwTf[:, cd, :],
                        start=(cd == 0), stop=(cd == 7)),
                        waits=w_)
                tok_tlg = prog.emit("vector", lambda e, tt=tt: e.tensor_copy(
                    out=logit[:, tt, :], in_=pr8),
                    waits=after(trt))
                prog.emit("vector", lambda e, tt=tt: e.max(
                    out=mx8[:, tt, :], in_=logit[:, tt, :]))
                fst["evo"][0] = s_v.n
                fst["evo"][1] = s_v.n
                yield 2600.0

            te = prog.emit("scalar", lambda e: e.activation(
                out=s12[:, :, :], in_=mx8[:, :, 0:2], func=AF.Exp, scale=-1.0),
                waits=[(s_v, s_v.n)])
            tv = prog.emit("vector", lambda e: e.tensor_scalar_add(
                s12[:, :, :], s12[:, :, :], 1.0), waits=after(te))
            tv = prog.emit("vector", lambda e: e.reciprocal(
                out=s12[:, :, :], in_=s12[:, :, :]))
            tv = prog.emit("vector", lambda e: e.tensor_add(
                wts[:, :, 0], s12[:, :, 0], s12[:, :, 1]))
            tv = prog.emit("vector", lambda e: e.tensor_scalar_add(
                wts[:, :, 0], wts[:, :, 0], 1e-6))
            tv = prog.emit("vector", lambda e: e.reciprocal(
                out=wts[:, :, 1], in_=wts[:, :, 0]))
            tv = prog.emit("vector", lambda e: e.tensor_mul(
                wts[:, :, 2], s12[:, :, 0], wts[:, :, 1]))
            tv = prog.emit("vector", lambda e: e.tensor_mul(
                wts[:, :, 3], s12[:, :, 1], wts[:, :, 1]))
            tv = prog.emit("vector", lambda e: e.tensor_tensor(
                out=msk1[:], in0=logit[:],
                in1=mx8[:, :, 0:1].to_broadcast([128, NTT, 8]), op=ALU.is_ge))
            tv = prog.emit("vector", lambda e: e.tensor_tensor(
                out=msk2[:], in0=logit[:],
                in1=mx8[:, :, 1:2].to_broadcast([128, NTT, 8]), op=ALU.is_ge))
            tv = prog.emit("vector", lambda e: e.tensor_sub(
                msk2[:], msk2[:], msk1[:]))
            tv = prog.emit("vector", lambda e: e.tensor_tensor(
                out=msk1[:], in0=msk1[:],
                in1=wts[:, :, 2:3].to_broadcast([128, NTT, 8]), op=ALU.mult))
            tv = prog.emit("vector", lambda e: e.tensor_tensor(
                out=msk2[:], in0=msk2[:],
                in1=wts[:, :, 3:4].to_broadcast([128, NTT, 8]), op=ALU.mult))
            tv = prog.emit("vector", lambda e: e.tensor_add(
                comb[:, :, 0:8], msk1[:], msk2[:]))
            tcomb = prog.emit("vector", lambda e: e.memset(comb[:, :, 8:9], 1.0))
            yield 2000.0

            for tt in range(NTT):
                w_ = [(s_v, s_v.n)]
                tp = prog.emit("tensor", lambda e, tt=tt: e.transpose(
                    pt9, comb[:, tt, :], identf[:]), waits=w_)
                tcc = prog.emit("vector", lambda e, tt=tt: e.tensor_copy(
                    out=combT[:, tt * 128:(tt + 1) * 128], in_=pt9),
                    waits=after(tp))
                tbs = {}
                for hf in range(2):
                    w_ = after(tcc)
                    if hf == 0 and (0 in fst["evo"]):
                        w_.append((s_v, fst["evo"][0]))
                    if hf == 1 and (1 in fst["evo"]):
                        w_.append((s_v, fst["evo"][1]))
                    tbs[hf] = prog.emit("tensor", lambda e, tt=tt, hf=hf: e.matmul(
                        po[hf][:], combT[:, tt * 128:(tt + 1) * 128],
                        eb2x[:, hf * 512:(hf + 1) * 512],
                        start=True, stop=True),
                        waits=w_)
                for hf in range(2):
                    tev = prog.emit("vector", lambda e, tt=tt, hf=hf: e.tensor_copy(
                        out=accb[:, tt, hf * 512:(hf + 1) * 512], in_=po[hf][:]),
                        waits=after(tbs[hf]))
                    fst["evo"][hf] = s_v.n
                yield 1500.0

            for sl in range(NEXP):
                for q in range(NQ):
                    qc = fst["qc"]
                    fst["qc"] += 1
                    wpd = []
                    t_w1 = prog.emit("sync", lambda e, sl=sl, q=q: e.dma_start(
                        out=w1q[:], in_=w1bf[sl, q]),
                        waits=[(s_pe, fst["pe_q"].get(("mm1", qc - 1), 0))] + wpd,
                        incs=[(s_ldw[qc % 2], 16)])
                    t_w2 = prog.emit("sync", lambda e, sl=sl, q=q: e.dma_start(
                        out=w2q[:], in_=w2bf[sl, q]),
                        waits=[(s_pe, fst["pe_q"].get(("mm2", qc - 1), 0))],
                        incs=[(s_ldw[2 + qc % 2], 16)])
                    fst["w1tok"] = after(t_w1)
                    fst["w2tok"] = after(t_w2)

                    for i in range(8):
                        for t2 in range(2):
                            par = (i * 2 + t2) % 2
                            hb = ph[par]
                            mmlast = None
                            for c in range(8):
                                w_ = []
                                if c == 0:
                                    if i == 0 and t2 == 0:
                                        w_ += fst["w1tok"]
                                    if ("h", par) in fst["evh"]:
                                        w_.append((s_sc, fst["evh"][("h", par)]))
                                mmlast = prog.emit("tensor",
                                    lambda e, i=i, c=c, t2=t2, hb=hb: e.matmul(
                                        hb[:],
                                        w1q[:, c, i * 128:(i + 1) * 128],
                                        nrmT[:, c, t2 * 512:(t2 + 1) * 512],
                                        start=(c == 0), stop=(c == 7)),
                                    waits=w_)
                                if c == 3:
                                    yield 900.0
                            prog.emit("scalar", lambda e, i=i, t2=t2, hb=hb, q=q, sl=sl: e.activation(
                                out=g[:, i, t2 * 512:(t2 + 1) * 512],
                                in_=hb[:], func=AF.Relu,
                                bias=eb1T[:, q * 8 + i, sl:sl + 1]),
                                waits=after(mmlast))
                            fst["evh"][("h", par)] = s_sc.n
                            yield 1100.0
                    fst["pe_q"][("mm1", qc)] = s_pe.n
                    g_ready = s_sc.n

                    for tt in range(NTT):
                        for hf in range(2):
                            par = (tt * 2 + hf) % 2
                            ob = po[par]
                            mmlast = None
                            for i in range(8):
                                w_ = []
                                if i == 0:
                                    if tt == 0 and hf == 0:
                                        w_ = [(s_sc, g_ready)] + fst["w2tok"]
                                    if par in fst["evo"]:
                                        w_.append((s_v, fst["evo"][par]))
                                mmlast = prog.emit("tensor",
                                    lambda e, tt=tt, i=i, hf=hf, ob=ob: e.matmul(
                                        ob[:],
                                        g[:, i, tt * 128:(tt + 1) * 128],
                                        w2q[:, i, hf * 512:(hf + 1) * 512],
                                        start=(i == 0), stop=(i == 7)),
                                    waits=w_)
                                if i == 3:
                                    yield 900.0
                            prog.emit("vector", lambda e, tt=tt, hf=hf, ob=ob, sl=sl: e.scalar_tensor_tensor(
                                out=accb[:, tt, hf * 512:(hf + 1) * 512], in0=ob[:],
                                scalar=comb[:, tt, sl:sl + 1], op0=ALU.mult,
                                in1=accb[:, tt, hf * 512:(hf + 1) * 512], op1=ALU.add),
                                waits=after(mmlast))
                            fst["evo"][par] = s_v.n
                            yield 1100.0
                    fst["pe_q"][("mm2", qc)] = s_pe.n

            acc_done = s_v.n
            for tt in range(NTT):
                xb = xt[tt % 2]
                row0 = base + tt * 128
                w_ = list(fst["xt_last"].get(tt % 2, [])) + dep
                tld = prog.emit("gpsimd", lambda e, xb=xb, row0=row0: e.dma_start(
                    out=xb[:], in_=src[row0:row0 + 128, :]),
                    waits=w_, incs=[(s_ldx[tt % 2], 16)])
                tap = prog.emit("vector", lambda e, xb=xb, tt=tt: e.scalar_tensor_tensor(
                    out=xb[:], in0=accb[:, tt, :], scalar=0.5,
                    op0=ALU.mult, in1=xb[:], op1=ALU.add),
                    waits=after(tld) + ([(s_v, acc_done)] if tt == 0 else []))
                tstr = prog.emit("gpsimd", lambda e, xb=xb, row0=row0: e.dma_start(
                    out=dst[row0:row0 + 128, :], in_=xb[:]),
                    waits=after(tap), incs=[(s_st[tt % 2], 16)])
                fst["xt_last"][tt % 2] = after(tap) + after(tstr)
                yield 800.0
            fst.setdefault("blk_store", {})[blk] = [
                (s_st[0], s_st[0].n), (s_st[1], s_st[1].n)]

        xnT = sbuf("xnT", [128, 8, BT], BF16)
        dtb = sbuf("dtb", [128, 8, BT], BF16)
        zcl = [sbuf("zcl0", [128, TC, 128], BF16), sbuf("zcl1", [128, TC, 128], BF16)]
        zs = sbuf("zs", [128, 8, 128], F32)
        dAs = [sbuf("dAs0", [128, 128], F32), sbuf("dAs1", [128, 128], F32)]
        ub = sbuf("ub", [128, TC, 128], BF16)
        hist = sbuf("hist", [128, RING, 128], F32)
        dtx = sbuf("dtx", [128, 8, TC], F32)
        qb = [sbuf("qb0", [128, 8, TC], F32), sbuf("qb1", [128, 8, TC], F32)]
        wred = sbuf("wred", [128, 8, 8], F32)
        yb = sbuf("yb", [128, 8, TC], F32)
        Bn = sbuf("Bn", [16, BT], BF16)
        Cn = sbuf("Cn", [16, BT], BF16)
        Brep = [sbuf("Brep0", [128, N16, TC], F32), sbuf("Brep1", [128, N16, TC], F32)]
        Crep = [sbuf("Crep0", [128, N16, TC], F32), sbuf("Crep1", [128, N16, TC], F32)]
        brow = sbuf("brow", [1, 2, N16 * TC], BF16)
        shist2 = sbuf("shist2", [128, TC + 1], F32)
        sq_s = sbuf("sq_s", [128, 128], F32)
        p2 = sbuf("p2", [128, 1], F32)
        p2b = sbuf("p2b", [128, 1], BF16)
        s0 = sbuf("s0", [128, 2], F32)
        xrow = sbuf("xrow", [TC, D], F32)
        sxt = sbuf("sxt", [128, D], F32)
        snrm = sbuf("snrm", [128, D], F32)
        sstats = sbuf("sstats", [128, 12], F32)
        tmp = sbuf("stmp", [128, 128], F32)
        wdtr = [sbuf("wdtr0", [128, 8, 128], BF16)]

        sst = {
            "init": False,
            "sxt_last": [],
            "xrow_last": [],
            "pdt_last": {},
            "bc_last": {},
            "bnc_ch": [],
            "wdt_last": {},
            "pk1_last": [],
            "pns_last": {},
            "lnp_last": [],
            "dA_tok": {},
        }

        def ssm_init():
            prog.emit("vector", lambda e: e.memset(hist[:, RING - 1, :], 0.0))
            prog.emit("vector", lambda e: e.memset(shist2[:, 0:1], 1.0))
            sst["init"] = True

        def ssm_block(src, dst, blk, dep=()):
            dep = list(dep)
            if not sst["init"]:
                ssm_init()
            base = blk * BT
            for tt in range(NTT):
                row0 = base + tt * 128
                tld = prog.emit("gpsimd", lambda e, row0=row0: e.dma_start(
                    out=sxt[:], in_=src[row0:row0 + 128, :]),
                    waits=list(sst["sxt_last"]) + dep, incs=[(s_sxt, 16)])
                tsum = prog.emit("vector", lambda e: e.reduce_sum(
                    out=sstats[:, 0:1], in_=sxt[:], axis=AX.X),
                    waits=after(tld))
                tsq = prog.emit("scalar", lambda e: e.activation(
                    out=snrm[:], in_=sxt[:], func=AF.Square,
                    accum_out=sstats[:, 1:2]),
                    waits=after(tld) + [(s_v, s_v.n)])
                tv = prog.emit("vector", lambda e: e.tensor_scalar_mul(
                    sstats[:, 2:3], sstats[:, 0:1], 1.0 / D), waits=after(tsum))
                tv = prog.emit("vector", lambda e: e.tensor_scalar_mul(
                    sstats[:, 3:4], sstats[:, 1:2], 1.0 / D), waits=after(tsq))
                tv = prog.emit("vector", lambda e: e.tensor_mul(
                    sstats[:, 4:5], sstats[:, 2:3], sstats[:, 2:3]))
                tv = prog.emit("vector", lambda e: e.tensor_sub(
                    sstats[:, 5:6], sstats[:, 3:4], sstats[:, 4:5]))
                tv5 = prog.emit("vector", lambda e: e.tensor_scalar_add(
                    sstats[:, 6:7], sstats[:, 5:6], 1e-5))
                tl1 = prog.emit("scalar", lambda e: e.activation(
                    out=sstats[:, 11:12], in_=sstats[:, 6:7], func=AF.Ln),
                    waits=after(tv5))
                tl2 = prog.emit("scalar", lambda e: e.activation(
                    out=sstats[:, 10:11], in_=sstats[:, 11:12], func=AF.Exp,
                    scale=-0.5))
                tnrm = prog.emit("vector", lambda e: e.scalar_tensor_tensor(
                    out=snrm[:], in0=sxt[:], scalar=sstats[:, 2:3],
                    in1=sstats[:, 10:11].to_broadcast([128, D]),
                    op0=ALU.subtract, op1=ALU.mult),
                    waits=after(tl2))
                sst["sxt_last"] = after(tnrm) + after(tsq)
                ttr = {}
                for cdx in range(4):
                    w_ = after(tnrm)
                    if cdx == 0:
                        w_ = list(w_) + sst["lnp_last"] + [(s_sc, sst["pdt_last"].get("any", 0))]
                    ttr[cdx] = prog.emit("tensor", lambda e, cdx=cdx: e.transpose(
                        pmisc[:, cdx * 128:(cdx + 1) * 128],
                        snrm[:, cdx * 128:(cdx + 1) * 128], identf[:]),
                        waits=w_)
                tc1 = prog.emit("vector", lambda e, tt=tt: e.tensor_copy(
                    out=xnT[:, 0:4, tt * 128:(tt + 1) * 128],
                    in_=pmisc[:].rearrange("p (c f) -> p c f", c=4)),
                    waits=after(ttr[3]))
                for cdx in range(4, 8):
                    w_ = after(tc1) if cdx == 4 else ()
                    ttr[cdx] = prog.emit("tensor", lambda e, cdx=cdx: e.transpose(
                        pmisc[:, (cdx % 4) * 128:(cdx % 4 + 1) * 128],
                        snrm[:, cdx * 128:(cdx + 1) * 128], identf[:]),
                        waits=w_)
                tc2 = prog.emit("vector", lambda e, tt=tt: e.tensor_copy(
                    out=xnT[:, 4:8, tt * 128:(tt + 1) * 128],
                    in_=pmisc[:].rearrange("p (c f) -> p c f", c=4)),
                    waits=after(ttr[7]))
                sst["lnp_last"] = after(tc2)
                yield 2600.0
            xn_v = s_v.n

            for co in range(8):
                wv = wdtr[0]
                twl = prog.emit("sync", lambda e, co=co, wv=wv: e.dma_start(
                    out=wv[:].rearrange("p c d -> p (c d)"),
                    in_=wdtd[co].rearrange("p c d -> p (c d)")),
                    waits=[(s_pe, sst["wdt_last"].get(0, 0)),
                           (s_wdl, s_wdl.n)],
                    incs=[(s_wdl, 16)])
                for tq in range(2):
                    mmlast = None
                    for cd in range(8):
                        w_ = []
                        if cd == 0:
                            w_ = [(s_v, xn_v)] + after(twl)
                            w_.append((s_sc, sst["pdt_last"].get("any", 0)))
                            w_ += sst["lnp_last"]
                        mmlast = prog.emit("tensor", lambda e, co=co, cd=cd, tq=tq, wv=wv: e.matmul(
                            pmisc[:], wv[:, cd, :],
                            xnT[:, cd, tq * 512:(tq + 1) * 512],
                            start=(cd == 0), stop=(cd == 7)),
                            waits=w_)
                    tso = prog.emit("scalar", lambda e, co=co: e.activation(
                        out=pmisc[:], in_=pmisc[:], func=AF.Exp,
                        bias=bdtT[:, co:co + 1]),
                        waits=after(mmlast))
                    tso2 = prog.emit("scalar", lambda e, co=co, tq=tq: e.activation(
                        out=dtb[:, co, tq * 512:(tq + 1) * 512], in_=pmisc[:],
                        func=AF.Ln, bias=one_col[:]),
                        waits=after(tso))
                    sst["pdt_last"]["any"] = s_sc.n
                    sst["lnp_last"] = []
                    yield 1800.0
                sst["wdt_last"][0] = s_pe.n

            for (wTs, dstN) in ((wbT_s, Bn), (wcT_s, Cn)):
                for tq in range(2):
                    mmlast = None
                    for cd in range(8):
                        w_ = []
                        if cd == 0:
                            w_ = [(s_sc, sst["pdt_last"]["any"]), (s_v, s_v.n)]
                            w_ += [(s_bnc, s_bnc.n)]
                        mmlast = prog.emit("tensor", lambda e, cd=cd, tq=tq, wTs=wTs: e.matmul(
                            pmisc[0:16, :], wTs[:, cd, :],
                            xnT[:, cd, tq * 512:(tq + 1) * 512],
                            start=(cd == 0), stop=(cd == 7)),
                            waits=w_)
                    tbc = prog.emit("vector", lambda e, tq=tq, dstN=dstN: e.tensor_copy(
                        out=dstN[:, tq * 512:(tq + 1) * 512], in_=pmisc[0:16, :]),
                        waits=after(mmlast))
                    sst["bc_last"]["any"] = s_v.n
                    yield 1200.0

            tth = prog.emit("scalar", lambda e: e.activation(
                out=dtb[:].rearrange("p c f -> p (c f)"),
                in_=dtb[:].rearrange("p c f -> p (c f)"),
                func=AF.Tanh, scale=K1DT, bias=tanh_bias[:]),
                waits=[(s_v, s_v.n)])
            tdt = prog.emit("vector", lambda e: e.tensor_scalar(
                out=dtb[:].rearrange("p c f -> p (c f)"),
                in0=dtb[:].rearrange("p c f -> p (c f)"),
                scalar1=K2DT, scalar2=CDT, op0=ALU.mult, op1=ALU.add),
                waits=after(tth))
            yield 8000.0
            for c in range(NCH):
                cpar = c % 2
                ttn = None
                for sl8 in range(TC // 8):
                    w_ = after(tdt) if (c == 0 and sl8 == 0) else []
                    tz = prog.emit("vector", lambda e, c=c, sl8=sl8: e.tensor_tensor(
                        out=zs[:].rearrange("p t (cc n) -> p t cc n", cc=8),
                        in0=dtb[:, :, c * TC + sl8 * 8:c * TC + (sl8 + 1) * 8]
                            .rearrange("p cc t -> p t cc")[:, :, :, None]
                            .to_broadcast([128, 8, 8, 16]),
                        in1=a1t[:].rearrange("p (cc n) -> p cc n", cc=8)[:, None, :, :]
                            .to_broadcast([128, 8, 8, 16]),
                        op=ALU.mult),
                        waits=w_ + [(s_sc, s_sc.n)])
                    w2_ = after(tz)
                    if sl8 == 0:
                        w2_ = w2_ + sst.get("zcs_war", {}).get(cpar, [])
                    ttn = prog.emit("scalar", lambda e, cpar=cpar, sl8=sl8: e.activation(
                        out=zcl[cpar][:, sl8 * 8:(sl8 + 1) * 8, :]
                            .rearrange("p t f -> p (t f)"),
                        in_=zs[:].rearrange("p t f -> p (t f)"),
                        func=AF.Tanh, scale=1.0 / 16.0),
                        waits=w2_)
                tzs = prog.emit("gpsimd", lambda e, c=c, cpar=cpar: e.dma_start(
                    out=zcd[c].rearrange("p t f -> p (t f)"),
                    in_=zcl[cpar][:].rearrange("p t f -> p (t f)")),
                    waits=after(ttn) + [(s_zcs[cpar], s_zcs[cpar].n)],
                    incs=[(s_zcs[cpar], 16)])
                sst.setdefault("zcs_war", {})[cpar] = after(tzs)
                yield 2200.0
            sst["zcl_tok"] = {}
            for c in (0, 1):
                cpar = c % 2
                tzl = prog.emit("sync", lambda e, c=c, cpar=cpar: e.dma_start(
                    out=zcl[cpar][:].rearrange("p t f -> p (t f)"),
                    in_=zcd[c].rearrange("p t f -> p (t f)")),
                    waits=[(s_zcs[cpar], s_zcs[cpar].n),
                           (s_zcl[cpar], s_zcl[cpar].n)],
                    incs=[(s_zcl[cpar], 16)])
                sst["zcl_tok"][cpar] = after(tzl)

            def prep_bounce(c):
                t0c = c * TC
                tb1 = prog.emit("sync", lambda e, t0c=t0c: e.dma_start(
                    out=bounce[0, :].rearrange("(n t) -> n t", n=16),
                    in_=Bn[:, t0c:t0c + TC]),
                    waits=[(s_bnc, s_bnc.n),
                           (s_v, sst["bc_last"].get("any", 0))]
                          + sst.get("brow_rd", []),
                    incs=[(s_bnc, 16)])
                tb2 = prog.emit("sync", lambda e, t0c=t0c: e.dma_start(
                    out=bounce[1, :].rearrange("(n t) -> n t", n=16),
                    in_=Cn[:, t0c:t0c + TC]),
                    waits=(), incs=[(s_bnc, 16)])
                tb3 = prog.emit("sync", lambda e: e.dma_start(
                    out=brow[:, :, :], in_=bounce[None, :, :]),
                    waits=maxwaits(after(tb1) + after(tb2)),
                    incs=[(s_bnc, 16)])
                sst["bounce_tok"] = after(tb3)

            def prep_brep(c):
                cpar = c % 2
                tk1 = prog.emit("tensor", lambda e: e.matmul(
                    pk1[:], ones_bf[:], brow[:, 0, :], start=True, stop=True),
                    waits=sst["bounce_tok"] + sst["pk1_last"])
                tkb = prog.emit("vector", lambda e, cpar=cpar: e.tensor_copy(
                    out=Brep[cpar][:].rearrange("p n t -> p (n t)"), in_=pk1[:]),
                    waits=after(tk1))
                sst["pk1_last"] = after(tkb)
                sst["brep_tok"] = after(tkb)

            def prep_crep(c):
                cpar = c % 2
                tk2 = prog.emit("tensor", lambda e: e.matmul(
                    pk1[:], ones_bf[:], brow[:, 1, :], start=True, stop=True),
                    waits=sst["bounce_tok"] + sst["pk1_last"])
                tkc = prog.emit("vector", lambda e, cpar=cpar: e.tensor_copy(
                    out=Crep[cpar][:].rearrange("p n t -> p (n t)"), in_=pk1[:]),
                    waits=after(tk2))
                sst["pk1_last"] = after(tkc)
                sst["brow_rd"] = after(tk2)

            def prep_dtx_qb(c):
                t0c = c * TC
                cpar = c % 2
                prog.emit("vector", lambda e, t0c=t0c: e.tensor_tensor(
                    out=dtx[:], in0=dtb[:, :, t0c:t0c + TC],
                    in1=xnT[:, :, t0c:t0c + TC], op=ALU.mult))
                prog.emit("vector", lambda e, t0c=t0c, cpar=cpar: e.tensor_tensor(
                    out=qb[cpar][:], in0=xnT[:, :, t0c:t0c + TC],
                    in1=dpT[:][:, :, None].to_broadcast([128, 8, TC]),
                    op=ALU.mult))

            def prep_uslice(c, j):
                cpar = c % 2
                prog.emit("vector", lambda e, j=j, cpar=cpar: e.tensor_tensor(
                    out=ub[:, j * 8:(j + 1) * 8, :]
                        .rearrange("p t (cc n) -> p t cc n", cc=8),
                    in0=dtx[:, :, j * 8:(j + 1) * 8]
                        .rearrange("p cc t -> p t cc")[:, :, :, None]
                        .to_broadcast([128, 8, 8, 16]),
                    in1=Brep[cpar][:, :, j * 8:(j + 1) * 8]
                        .rearrange("p n t -> p t n")[:, :, None, :]
                        .to_broadcast([128, 8, 8, 16]),
                    op=ALU.mult),
                    waits=sst["brep_tok"] if j == 0 else ())

            def prep_zcload(c):
                cpar = c % 2
                tzl = prog.emit("sync", lambda e, c=c, cpar=cpar: e.dma_start(
                    out=zcl[cpar][:].rearrange("p t f -> p (t f)"),
                    in_=zcd[c].rearrange("p t f -> p (t f)")),
                    waits=[(s_zcl[cpar], s_zcl[cpar].n),
                           (s_sc, sst.get("zc_rd", {}).get(cpar, 0))],
                    incs=[(s_zcl[cpar], 16)])
                sst["zcl_tok"][cpar] = after(tzl)

            for ch in range(NCH):
                par_c = ch % 2
                t0 = ch * TC
                gt0 = base + t0
                if ch == 0:
                    prep_bounce(0)
                    prep_brep(0)
                    prep_crep(0)
                    prep_dtx_qb(0)
                    for j in range(4):
                        prep_uslice(0, j)
                    yield 6000.0

                nxt = ch + 1
                extras = {}
                if nxt < NCH:
                    extras[1] = [lambda: prep_bounce(nxt)]
                    extras[3] = [lambda: prep_dtx_qb(nxt)]
                    extras[5] = [lambda: prep_brep(nxt)]
                    extras[9] = [lambda: prep_crep(nxt)]
                    for j in range(4):
                        extras.setdefault(7 + 8 * j, []).append(
                            lambda j=j: prep_uslice(nxt, j))

                tldx = prog.emit("gpsimd", lambda e, gt0=gt0: e.dma_start(
                    out=xrow[:], in_=src[gt0:gt0 + TC, :]),
                    waits=list(sst["xrow_last"]) + dep, incs=[(s_sst, 16)])

                dA_prev_tok = sst["dA_tok"]
                t_dap = prog.emit("scalar", lambda e, par_c=par_c: e.activation(
                    out=dAs[gt0 % 2][:],
                    in_=zcl[par_c][:, 0, :], func=AF.Exp, scale=16.0),
                    waits=[(s_v, dA_prev_tok.get(("use", gt0 % 2), 0))]
                          + sst["zcl_tok"].get(par_c, []))
                dA_prev_tok[("rdy", gt0 % 2)] = t_dap
                for ti in range(TC):
                    gstep = gt0 + ti
                    par = gstep % 2
                    rs = gstep % RING
                    prs = (gstep - 1) % RING
                    tha = prog.emit("vector", lambda e, prs=prs, par=par: e.tensor_tensor(
                        out=tmp[:], in0=hist[:, prs, :], in1=dAs[par][:],
                        op=ALU.mult),
                        waits=after(dA_prev_tok[("rdy", par)]))
                    dA_prev_tok[("use", par)] = s_v.n
                    if ti + 1 < TC:
                        t_dap = prog.emit("scalar", lambda e, ti=ti, par=par, par_c=par_c: e.activation(
                            out=dAs[1 - par][:],
                            in_=zcl[par_c][:, ti + 1, :], func=AF.Exp, scale=16.0),
                            waits=[(s_v, dA_prev_tok.get(("use", 1 - par), 0))])
                        dA_prev_tok[("rdy", 1 - par)] = t_dap
                    t1 = prog.emit("vector", lambda e, ti=ti, rs=rs: e.scalar_tensor_tensor(
                        out=hist[:, rs, :], in0=tmp[:],
                        scalar=shist2[:, ti:ti + 1],
                        op0=ALU.mult, in1=ub[:, ti, :], op1=ALU.add))
                    t3 = prog.emit("vector", lambda e, rs=rs: e.scalar_tensor_tensor(
                        out=sq_s[:], in0=hist[:, rs, :], scalar=1.0,
                        op0=ALU.mult, in1=hist[:, rs, :], op1=ALU.mult,
                        accum_out=p2[:]))
                    tcpb = prog.emit("vector", lambda e: e.tensor_copy(
                        out=p2b[:], in_=p2[:]))
                    t4 = prog.emit("tensor", lambda e, par=par: e.matmul(
                        pns[par][:, 0:1], o400[:], p2b[:], start=True, stop=True),
                        waits=after(tcpb) + [(s_sc, sst["pns_last"].get(par, 0))])
                    t5 = prog.emit("scalar", lambda e, par=par: e.activation(
                        out=s0[:, par:par + 1], in_=pns[par][:, 0:1], func=AF.Ln),
                        waits=after(t4))
                    t6 = prog.emit("scalar", lambda e, par=par: e.activation(
                        out=s0[:, par:par + 1], in_=s0[:, par:par + 1],
                        func=AF.Exp, scale=-0.5))
                    sst["pns_last"][par] = s_sc.n
                    prog.emit("vector", lambda e, ti=ti, par=par: e.tensor_scalar_min(
                        shist2[:, ti + 1:ti + 2], s0[:, par:par + 1], 1.0),
                        waits=after(t6))
                    if ti % 8 == 7:
                        k8 = ti // 8
                        tw1 = prog.emit("vector", lambda e, k8=k8, par_c=par_c: e.tensor_tensor(
                            out=zs[:].rearrange("p t (cc n) -> p t cc n", cc=8),
                            in0=hist[:].rearrange("p t (cc n) -> p t cc n", cc=8),
                            in1=Crep[par_c][:, :, k8 * 8:(k8 + 1) * 8]
                                .rearrange("p n t -> p t n")[:, :, None, :]
                                .to_broadcast([128, 8, 8, 16]),
                            op=ALU.mult))
                        tw2 = prog.emit("vector", lambda e: e.reduce_sum(
                            out=wred[:], in_=zs[:].rearrange("p t (cc n) -> p t cc n", cc=8),
                            axis=AX.X),
                            waits=after(tw1))
                        ty = prog.emit("vector", lambda e, k8=k8: e.tensor_tensor(
                            out=yb[:, :, k8 * 8:(k8 + 1) * 8].rearrange("p cc t -> p t cc"),
                            in0=wred[:],
                            in1=shist2[:, k8 * 8 + 1:(k8 + 1) * 8 + 1][:, :, None]
                                .to_broadcast([128, 8, 8]),
                            op=ALU.mult),
                            waits=after(tw2))
                        yield 1200.0
                    for fn_ in extras.get(ti, []):
                        fn_()
                    yield 2400.0
                sst.setdefault("zc_rd", {})[par_c] = s_sc.n
                if ch + 2 < NCH:
                    prep_zcload(ch + 2)

                ty2 = prog.emit("vector", lambda e, par_c=par_c: e.tensor_add(
                    yb[:], yb[:], qb[par_c][:]))
                prog.emit("vector", lambda e: e.tensor_copy(
                    out=shist2[:, 0:1], in_=shist2[:, TC:TC + 1]))

                txa = {}
                for half in range(2):
                    ttr = None
                    for cd4 in range(4):
                        cd = half * 4 + cd4
                        w_ = []
                        if cd4 == 0:
                            w_ = after(ty2) + (sst["pk1_last"] if half == 0 else after(txa[0]))
                        ttr = prog.emit("tensor", lambda e, cd=cd, cd4=cd4: e.transpose(
                            pk1[0:TC, cd4 * 128:(cd4 + 1) * 128], yb[:, cd, :],
                            identf[:]),
                            waits=w_)
                    txa[half] = prog.emit("vector", lambda e, half=half: e.tensor_add(
                        xrow[:, half * 512:(half + 1) * 512],
                        xrow[:, half * 512:(half + 1) * 512],
                        pk1[0:TC, :]),
                        waits=after(ttr) + (after(tldx) if half == 0 else []))
                sst["pk1_last"] = after(txa[1])
                tstr = prog.emit("gpsimd", lambda e, gt0=gt0: e.dma_start(
                    out=dst[gt0:gt0 + TC, :], in_=xrow[:]),
                    waits=after(txa[1]), incs=[(s_sst, 16)])
                sst["xrow_last"] = after(tstr)
                yield 1500.0
            sst.setdefault("blk_sst", {})[blk] = [(s_sst, s_sst.n)]

        wbT_s = sbuf("wbT_s", [128, 8, N16], BF16)
        wcT_s = sbuf("wcT_s", [128, 8, N16], BF16)
        for (wsrc, wdst) in ((WB, wbT_s), (WC, wcT_s)):
            tl = prog.emit("sync", lambda e, wsrc=wsrc: e.dma_start(
                out=xrow[0:16, :], in_=wsrc[:, :]),
                waits=[(s_ld, s_ld.n), (s_v, s_v.n)], incs=[(s_ld, 16)])
            for cd in range(8):
                tp = prog.emit("tensor", lambda e, cd=cd: e.transpose(
                    pmisc[:, 0:16], xrow[0:16, cd * 128:(cd + 1) * 128],
                    identf[0:16, 0:16]),
                    waits=after(tl) + [(s_v, s_v.n)])
                prog.emit("vector", lambda e, cd=cd, wdst=wdst: e.tensor_copy(
                    out=wdst[:, cd, :], in_=pmisc[:, 0:16]),
                    waits=after(tp))
        bc_const = [(s_v, s_v.n)]
        for _eng in ("tensor", "vector", "scalar", "sync", "gpsimd"):
            prog.emit(_eng, lambda e: None, waits=list(bc_const))

        use_ffn = "ffn1" in phases
        use_ssm = "ssm" in phases
        use_ffn2 = "ffn2" in phases
        src1, dst1 = xin, xr1
        src2 = xr1 if use_ffn else xin
        dst2 = xr2
        src3 = xr2 if use_ssm else src2
        dst3 = out

        def barrier():
            w = [(s_st[0], s_st[0].n), (s_st[1], s_st[1].n),
                 (s_sst, s_sst.n), (s_v, s_v.n), (s_sc, s_sc.n),
                 (s_pe, s_pe.n), (s_bnc, s_bnc.n), (s_sxt, s_sxt.n),
                 (s_wdt, s_wdt.n), (s_wdl, s_wdl.n)]
            w += [(s_ldx[i], s_ldx[i].n) for i in range(2)]
            w += [(s_ldw[i], s_ldw[i].n) for i in range(4)]
            for eng in ENGINES:
                prog.emit(eng, lambda e: None, waits=list(w))

        def stdep():
            return [(s_st[0], s_st[0].n), (s_st[1], s_st[1].n)]

        def sstdep():
            return [(s_sst, s_sst.n)]

        if use_ffn and use_ssm and use_ffn2:
            drain(ffn_block(src1, dst1, 0))
            drain(ffn_block(src1, dst1, 1))
            progress = {"ffn1": NBLK - 1, "ssm": -1}

            def f1_tail():
                for b in range(2, NBLK):
                    for c in ffn_block(src1, dst1, b):
                        yield c

            def ssm0_stream():
                for c in ssm_block(src2, dst2, 0, fst["blk_store"][0]):
                    yield c
                progress["ssm"] = 0

            merge([f1_tail(), ssm0_stream()])

            def ffn_stream():
                for b in range(NBLK):
                    while progress["ssm"] < b:
                        yield ("need", "ssm", b)
                    for c in ffn_block(src3, dst3, b, sst["blk_sst"][b]):
                        yield c

            def ssm_stream():
                for b in range(1, NBLK):
                    for c in ssm_block(src2, dst2, b, fst["blk_store"][b]):
                        yield c
                    progress["ssm"] = b

            g1, g2 = ffn_stream(), ssm_stream()
            clocks = [0.0, 0.0]
            alive = [True, True]
            blocked = [None, None]
            gens = [g1, g2]
            while any(alive):
                cands = []
                for j in range(2):
                    if not alive[j]:
                        continue
                    if blocked[j] is not None:
                        key, bb = blocked[j]
                        if progress[key] < bb:
                            continue
                        blocked[j] = None
                        clocks[j] = max(clocks)
                    cands.append(j)
                if not cands:
                    raise RuntimeError("scheduler deadlock")
                j = min(cands, key=lambda q: clocks[q])
                try:
                    c = next(gens[j])
                    if isinstance(c, tuple):
                        blocked[j] = (c[1], c[2])
                    else:
                        clocks[j] += (c or 0.0)
                except StopIteration:
                    alive[j] = False
        else:
            if use_ffn:
                for b in range(NBLK):
                    drain(ffn_block(src1, dst1, b))
                barrier()
            if use_ssm:
                for b in range(NBLK):
                    drain(ssm_block(src2, dst2, b, stdep()))
                barrier()
            if use_ffn2:
                for b in range(NBLK):
                    drain(ffn_block(src3, dst3, b, sstdep()))
            else:
                cp = sbuf("cpbuf", [128, D], F32)
                for r in range(0, T, 128):
                    tld = prog.emit("gpsimd", lambda e, r=r: e.dma_start(
                        out=cp[:], in_=src3[r:r + 128, :]),
                        waits=[(s_st[0], s_st[0].n)], incs=[(s_ldx[0], 16)])
                    prog.emit("gpsimd", lambda e, r=r: e.dma_start(
                        out=out[r:r + 128, :], in_=cp[:]),
                        waits=after(tld), incs=[(s_st[0], 16)])

        barrier()

        with nc.Block() as block:
            @block.sync
            def _(eng):
                prog.replay("sync", eng)

            @block.gpsimd
            def _(eng):
                prog.replay("gpsimd", eng)

            @block.tensor
            def _(eng):
                prog.replay("tensor", eng)

            @block.vector
            def _(eng):
                prog.replay("vector", eng)

            @block.scalar
            def _(eng):
                prog.replay("scalar", eng)

    return nc


_nc_cache = {}


def _get_nc(phases):
    key = tuple(phases)
    if key not in _nc_cache:
        _nc_cache[key] = build_program(phases)
    return _nc_cache[key]


def kernel(L0_attn_x, L0_attn_Wdt, L0_attn_bdt, L0_attn_WB, L0_attn_WC,
           L0_attn_A_log, L0_attn_Dp, L0_ffn_norm_weight, L0_ffn_router_w,
           L0_ffn_sw1, L0_ffn_sb1, L0_ffn_sw2, L0_ffn_sb2, L0_ffn_ew1,
           L0_ffn_eb1, L0_ffn_ew2, L0_ffn_eb2, _phases=("ffn1", "ssm", "ffn2")):
    x = np.ascontiguousarray(np.asarray(L0_attn_x, dtype=np.float32))
    W1 = np.concatenate([np.asarray(L0_ffn_ew1, dtype=np.float32),
                         np.asarray(L0_ffn_sw1, dtype=np.float32)[None]], axis=0)
    W2 = np.concatenate([np.asarray(L0_ffn_ew2, dtype=np.float32),
                         np.asarray(L0_ffn_sw2, dtype=np.float32)[None]], axis=0)
    b1c = np.concatenate([np.asarray(L0_ffn_eb1, dtype=np.float32),
                          np.asarray(L0_ffn_sb1, dtype=np.float32)[None]], axis=0)
    b2c = np.concatenate([np.asarray(L0_ffn_eb2, dtype=np.float32),
                          np.asarray(L0_ffn_sb2, dtype=np.float32)[None]], axis=0)
    common = {
        "Wdt": np.ascontiguousarray(L0_attn_Wdt, dtype=np.float32),
        "bdt": np.ascontiguousarray(np.asarray(L0_attn_bdt, dtype=np.float32).reshape(1, D)),
        "WB": np.ascontiguousarray(L0_attn_WB, dtype=np.float32),
        "WC": np.ascontiguousarray(L0_attn_WC, dtype=np.float32),
        "A_log": np.ascontiguousarray(L0_attn_A_log, dtype=np.float32),
        "Dp": np.ascontiguousarray(np.asarray(L0_attn_Dp, dtype=np.float32).reshape(1, D)),
        "nw": np.ascontiguousarray(np.asarray(L0_ffn_norm_weight, dtype=np.float32).reshape(1, D)),
        "rw": np.ascontiguousarray(L0_ffn_router_w, dtype=np.float32),
        "W1": np.ascontiguousarray(W1),
        "W2": np.ascontiguousarray(W2),
        "b1": np.ascontiguousarray(b1c),
        "b2": np.ascontiguousarray(b2c),
        "ident": np.eye(128, dtype=np.float32),
    }
    nc = _get_nc(_phases)
    in_maps = []
    for c in range(8):
        m = dict(common)
        m["x"] = np.ascontiguousarray(x[c])
        in_maps.append(m)
    res = run_bass_kernel_spmd(nc, in_maps, core_ids=list(range(8)))
    outs = [np.asarray(res.results[c]["out"]) for c in range(8)]
    return np.stack(outs, axis=0).astype(np.float32)


# version 09
# speedup vs baseline: 1.0701x; 1.0701x over previous
import numpy as np
from contextlib import ExitStack

import concourse.bass as bass
import concourse.mybir as mybir
from concourse.bass_utils import run_bass_kernel_spmd

F32 = mybir.dt.float32
BF16 = mybir.dt.bfloat16
AF = mybir.ActivationFunctionType
ALU = mybir.AluOpType
AX = mybir.AxisListType

T = 4096
D = 1024
H = 4096
NEXP = 9
N16 = 16
BT = 1024
NBLK = T // BT
NTT = BT // 128
NQ = 4
QH = H // NQ
TC = 32
NCH = BT // TC
GRP = 2
RING = 8
ENGINES = ("sync", "gpsimd", "tensor", "vector", "scalar")

K1DT = 0.5 / (0.0495 + 1e-8)
K2DT = 0.0495 / 0.5
CDT = 0.0505


class SemT:
    def __init__(self, h):
        self.h = h
        self.n = 0


class Prog:
    def __init__(self):
        self.events = []

    def set_selfsems(self, m):
        self._selfsems = m

    def emit(self, engine, fn, waits=(), incs=()):
        waits = list(waits)
        incs = list(incs)
        ent = getattr(self, "_selfsems", {}).get(engine)
        if ent is not None:
            gen, do_wait = ent
            assert not incs
            incs = [(gen, 1)]
            if do_wait:
                waits.append((gen, gen.n))
        post = []
        for s, k in incs:
            s.n += k
            post.append((s, s.n))
        assert len(incs) <= 1
        self.events.append((engine, fn, waits, incs))
        return post

    def replay(self, engine, eng):
        for (e, fn, waits, incs) in self.events:
            if e != engine:
                continue
            for (s, v) in waits:
                eng.wait_ge(s.h, v)
            inst = fn(eng)
            for (s, k) in incs:
                if inst is None:
                    eng.sem_inc(s.h, k)
                else:
                    inst.then_inc(s.h, k)


def after(tok):
    return [(s, v) for (s, v) in tok]


def maxwaits(toks):
    best = {}
    for (s, v) in toks:
        if id(s) not in best or best[id(s)][1] < v:
            best[id(s)] = (s, v)
    return list(best.values())


def merge(gens):
    clocks = [0.0] * len(gens)
    alive = [True] * len(gens)
    while any(alive):
        i = min((j for j in range(len(gens)) if alive[j]), key=lambda j: clocks[j])
        try:
            c = next(gens[i])
            clocks[i] += (c or 0.0)
        except StopIteration:
            alive[i] = False


def drain(gen):
    for _ in gen:
        pass


def build_program(phases=("ffn1", "ssm", "ffn2")):
    nc = bass.Bass("TRN2", target_bir_lowering=False)

    xin = nc.dram_tensor("x", [T, D], F32, kind="ExternalInput").ap()
    Wdt = nc.dram_tensor("Wdt", [D, D], F32, kind="ExternalInput").ap()
    bdt = nc.dram_tensor("bdt", [1, D], F32, kind="ExternalInput").ap()
    WB = nc.dram_tensor("WB", [N16, D], F32, kind="ExternalInput").ap()
    WC = nc.dram_tensor("WC", [N16, D], F32, kind="ExternalInput").ap()
    A_log = nc.dram_tensor("A_log", [D, N16], F32, kind="ExternalInput").ap()
    Dp = nc.dram_tensor("Dp", [1, D], F32, kind="ExternalInput").ap()
    nw = nc.dram_tensor("nw", [1, D], F32, kind="ExternalInput").ap()
    rw = nc.dram_tensor("rw", [8, D], F32, kind="ExternalInput").ap()
    W1 = nc.dram_tensor("W1", [NEXP, D, H], F32, kind="ExternalInput").ap()
    W2 = nc.dram_tensor("W2", [NEXP, H, D], F32, kind="ExternalInput").ap()
    b1 = nc.dram_tensor("b1", [NEXP, H], F32, kind="ExternalInput").ap()
    b2 = nc.dram_tensor("b2", [NEXP, D], F32, kind="ExternalInput").ap()
    ident_in = nc.dram_tensor("ident", [128, 128], F32, kind="ExternalInput").ap()

    out = nc.dram_tensor("out", [T, D], F32, kind="ExternalOutput").ap()
    xr1 = nc.dram_tensor("xr1", [T, D], F32, kind="Internal").ap()
    xr2 = nc.dram_tensor("xr2", [T, D], F32, kind="Internal").ap()
    bounce = nc.dram_tensor("bounce", [2, N16 * TC], BF16, kind="Internal").ap()
    w1bf = nc.dram_tensor("w1bf", [NEXP, NQ, 128, 8, QH], BF16, kind="Internal").ap()
    w2bf = nc.dram_tensor("w2bf", [NEXP, NQ, 128, 8, D], BF16, kind="Internal").ap()
    wdtd = nc.dram_tensor("wdtd", [8, 128, 8, 128], BF16, kind="Internal").ap()
    zcd = nc.dram_tensor("zcd", [NCH, 128, TC, 128], BF16, kind="Internal").ap()

    prog = Prog()
    ctx = ExitStack()

    def sbuf(name, shape, dt):
        return ctx.enter_context(nc.sbuf_tensor(name, shape, dt))

    def psum(name, shape, dt=F32):
        return ctx.enter_context(nc.psum_tensor(name, shape, dt))

    def sem(name):
        return SemT(ctx.enter_context(nc.semaphore(name)))

    with ctx:
        identf = sbuf("identf", [128, 128], F32)
        ones_row = sbuf("ones_row", [1, 128], F32)
        ones_bf = sbuf("ones_bf", [1, 128], BF16)
        o400 = sbuf("o400", [128, 128], BF16)
        one_col = sbuf("one_col", [128, 1], F32)
        tanh_bias = sbuf("tanh_bias", [128, 1], F32)
        rwTf = sbuf("rwTf", [128, 8, 8], F32)
        nwT = sbuf("nwT", [128, 8], F32)
        eb1T = sbuf("eb1T", [128, 32, NEXP], F32)
        eb2x = sbuf("eb2x", [NEXP, D], BF16)
        a1t = sbuf("a1t", [128, 128], F32)
        dpT = sbuf("dpT", [128, 8], F32)
        bdtT = sbuf("bdtT", [128, 8], F32)

        s_pe = sem("s_pe")
        s_v = sem("s_v")
        s_sc = sem("s_sc")
        s_ld = sem("s_ld")
        s_ldx = [sem("s_ldx0"), sem("s_ldx1")]
        s_sxt = sem("s_sxt")
        s_ldw = [sem("s_ldw0"), sem("s_ldw1"), sem("s_ldw2"), sem("s_ldw3")]
        s_wdt = sem("s_wdt")
        s_wdl = sem("s_wdl")
        s_zcs = [sem("s_zcs0"), sem("s_zcs1")]
        s_zcl = [sem("s_zcl0"), sem("s_zcl1")]
        s_st = [sem("s_st0"), sem("s_st1")]
        s_sst = sem("s_sst")
        s_bnc = sem("s_bnc")
        s_pw = [sem(f"s_pw{i}") for i in range(4)]
        s_ps = [sem(f"s_ps{i}") for i in range(4)]
        prog.set_selfsems({"vector": (s_v, True), "scalar": (s_sc, True),
                           "tensor": (s_pe, False)})

        ph = [psum("ph0", [128, 512]), psum("ph1", [128, 512])]
        po = [psum("po0", [128, 512]), psum("po1", [128, 512])]
        pns = [psum("pns0", [128, 512]), psum("pns1", [128, 512])]
        pmisc = psum("pmisc", [128, 512])
        pk1 = psum("pk1", [128, 512])
        pr8 = po[0][:, 0:8]
        pt9 = po[1][0:9, 0:128]

        p0 = ExitStack()
        stg = p0.enter_context(nc.sbuf_tensor("p0stage", [128, 1024], F32))
        stgw = p0.enter_context(nc.sbuf_tensor("p0stagew", [16, 1024], F32))
        stgb = p0.enter_context(nc.sbuf_tensor("p0stageb", [128, 1024], BF16))
        stgb1 = p0.enter_context(nc.sbuf_tensor("p0stgb1", [9, 4096], F32))

        t_id = prog.emit("sync", lambda e: e.dma_start(out=identf[:], in_=ident_in[:, :]),
                         waits=[(s_ld, s_ld.n)], incs=[(s_ld, 16)])
        prog.emit("vector", lambda e: e.memset(ones_row[:], 1.0))
        prog.emit("vector", lambda e: e.memset(ones_bf[:], 1.0))
        prog.emit("vector", lambda e: e.memset(o400[:], 1.0 / 400.0))
        prog.emit("vector", lambda e: e.memset(one_col[:], 1.0))
        t_ms = prog.emit("vector", lambda e: e.memset(tanh_bias[:], -CDT * K1DT))

        def row_to_dmajor(src_row, dst):
            tl = prog.emit("sync", lambda e: e.dma_start(
                out=stgw[0:8, 0:128],
                in_=src_row.rearrange("o (c p) -> (o c) p", c=8)),
                waits=[(s_ld, s_ld.n), (s_pe, s_pe.n)], incs=[(s_ld, 16)])
            tp = prog.emit("tensor", lambda e: e.transpose(
                pmisc[:, 0:8], stgw[0:8, 0:128], identf[0:8, 0:8]),
                waits=after(tl) + after(t_id) + [(s_v, s_v.n)])
            return prog.emit("vector", lambda e: e.tensor_copy(out=dst, in_=pmisc[:, 0:8]),
                             waits=after(tp))

        tok_nw = row_to_dmajor(nw, nwT[:, :])
        tok_dp = row_to_dmajor(Dp, dpT[:, :])
        tok_bdt = row_to_dmajor(bdt, bdtT[:, :])

        tl = prog.emit("sync", lambda e: e.dma_start(
            out=a1t[:].rearrange("p (c n) -> p c n", c=8),
            in_=A_log.rearrange("(c p) n -> p c n", p=128)),
            waits=[(s_ld, s_ld.n)], incs=[(s_ld, 16)])
        tsa = prog.emit("scalar", lambda e: e.activation(
            out=a1t[:], in_=a1t[:], func=AF.Tanh, scale=0.1),
            waits=after(tl))
        tsb = prog.emit("scalar", lambda e: e.activation(
            out=a1t[:], in_=a1t[:], func=AF.Exp, scale=10.0),
            waits=after(tsa))
        tok_a1 = prog.emit("vector", lambda e: e.tensor_scalar_mul(a1t[:], a1t[:], -1.0),
                           waits=after(tsb))

        tok_rwT = []
        tl = prog.emit("sync", lambda e: e.dma_start(out=stgw[0:8, :], in_=rw[:, :]),
                       waits=[(s_ld, s_ld.n), (s_pe, s_pe.n)], incs=[(s_ld, 16)])
        for cd in range(8):
            tp = prog.emit("tensor", lambda e, cd=cd: e.transpose(
                pmisc[:, 0:8], stgw[0:8, cd * 128:(cd + 1) * 128], identf[0:8, 0:8]),
                waits=after(tl) + after(tok_nw) + [(s_v, s_v.n)])
            tok_rwT.append(prog.emit("vector", lambda e, cd=cd: e.tensor_scalar(
                out=rwTf[:, cd, :], in0=pmisc[:, 0:8],
                scalar1=nwT[:, cd:cd + 1], scalar2=None, op0=ALU.mult),
                waits=after(tp)))

        tl_b1 = prog.emit("sync", lambda e: e.dma_start(out=stgb1[:, :], in_=b1[:, :]),
                          waits=[(s_ld, s_ld.n), (s_pe, s_pe.n)], incs=[(s_ld, 16)])
        tok_eb1 = []
        for ch in range(32):
            tp = prog.emit("tensor", lambda e, ch=ch: e.transpose(
                pmisc[:, 0:9], stgb1[0:9, ch * 128:(ch + 1) * 128], identf[0:9, 0:9]),
                waits=after(tl_b1) + [(s_v, s_v.n)])
            tok_eb1.append(prog.emit("vector", lambda e, ch=ch: e.tensor_copy(
                out=eb1T[:, ch, :], in_=pmisc[:, 0:9]),
                waits=after(tp)))
        tl_b2 = prog.emit("sync", lambda e: e.dma_start(out=stgw[0:9, :], in_=b2[:, :]),
                          waits=[(s_ld, s_ld.n), (s_pe, s_pe.n), (s_v, s_v.n)],
                          incs=[(s_ld, 16)])
        tok_eb2 = prog.emit("vector", lambda e: e.tensor_copy(out=eb2x[:, :], in_=stgw[0:9, :]),
                            waits=after(tl_b2))

        wdt_st_tok = []
        for co in range(8):
            tl = prog.emit("sync", lambda e, co=co: e.dma_start(
                out=stg[:], in_=Wdt[co * 128:(co + 1) * 128, :]),
                waits=[(s_ld, s_ld.n), (s_v, s_v.n), (s_pe, s_pe.n)],
                incs=[(s_ld, 16)])
            tcps = []
            for cd in range(8):
                tp = prog.emit("tensor", lambda e, cd=cd: e.transpose(
                    pmisc[:, 0:128], stg[:, cd * 128:(cd + 1) * 128], identf[:]),
                    waits=after(tl) + [(s_v, s_v.n)])
                tcps.append(prog.emit("vector", lambda e, cd=cd: e.tensor_copy(
                    out=stgb[:, cd * 128:(cd + 1) * 128], in_=pmisc[:, 0:128]),
                    waits=after(tp) + (wdt_st_tok if cd == 0 else [])))
            wdt_st_tok = after(prog.emit("gpsimd", lambda e, co=co: e.dma_start(
                out=wdtd[co].rearrange("p c d -> p (c d)"), in_=stgb[:]),
                waits=after(tcps[-1]), incs=[(s_wdt, 16)]))
        wdt_done = s_wdt.n

        pstg = [p0.enter_context(nc.sbuf_tensor(f"pstg{i}", [128, 4, 1024], F32)) for i in range(4)]
        pcv = [p0.enter_context(nc.sbuf_tensor(f"pcv{i}", [128, 4, 1024], BF16)) for i in range(4)]
        last_conv = {}
        last_store = {}
        for sl in range(NEXP):
            for q in range(NQ):
                for g4 in range(2):
                    chn = g4 % 2
                    wld = prog.emit("sync", lambda e, sl=sl, q=q, g4=g4, chn=chn: e.dma_start(
                        out=pstg[chn][:],
                        in_=W1[sl][g4 * 512:(g4 + 1) * 512, q * QH:(q + 1) * QH]
                            .rearrange("(a p) h -> p a h", p=128)),
                        waits=last_conv.get(chn, []), incs=[(s_pw[chn], 16)])
                    w_ = after(wld) + last_store.get(chn, [])
                    if sl == 0 and q == 0 and g4 == 0:
                        w_ += after(tok_nw)
                    tcv = None
                    for a in range(4):
                        tcv = prog.emit("vector", lambda e, g4=g4, a=a, chn=chn: e.tensor_scalar(
                            out=pcv[chn][:, a, :], in0=pstg[chn][:, a, :],
                            scalar1=nwT[:, g4 * 4 + a:g4 * 4 + a + 1],
                            scalar2=None, op0=ALU.mult),
                            waits=w_ if a == 0 else ())
                    last_conv[chn] = after(tcv)
                    tst = prog.emit("gpsimd", lambda e, sl=sl, q=q, g4=g4, chn=chn: e.dma_start(
                        out=w1bf[sl, q, :, g4 * 4:(g4 + 1) * 4, :], in_=pcv[chn][:]),
                        waits=after(tcv), incs=[(s_ps[chn], 16)])
                    last_store[chn] = after(tst)
                for g4 in range(2):
                    chn = 2 + g4 % 2
                    wld = prog.emit("sync", lambda e, sl=sl, q=q, g4=g4, chn=chn: e.dma_start(
                        out=pstg[chn][:],
                        in_=W2[sl][q * QH + g4 * 512:q * QH + (g4 + 1) * 512, :]
                            .rearrange("(a p) h -> p a h", p=128)),
                        waits=last_conv.get(chn, []), incs=[(s_pw[chn], 16)])
                    tcv = prog.emit("scalar", lambda e, chn=chn: e.activation(
                        out=pcv[chn][:].rearrange("p a h -> p (a h)"),
                        in_=pstg[chn][:].rearrange("p a h -> p (a h)"), func=AF.Copy),
                        waits=after(wld) + last_store.get(chn, []))
                    last_conv[chn] = after(tcv)
                    tst = prog.emit("gpsimd", lambda e, sl=sl, q=q, g4=g4, chn=chn: e.dma_start(
                        out=w2bf[sl, q, :, g4 * 4:(g4 + 1) * 4, :], in_=pcv[chn][:]),
                        waits=after(tcv), incs=[(s_ps[chn], 16)])
                    last_store[chn] = after(tst)

        allc = (after(tok_a1) + after(tok_dp) + after(tok_bdt) + after(tok_eb2)
                + [(s_wdt, wdt_done)]
                + [(s_ps[i], s_ps[i].n) for i in range(4)]
                + [(s_pw[i], s_pw[i].n) for i in range(4)]
                + [(s_v, s_v.n), (s_sc, s_sc.n), (s_pe, s_pe.n), (s_ld, s_ld.n)])
        for t_ in tok_rwT + tok_eb1:
            allc += after(t_)
        const_waits = maxwaits(allc)
        for _eng in ENGINES:
            prog.emit(_eng, lambda e: None, waits=list(const_waits))
        p0.close()

        nrmT = sbuf("nrmT", [128, 8, BT], BF16)
        nrmTf = sbuf("nrmTf", [128, 8, 128], F32)
        accb = sbuf("accb", [128, NTT, D], BF16)
        g = sbuf("gq", [128, 8, BT], BF16)
        w1q = sbuf("w1q", [128, 8, QH], BF16)
        w2q = sbuf("w2q", [128, 8, D], BF16)
        xt = [sbuf("xt0", [128, D], F32), sbuf("xt1", [128, D], F32)]
        nrm_s = sbuf("nrm_s", [128, D], F32)
        stats = sbuf("stats", [128, 12], F32)
        logit = sbuf("logit", [128, NTT, 8], F32)
        mx8 = sbuf("mx8", [128, NTT, 8], F32)
        s12 = sbuf("s12", [128, NTT, 2], F32)
        wts = sbuf("wts", [128, NTT, 4], F32)
        msk1 = sbuf("msk1", [128, NTT, 8], F32)
        msk2 = sbuf("msk2", [128, NTT, 8], F32)
        comb = sbuf("comb", [128, NTT, NEXP], F32)
        combT = sbuf("combT", [NEXP, BT], BF16)

        fst = {
            "xt_last": {},
            "evh": {},
            "evo": {},
            "pe_q": {},
            "qc": 0,
            "w1tok": None, "w2tok": None,
            "lnv": {},
        }

        def ffn_block(src, dst, blk, dep=()):
            dep = list(dep)
            base = blk * BT

            def flush():
                for fn_ in fst.get("pending", []):
                    fn_()
                fst["pending"] = []

            def defer(fn_):
                fst.setdefault("pending", []).append(fn_)
            flush()
            tok_tcp = None
            tok_tlg = None
            for tt in range(NTT):
                xb = xt[tt % 2]
                row0 = base + tt * 128
                w_ = list(fst["xt_last"].get(tt % 2, [])) + dep
                tld = prog.emit("gpsimd", lambda e, xb=xb, row0=row0: e.dma_start(
                    out=xb[:], in_=src[row0:row0 + 128, :]),
                    waits=w_, incs=[(s_ldx[tt % 2], 16)])
                tsum = prog.emit("vector", lambda e, xb=xb: e.reduce_sum(
                    out=stats[:, 0:1], in_=xb[:], axis=AX.X),
                    waits=after(tld))
                tsq = prog.emit("scalar", lambda e, xb=xb: e.activation(
                    out=nrm_s[:], in_=xb[:], func=AF.Square,
                    accum_out=stats[:, 1:2]),
                    waits=after(tld) + [(s_v, s_v.n)])
                tv = prog.emit("vector", lambda e: e.tensor_scalar_mul(
                    stats[:, 2:3], stats[:, 0:1], 1.0 / D), waits=after(tsum))
                tv = prog.emit("vector", lambda e: e.tensor_scalar_mul(
                    stats[:, 3:4], stats[:, 1:2], 1.0 / D), waits=after(tsq))
                tv = prog.emit("vector", lambda e: e.tensor_mul(
                    stats[:, 4:5], stats[:, 2:3], stats[:, 2:3]))
                tv = prog.emit("vector", lambda e: e.tensor_sub(
                    stats[:, 5:6], stats[:, 3:4], stats[:, 4:5]))
                tv = prog.emit("vector", lambda e: e.tensor_scalar_add(
                    stats[:, 6:7], stats[:, 5:6], 1e-5))
                tv = prog.emit("vector", lambda e: e.reciprocal(
                    out=stats[:, 7:8], in_=stats[:, 6:7]))
                tv = prog.emit("vector", lambda e: e.tensor_mul(
                    stats[:, 8:9], stats[:, 5:6], stats[:, 7:8]))
                tv = prog.emit("vector", lambda e: e.tensor_scalar_add(
                    stats[:, 9:10], stats[:, 8:9], 1e-8))
                tv9 = prog.emit("vector", lambda e: e.tensor_mul(
                    stats[:, 10:11], stats[:, 6:7], stats[:, 9:10]))
                tl1 = prog.emit("scalar", lambda e: e.activation(
                    out=stats[:, 11:12], in_=stats[:, 10:11], func=AF.Ln),
                    waits=after(tv9))
                tl2 = prog.emit("scalar", lambda e: e.activation(
                    out=stats[:, 10:11], in_=stats[:, 11:12], func=AF.Exp,
                    scale=-0.5))
                tnrm = prog.emit("vector", lambda e, xb=xb: e.scalar_tensor_tensor(
                    out=nrm_s[:], in0=xb[:], scalar=stats[:, 2:3],
                    in1=stats[:, 10:11].to_broadcast([128, D]),
                    op0=ALU.subtract, op1=ALU.mult),
                    waits=after(tl2))
                fst["xt_last"][tt % 2] = after(tnrm) + after(tsq)
                ttr = {}
                for cdx in range(8):
                    pb = po[cdx // 4]
                    w_ = after(tnrm)
                    if cdx == 0:
                        for par in (0, 1):
                            if par in fst["evo"]:
                                w_.append((s_v, fst["evo"][par]))
                        if tok_tlg is not None:
                            w_ += after(tok_tlg)
                    ttr[cdx] = prog.emit("tensor", lambda e, cdx=cdx, pb=pb: e.transpose(
                        pb[:, (cdx % 4) * 128:(cdx % 4 + 1) * 128],
                        nrm_s[:, cdx * 128:(cdx + 1) * 128], identf[:]),
                        waits=w_)
                tok_tcp = prog.emit("vector", lambda e, tt=tt: e.tensor_copy(
                    out=nrmT[:, 0:4, tt * 128:(tt + 1) * 128],
                    in_=po[0][:].rearrange("p (c f) -> p c f", c=4)),
                    waits=after(ttr[3]))
                tok_tcp2 = prog.emit("vector", lambda e, tt=tt: e.tensor_copy(
                    out=nrmT[:, 4:8, tt * 128:(tt + 1) * 128],
                    in_=po[1][:].rearrange("p (c f) -> p c f", c=4)),
                    waits=after(ttr[7]))
                tcpf = prog.emit("vector", lambda e: e.tensor_copy(
                    out=nrmTf[:, 0:4, :],
                    in_=po[0][:].rearrange("p (c f) -> p c f", c=4)))
                tcpf2 = prog.emit("vector", lambda e: e.tensor_copy(
                    out=nrmTf[:, 4:8, :],
                    in_=po[1][:].rearrange("p (c f) -> p c f", c=4)))
                trt = None
                for cd in range(8):
                    w_ = after(tcpf2) if cd == 0 else ()
                    trt = prog.emit("tensor", lambda e, cd=cd: e.matmul(
                        pr8, nrmTf[:, cd, :], rwTf[:, cd, :],
                        start=(cd == 0), stop=(cd == 7)),
                        waits=w_)
                tok_tlg = prog.emit("vector", lambda e, tt=tt: e.tensor_copy(
                    out=logit[:, tt, :], in_=pr8),
                    waits=after(trt))
                prog.emit("vector", lambda e, tt=tt: e.max(
                    out=mx8[:, tt, :], in_=logit[:, tt, :]))
                fst["evo"][0] = s_v.n
                fst["evo"][1] = s_v.n
                yield 2600.0

            te = prog.emit("scalar", lambda e: e.activation(
                out=s12[:, :, :], in_=mx8[:, :, 0:2], func=AF.Exp, scale=-1.0),
                waits=[(s_v, s_v.n)])
            tv = prog.emit("vector", lambda e: e.tensor_scalar_add(
                s12[:, :, :], s12[:, :, :], 1.0), waits=after(te))
            tv = prog.emit("vector", lambda e: e.reciprocal(
                out=s12[:, :, :], in_=s12[:, :, :]))
            tv = prog.emit("vector", lambda e: e.tensor_add(
                wts[:, :, 0], s12[:, :, 0], s12[:, :, 1]))
            tv = prog.emit("vector", lambda e: e.tensor_scalar_add(
                wts[:, :, 0], wts[:, :, 0], 1e-6))
            tv = prog.emit("vector", lambda e: e.reciprocal(
                out=wts[:, :, 1], in_=wts[:, :, 0]))
            tv = prog.emit("vector", lambda e: e.tensor_mul(
                wts[:, :, 2], s12[:, :, 0], wts[:, :, 1]))
            tv = prog.emit("vector", lambda e: e.tensor_mul(
                wts[:, :, 3], s12[:, :, 1], wts[:, :, 1]))
            tv = prog.emit("vector", lambda e: e.tensor_tensor(
                out=msk1[:], in0=logit[:],
                in1=mx8[:, :, 0:1].to_broadcast([128, NTT, 8]), op=ALU.is_ge))
            tv = prog.emit("vector", lambda e: e.tensor_tensor(
                out=msk2[:], in0=logit[:],
                in1=mx8[:, :, 1:2].to_broadcast([128, NTT, 8]), op=ALU.is_ge))
            tv = prog.emit("vector", lambda e: e.tensor_sub(
                msk2[:], msk2[:], msk1[:]))
            tv = prog.emit("vector", lambda e: e.tensor_tensor(
                out=msk1[:], in0=msk1[:],
                in1=wts[:, :, 2:3].to_broadcast([128, NTT, 8]), op=ALU.mult))
            tv = prog.emit("vector", lambda e: e.tensor_tensor(
                out=msk2[:], in0=msk2[:],
                in1=wts[:, :, 3:4].to_broadcast([128, NTT, 8]), op=ALU.mult))
            tv = prog.emit("vector", lambda e: e.tensor_add(
                comb[:, :, 0:8], msk1[:], msk2[:]))
            tcomb = prog.emit("vector", lambda e: e.memset(comb[:, :, 8:9], 1.0))
            yield 2000.0

            for tt in range(NTT):
                w_ = [(s_v, s_v.n)]
                tp = prog.emit("tensor", lambda e, tt=tt: e.transpose(
                    pt9, comb[:, tt, :], identf[:]), waits=w_)
                tcc = prog.emit("vector", lambda e, tt=tt: e.tensor_copy(
                    out=combT[:, tt * 128:(tt + 1) * 128], in_=pt9),
                    waits=after(tp))
                tbs = {}
                for hf in range(2):
                    w_ = after(tcc)
                    if hf == 0 and (0 in fst["evo"]):
                        w_.append((s_v, fst["evo"][0]))
                    if hf == 1 and (1 in fst["evo"]):
                        w_.append((s_v, fst["evo"][1]))
                    tbs[hf] = prog.emit("tensor", lambda e, tt=tt, hf=hf: e.matmul(
                        po[hf][:], combT[:, tt * 128:(tt + 1) * 128],
                        eb2x[:, hf * 512:(hf + 1) * 512],
                        start=True, stop=True),
                        waits=w_)
                for hf in range(2):
                    tev = prog.emit("vector", lambda e, tt=tt, hf=hf: e.tensor_copy(
                        out=accb[:, tt, hf * 512:(hf + 1) * 512], in_=po[hf][:]),
                        waits=after(tbs[hf]))
                    fst["evo"][hf] = s_v.n
                yield 1500.0

            for sl in range(NEXP):
                for q in range(NQ):
                    qc = fst["qc"]
                    fst["qc"] += 1
                    wpd = []
                    t_w1 = prog.emit("sync", lambda e, sl=sl, q=q: e.dma_start(
                        out=w1q[:], in_=w1bf[sl, q]),
                        waits=[(s_pe, fst["pe_q"].get(("mm1", qc - 1), 0))] + wpd,
                        incs=[(s_ldw[qc % 2], 16)])
                    t_w2 = prog.emit("sync", lambda e, sl=sl, q=q: e.dma_start(
                        out=w2q[:], in_=w2bf[sl, q]),
                        waits=[(s_pe, fst["pe_q"].get(("mm2", qc - 1), 0))],
                        incs=[(s_ldw[2 + qc % 2], 16)])
                    fst["w1tok"] = after(t_w1)
                    fst["w2tok"] = after(t_w2)

                    for i in range(8):
                        for t2 in range(2):
                            par = (i * 2 + t2) % 2
                            hb = ph[par]
                            flush()
                            mmlast = None
                            for c in range(8):
                                w_ = []
                                if c == 0:
                                    if i == 0 and t2 == 0:
                                        w_ += fst["w1tok"]
                                    if ("h", par) in fst["evh"]:
                                        w_.append((s_sc, fst["evh"][("h", par)]))
                                mmlast = prog.emit("tensor",
                                    lambda e, i=i, c=c, t2=t2, hb=hb: e.matmul(
                                        hb[:],
                                        w1q[:, c, i * 128:(i + 1) * 128],
                                        nrmT[:, c, t2 * 512:(t2 + 1) * 512],
                                        start=(c == 0), stop=(c == 7)),
                                    waits=w_)
                                if c == 3:
                                    yield 900.0

                            def ev1(i=i, t2=t2, hb=hb, q=q, sl=sl, par=par, mmlast=mmlast):
                                prog.emit("scalar", lambda e: e.activation(
                                    out=g[:, i, t2 * 512:(t2 + 1) * 512],
                                    in_=hb[:], func=AF.Relu,
                                    bias=eb1T[:, q * 8 + i, sl:sl + 1]),
                                    waits=after(mmlast))
                                fst["evh"][("h", par)] = s_sc.n
                            defer(ev1)
                            yield 1100.0
                    flush()
                    fst["pe_q"][("mm1", qc)] = s_pe.n
                    g_ready = s_sc.n

                    for tt in range(NTT):
                        for hf in range(2):
                            par = (tt * 2 + hf) % 2
                            ob = po[par]
                            flush()
                            mmlast = None
                            for i in range(8):
                                w_ = []
                                if i == 0:
                                    if tt == 0 and hf == 0:
                                        w_ = [(s_sc, g_ready)] + fst["w2tok"]
                                    if par in fst["evo"]:
                                        w_.append((s_v, fst["evo"][par]))
                                mmlast = prog.emit("tensor",
                                    lambda e, tt=tt, i=i, hf=hf, ob=ob: e.matmul(
                                        ob[:],
                                        g[:, i, tt * 128:(tt + 1) * 128],
                                        w2q[:, i, hf * 512:(hf + 1) * 512],
                                        start=(i == 0), stop=(i == 7)),
                                    waits=w_)
                                if i == 3:
                                    yield 900.0

                            def ev2(tt=tt, hf=hf, ob=ob, sl=sl, par=par, mmlast=mmlast):
                                prog.emit("vector", lambda e: e.scalar_tensor_tensor(
                                    out=accb[:, tt, hf * 512:(hf + 1) * 512], in0=ob[:],
                                    scalar=comb[:, tt, sl:sl + 1], op0=ALU.mult,
                                    in1=accb[:, tt, hf * 512:(hf + 1) * 512], op1=ALU.add),
                                    waits=after(mmlast))
                                fst["evo"][par] = s_v.n
                            defer(ev2)
                            yield 1100.0
                    flush()
                    fst["pe_q"][("mm2", qc)] = s_pe.n

            acc_done = s_v.n
            for tt in range(NTT):
                xb = xt[tt % 2]
                row0 = base + tt * 128
                w_ = list(fst["xt_last"].get(tt % 2, [])) + dep
                tld = prog.emit("gpsimd", lambda e, xb=xb, row0=row0: e.dma_start(
                    out=xb[:], in_=src[row0:row0 + 128, :]),
                    waits=w_, incs=[(s_ldx[tt % 2], 16)])
                tap = prog.emit("vector", lambda e, xb=xb, tt=tt: e.scalar_tensor_tensor(
                    out=xb[:], in0=accb[:, tt, :], scalar=0.5,
                    op0=ALU.mult, in1=xb[:], op1=ALU.add),
                    waits=after(tld) + ([(s_v, acc_done)] if tt == 0 else []))
                tstr = prog.emit("gpsimd", lambda e, xb=xb, row0=row0: e.dma_start(
                    out=dst[row0:row0 + 128, :], in_=xb[:]),
                    waits=after(tap), incs=[(s_st[tt % 2], 16)])
                fst["xt_last"][tt % 2] = after(tap) + after(tstr)
                yield 800.0
            fst.setdefault("blk_store", {})[blk] = [
                (s_st[0], s_st[0].n), (s_st[1], s_st[1].n)]

        xnT = sbuf("xnT", [128, 8, BT], BF16)
        dtb = sbuf("dtb", [128, 8, BT], BF16)
        zcl = [sbuf("zcl0", [128, TC, 128], BF16), sbuf("zcl1", [128, TC, 128], BF16)]
        zs = sbuf("zs", [128, 8, 128], F32)
        dAs = [sbuf("dAs0", [128, 128], F32), sbuf("dAs1", [128, 128], F32)]
        ub = sbuf("ub", [128, TC, 128], BF16)
        hist = sbuf("hist", [128, RING, 128], F32)
        dtx = sbuf("dtx", [128, 8, TC], F32)
        qb = [sbuf("qb0", [128, 8, TC], F32), sbuf("qb1", [128, 8, TC], F32)]
        wred = sbuf("wred", [128, 8, 8], F32)
        yb = sbuf("yb", [128, 8, TC], F32)
        Bn = sbuf("Bn", [16, BT], BF16)
        Cn = sbuf("Cn", [16, BT], BF16)
        Brep = [sbuf("Brep0", [128, N16, TC], F32), sbuf("Brep1", [128, N16, TC], F32)]
        Crep = [sbuf("Crep0", [128, N16, TC], F32), sbuf("Crep1", [128, N16, TC], F32)]
        brow = sbuf("brow", [1, 2, N16 * TC], BF16)
        shist2 = sbuf("shist2", [128, TC + 1], F32)
        sq_s = sbuf("sq_s", [128, 128], F32)
        p2 = sbuf("p2", [128, 1], F32)
        p2b = sbuf("p2b", [128, 1], BF16)
        s0 = sbuf("s0", [128, 2], F32)
        xrow = sbuf("xrow", [TC, D], F32)
        sxt = sbuf("sxt", [128, D], F32)
        snrm = sbuf("snrm", [128, D], F32)
        sstats = sbuf("sstats", [128, 12], F32)
        tmp = sbuf("stmp", [128, 128], F32)
        wdtr = [sbuf("wdtr0", [128, 8, 128], BF16)]

        sst = {
            "init": False,
            "sxt_last": [],
            "xrow_last": [],
            "pdt_last": {},
            "bc_last": {},
            "bnc_ch": [],
            "wdt_last": {},
            "pk1_last": [],
            "pns_last": {},
            "lnp_last": [],
            "dA_tok": {},
        }

        def ssm_init():
            prog.emit("vector", lambda e: e.memset(hist[:, RING - 1, :], 0.0))
            prog.emit("vector", lambda e: e.memset(shist2[:, 0:1], 1.0))
            sst["init"] = True

        def ssm_block(src, dst, blk, dep=()):
            dep = list(dep)
            if not sst["init"]:
                ssm_init()
            base = blk * BT
            for tt in range(NTT):
                row0 = base + tt * 128
                tld = prog.emit("gpsimd", lambda e, row0=row0: e.dma_start(
                    out=sxt[:], in_=src[row0:row0 + 128, :]),
                    waits=list(sst["sxt_last"]) + dep, incs=[(s_sxt, 16)])
                tsum = prog.emit("vector", lambda e: e.reduce_sum(
                    out=sstats[:, 0:1], in_=sxt[:], axis=AX.X),
                    waits=after(tld))
                tsq = prog.emit("scalar", lambda e: e.activation(
                    out=snrm[:], in_=sxt[:], func=AF.Square,
                    accum_out=sstats[:, 1:2]),
                    waits=after(tld) + [(s_v, s_v.n)])
                tv = prog.emit("vector", lambda e: e.tensor_scalar_mul(
                    sstats[:, 2:3], sstats[:, 0:1], 1.0 / D), waits=after(tsum))
                tv = prog.emit("vector", lambda e: e.tensor_scalar_mul(
                    sstats[:, 3:4], sstats[:, 1:2], 1.0 / D), waits=after(tsq))
                tv = prog.emit("vector", lambda e: e.tensor_mul(
                    sstats[:, 4:5], sstats[:, 2:3], sstats[:, 2:3]))
                tv = prog.emit("vector", lambda e: e.tensor_sub(
                    sstats[:, 5:6], sstats[:, 3:4], sstats[:, 4:5]))
                tv5 = prog.emit("vector", lambda e: e.tensor_scalar_add(
                    sstats[:, 6:7], sstats[:, 5:6], 1e-5))
                tl1 = prog.emit("scalar", lambda e: e.activation(
                    out=sstats[:, 11:12], in_=sstats[:, 6:7], func=AF.Ln),
                    waits=after(tv5))
                tl2 = prog.emit("scalar", lambda e: e.activation(
                    out=sstats[:, 10:11], in_=sstats[:, 11:12], func=AF.Exp,
                    scale=-0.5))
                tnrm = prog.emit("vector", lambda e: e.scalar_tensor_tensor(
                    out=snrm[:], in0=sxt[:], scalar=sstats[:, 2:3],
                    in1=sstats[:, 10:11].to_broadcast([128, D]),
                    op0=ALU.subtract, op1=ALU.mult),
                    waits=after(tl2))
                sst["sxt_last"] = after(tnrm) + after(tsq)
                ttr = {}
                for cdx in range(4):
                    w_ = after(tnrm)
                    if cdx == 0:
                        w_ = list(w_) + sst["lnp_last"] + [(s_sc, sst["pdt_last"].get("any", 0))]
                    ttr[cdx] = prog.emit("tensor", lambda e, cdx=cdx: e.transpose(
                        pmisc[:, cdx * 128:(cdx + 1) * 128],
                        snrm[:, cdx * 128:(cdx + 1) * 128], identf[:]),
                        waits=w_)
                tc1 = prog.emit("vector", lambda e, tt=tt: e.tensor_copy(
                    out=xnT[:, 0:4, tt * 128:(tt + 1) * 128],
                    in_=pmisc[:].rearrange("p (c f) -> p c f", c=4)),
                    waits=after(ttr[3]))
                for cdx in range(4, 8):
                    w_ = after(tc1) if cdx == 4 else ()
                    ttr[cdx] = prog.emit("tensor", lambda e, cdx=cdx: e.transpose(
                        pmisc[:, (cdx % 4) * 128:(cdx % 4 + 1) * 128],
                        snrm[:, cdx * 128:(cdx + 1) * 128], identf[:]),
                        waits=w_)
                tc2 = prog.emit("vector", lambda e, tt=tt: e.tensor_copy(
                    out=xnT[:, 4:8, tt * 128:(tt + 1) * 128],
                    in_=pmisc[:].rearrange("p (c f) -> p c f", c=4)),
                    waits=after(ttr[7]))
                sst["lnp_last"] = after(tc2)
                yield 2600.0
            xn_v = s_v.n

            for co in range(8):
                wv = wdtr[0]
                twl = prog.emit("sync", lambda e, co=co, wv=wv: e.dma_start(
                    out=wv[:].rearrange("p c d -> p (c d)"),
                    in_=wdtd[co].rearrange("p c d -> p (c d)")),
                    waits=[(s_pe, sst["wdt_last"].get(0, 0)),
                           (s_wdl, s_wdl.n)],
                    incs=[(s_wdl, 16)])
                for tq in range(2):
                    mmlast = None
                    for cd in range(8):
                        w_ = []
                        if cd == 0:
                            w_ = [(s_v, xn_v)] + after(twl)
                            w_.append((s_sc, sst["pdt_last"].get("any", 0)))
                            w_ += sst["lnp_last"]
                        mmlast = prog.emit("tensor", lambda e, co=co, cd=cd, tq=tq, wv=wv: e.matmul(
                            pmisc[:], wv[:, cd, :],
                            xnT[:, cd, tq * 512:(tq + 1) * 512],
                            start=(cd == 0), stop=(cd == 7)),
                            waits=w_)
                    tso = prog.emit("scalar", lambda e, co=co: e.activation(
                        out=pmisc[:], in_=pmisc[:], func=AF.Exp,
                        bias=bdtT[:, co:co + 1]),
                        waits=after(mmlast))
                    tso2 = prog.emit("scalar", lambda e, co=co, tq=tq: e.activation(
                        out=dtb[:, co, tq * 512:(tq + 1) * 512], in_=pmisc[:],
                        func=AF.Ln, bias=one_col[:]),
                        waits=after(tso))
                    sst["pdt_last"]["any"] = s_sc.n
                    sst["lnp_last"] = []
                    yield 1800.0
                sst["wdt_last"][0] = s_pe.n

            for (wTs, dstN) in ((wbT_s, Bn), (wcT_s, Cn)):
                for tq in range(2):
                    mmlast = None
                    for cd in range(8):
                        w_ = []
                        if cd == 0:
                            w_ = [(s_sc, sst["pdt_last"]["any"]), (s_v, s_v.n)]
                            w_ += [(s_bnc, s_bnc.n)]
                        mmlast = prog.emit("tensor", lambda e, cd=cd, tq=tq, wTs=wTs: e.matmul(
                            pmisc[0:16, :], wTs[:, cd, :],
                            xnT[:, cd, tq * 512:(tq + 1) * 512],
                            start=(cd == 0), stop=(cd == 7)),
                            waits=w_)
                    tbc = prog.emit("vector", lambda e, tq=tq, dstN=dstN: e.tensor_copy(
                        out=dstN[:, tq * 512:(tq + 1) * 512], in_=pmisc[0:16, :]),
                        waits=after(mmlast))
                    sst["bc_last"]["any"] = s_v.n
                    yield 1200.0

            tth = prog.emit("scalar", lambda e: e.activation(
                out=dtb[:].rearrange("p c f -> p (c f)"),
                in_=dtb[:].rearrange("p c f -> p (c f)"),
                func=AF.Tanh, scale=K1DT, bias=tanh_bias[:]),
                waits=[(s_v, s_v.n)])
            tdt = prog.emit("vector", lambda e: e.tensor_scalar(
                out=dtb[:].rearrange("p c f -> p (c f)"),
                in0=dtb[:].rearrange("p c f -> p (c f)"),
                scalar1=K2DT, scalar2=CDT, op0=ALU.mult, op1=ALU.add),
                waits=after(tth))
            yield 8000.0
            for c in range(NCH):
                cpar = c % 2
                ttn = None
                for sl8 in range(TC // 8):
                    w_ = after(tdt) if (c == 0 and sl8 == 0) else []
                    tz = prog.emit("vector", lambda e, c=c, sl8=sl8: e.tensor_tensor(
                        out=zs[:].rearrange("p t (cc n) -> p t cc n", cc=8),
                        in0=dtb[:, :, c * TC + sl8 * 8:c * TC + (sl8 + 1) * 8]
                            .rearrange("p cc t -> p t cc")[:, :, :, None]
                            .to_broadcast([128, 8, 8, 16]),
                        in1=a1t[:].rearrange("p (cc n) -> p cc n", cc=8)[:, None, :, :]
                            .to_broadcast([128, 8, 8, 16]),
                        op=ALU.mult),
                        waits=w_ + [(s_sc, s_sc.n)])
                    w2_ = after(tz)
                    if sl8 == 0:
                        w2_ = w2_ + sst.get("zcs_war", {}).get(cpar, [])
                    ttn = prog.emit("scalar", lambda e, cpar=cpar, sl8=sl8: e.activation(
                        out=zcl[cpar][:, sl8 * 8:(sl8 + 1) * 8, :]
                            .rearrange("p t f -> p (t f)"),
                        in_=zs[:].rearrange("p t f -> p (t f)"),
                        func=AF.Tanh, scale=1.0 / 16.0),
                        waits=w2_)
                tzs = prog.emit("gpsimd", lambda e, c=c, cpar=cpar: e.dma_start(
                    out=zcd[c].rearrange("p t f -> p (t f)"),
                    in_=zcl[cpar][:].rearrange("p t f -> p (t f)")),
                    waits=after(ttn) + [(s_zcs[cpar], s_zcs[cpar].n)],
                    incs=[(s_zcs[cpar], 16)])
                sst.setdefault("zcs_war", {})[cpar] = after(tzs)
                yield 2200.0
            sst["zcl_tok"] = {}
            for c in (0, 1):
                cpar = c % 2
                tzl = prog.emit("sync", lambda e, c=c, cpar=cpar: e.dma_start(
                    out=zcl[cpar][:].rearrange("p t f -> p (t f)"),
                    in_=zcd[c].rearrange("p t f -> p (t f)")),
                    waits=[(s_zcs[cpar], s_zcs[cpar].n),
                           (s_zcl[cpar], s_zcl[cpar].n)],
                    incs=[(s_zcl[cpar], 16)])
                sst["zcl_tok"][cpar] = after(tzl)

            def prep_bounce(c):
                t0c = c * TC
                tb1 = prog.emit("sync", lambda e, t0c=t0c: e.dma_start(
                    out=bounce[0, :].rearrange("(n t) -> n t", n=16),
                    in_=Bn[:, t0c:t0c + TC]),
                    waits=[(s_bnc, s_bnc.n),
                           (s_v, sst["bc_last"].get("any", 0))]
                          + sst.get("brow_rd", []),
                    incs=[(s_bnc, 16)])
                tb2 = prog.emit("sync", lambda e, t0c=t0c: e.dma_start(
                    out=bounce[1, :].rearrange("(n t) -> n t", n=16),
                    in_=Cn[:, t0c:t0c + TC]),
                    waits=(), incs=[(s_bnc, 16)])
                tb3 = prog.emit("sync", lambda e: e.dma_start(
                    out=brow[:, :, :], in_=bounce[None, :, :]),
                    waits=maxwaits(after(tb1) + after(tb2)),
                    incs=[(s_bnc, 16)])
                sst["bounce_tok"] = after(tb3)

            def prep_brep(c):
                cpar = c % 2
                tk1 = prog.emit("tensor", lambda e: e.matmul(
                    pk1[:], ones_bf[:], brow[:, 0, :], start=True, stop=True),
                    waits=sst["bounce_tok"] + sst["pk1_last"])
                tkb = prog.emit("vector", lambda e, cpar=cpar: e.tensor_copy(
                    out=Brep[cpar][:].rearrange("p n t -> p (n t)"), in_=pk1[:]),
                    waits=after(tk1))
                sst["pk1_last"] = after(tkb)
                sst["brep_tok"] = after(tkb)

            def prep_crep(c):
                cpar = c % 2
                tk2 = prog.emit("tensor", lambda e: e.matmul(
                    pk1[:], ones_bf[:], brow[:, 1, :], start=True, stop=True),
                    waits=sst["bounce_tok"] + sst["pk1_last"])
                tkc = prog.emit("vector", lambda e, cpar=cpar: e.tensor_copy(
                    out=Crep[cpar][:].rearrange("p n t -> p (n t)"), in_=pk1[:]),
                    waits=after(tk2))
                sst["pk1_last"] = after(tkc)
                sst["brow_rd"] = after(tk2)

            def prep_dtx_qb(c):
                t0c = c * TC
                cpar = c % 2
                prog.emit("vector", lambda e, t0c=t0c: e.tensor_tensor(
                    out=dtx[:], in0=dtb[:, :, t0c:t0c + TC],
                    in1=xnT[:, :, t0c:t0c + TC], op=ALU.mult))
                prog.emit("vector", lambda e, t0c=t0c, cpar=cpar: e.tensor_tensor(
                    out=qb[cpar][:], in0=xnT[:, :, t0c:t0c + TC],
                    in1=dpT[:][:, :, None].to_broadcast([128, 8, TC]),
                    op=ALU.mult))

            def prep_uslice(c, j):
                cpar = c % 2
                prog.emit("vector", lambda e, j=j, cpar=cpar: e.tensor_tensor(
                    out=ub[:, j * 8:(j + 1) * 8, :]
                        .rearrange("p t (cc n) -> p t cc n", cc=8),
                    in0=dtx[:, :, j * 8:(j + 1) * 8]
                        .rearrange("p cc t -> p t cc")[:, :, :, None]
                        .to_broadcast([128, 8, 8, 16]),
                    in1=Brep[cpar][:, :, j * 8:(j + 1) * 8]
                        .rearrange("p n t -> p t n")[:, :, None, :]
                        .to_broadcast([128, 8, 8, 16]),
                    op=ALU.mult),
                    waits=sst["brep_tok"] if j == 0 else ())

            def prep_zcload(c):
                cpar = c % 2
                tzl = prog.emit("sync", lambda e, c=c, cpar=cpar: e.dma_start(
                    out=zcl[cpar][:].rearrange("p t f -> p (t f)"),
                    in_=zcd[c].rearrange("p t f -> p (t f)")),
                    waits=[(s_zcl[cpar], s_zcl[cpar].n),
                           (s_sc, sst.get("zc_rd", {}).get(cpar, 0))],
                    incs=[(s_zcl[cpar], 16)])
                sst["zcl_tok"][cpar] = after(tzl)

            for ch in range(NCH):
                par_c = ch % 2
                t0 = ch * TC
                gt0 = base + t0
                if ch == 0:
                    prep_bounce(0)
                    prep_brep(0)
                    prep_crep(0)
                    prep_dtx_qb(0)
                    for j in range(4):
                        prep_uslice(0, j)
                    yield 6000.0

                nxt = ch + 1
                extras = {}
                if nxt < NCH:
                    extras[1] = [lambda: prep_bounce(nxt)]
                    extras[3] = [lambda: prep_dtx_qb(nxt)]
                    extras[5] = [lambda: prep_brep(nxt)]
                    extras[9] = [lambda: prep_crep(nxt)]
                    for j in range(4):
                        extras.setdefault(7 + 8 * j, []).append(
                            lambda j=j: prep_uslice(nxt, j))

                tldx = prog.emit("gpsimd", lambda e, gt0=gt0: e.dma_start(
                    out=xrow[:], in_=src[gt0:gt0 + TC, :]),
                    waits=list(sst["xrow_last"]) + dep, incs=[(s_sst, 16)])

                dA_prev_tok = sst["dA_tok"]
                t_dap = prog.emit("scalar", lambda e, par_c=par_c: e.activation(
                    out=dAs[gt0 % 2][:],
                    in_=zcl[par_c][:, 0, :], func=AF.Exp, scale=16.0),
                    waits=[(s_v, dA_prev_tok.get(("use", gt0 % 2), 0))]
                          + sst["zcl_tok"].get(par_c, []))
                dA_prev_tok[("rdy", gt0 % 2)] = t_dap
                for ti in range(TC):
                    gstep = gt0 + ti
                    par = gstep % 2
                    rs = gstep % RING
                    prs = (gstep - 1) % RING
                    tha = prog.emit("vector", lambda e, prs=prs, par=par: e.tensor_tensor(
                        out=tmp[:], in0=hist[:, prs, :], in1=dAs[par][:],
                        op=ALU.mult),
                        waits=after(dA_prev_tok[("rdy", par)]))
                    dA_prev_tok[("use", par)] = s_v.n
                    if ti + 1 < TC:
                        t_dap = prog.emit("scalar", lambda e, ti=ti, par=par, par_c=par_c: e.activation(
                            out=dAs[1 - par][:],
                            in_=zcl[par_c][:, ti + 1, :], func=AF.Exp, scale=16.0),
                            waits=[(s_v, dA_prev_tok.get(("use", 1 - par), 0))])
                        dA_prev_tok[("rdy", 1 - par)] = t_dap
                    t1 = prog.emit("vector", lambda e, ti=ti, rs=rs: e.scalar_tensor_tensor(
                        out=hist[:, rs, :], in0=tmp[:],
                        scalar=shist2[:, ti:ti + 1],
                        op0=ALU.mult, in1=ub[:, ti, :], op1=ALU.add))
                    t3 = prog.emit("vector", lambda e, rs=rs: e.scalar_tensor_tensor(
                        out=sq_s[:], in0=hist[:, rs, :], scalar=1.0,
                        op0=ALU.mult, in1=hist[:, rs, :], op1=ALU.mult,
                        accum_out=p2[:]))
                    tcpb = prog.emit("vector", lambda e: e.tensor_copy(
                        out=p2b[:], in_=p2[:]))
                    t4 = prog.emit("tensor", lambda e, par=par: e.matmul(
                        pns[par][:, 0:1], o400[:], p2b[:], start=True, stop=True),
                        waits=after(tcpb) + [(s_sc, sst["pns_last"].get(par, 0))])
                    t5 = prog.emit("scalar", lambda e, par=par: e.activation(
                        out=s0[:, par:par + 1], in_=pns[par][:, 0:1], func=AF.Ln),
                        waits=after(t4))
                    t6 = prog.emit("scalar", lambda e, par=par: e.activation(
                        out=s0[:, par:par + 1], in_=s0[:, par:par + 1],
                        func=AF.Exp, scale=-0.5))
                    sst["pns_last"][par] = s_sc.n
                    prog.emit("vector", lambda e, ti=ti, par=par: e.tensor_scalar_min(
                        shist2[:, ti + 1:ti + 2], s0[:, par:par + 1], 1.0),
                        waits=after(t6))
                    if ti % 8 == 7:
                        k8 = ti // 8
                        tw1 = prog.emit("vector", lambda e, k8=k8, par_c=par_c: e.tensor_tensor(
                            out=zs[:].rearrange("p t (cc n) -> p t cc n", cc=8),
                            in0=hist[:].rearrange("p t (cc n) -> p t cc n", cc=8),
                            in1=Crep[par_c][:, :, k8 * 8:(k8 + 1) * 8]
                                .rearrange("p n t -> p t n")[:, :, None, :]
                                .to_broadcast([128, 8, 8, 16]),
                            op=ALU.mult))
                        tw2 = prog.emit("vector", lambda e: e.reduce_sum(
                            out=wred[:], in_=zs[:].rearrange("p t (cc n) -> p t cc n", cc=8),
                            axis=AX.X),
                            waits=after(tw1))
                        ty = prog.emit("vector", lambda e, k8=k8: e.tensor_tensor(
                            out=yb[:, :, k8 * 8:(k8 + 1) * 8].rearrange("p cc t -> p t cc"),
                            in0=wred[:],
                            in1=shist2[:, k8 * 8 + 1:(k8 + 1) * 8 + 1][:, :, None]
                                .to_broadcast([128, 8, 8]),
                            op=ALU.mult),
                            waits=after(tw2))
                        yield 1200.0
                    for fn_ in extras.get(ti, []):
                        fn_()
                    yield 2400.0
                sst.setdefault("zc_rd", {})[par_c] = s_sc.n
                if ch + 2 < NCH:
                    prep_zcload(ch + 2)

                ty2 = prog.emit("vector", lambda e, par_c=par_c: e.tensor_add(
                    yb[:], yb[:], qb[par_c][:]))
                prog.emit("vector", lambda e: e.tensor_copy(
                    out=shist2[:, 0:1], in_=shist2[:, TC:TC + 1]))

                txa = {}
                for half in range(2):
                    ttr = None
                    for cd4 in range(4):
                        cd = half * 4 + cd4
                        w_ = []
                        if cd4 == 0:
                            w_ = after(ty2) + (sst["pk1_last"] if half == 0 else after(txa[0]))
                        ttr = prog.emit("tensor", lambda e, cd=cd, cd4=cd4: e.transpose(
                            pk1[0:TC, cd4 * 128:(cd4 + 1) * 128], yb[:, cd, :],
                            identf[:]),
                            waits=w_)
                    txa[half] = prog.emit("vector", lambda e, half=half: e.tensor_add(
                        xrow[:, half * 512:(half + 1) * 512],
                        xrow[:, half * 512:(half + 1) * 512],
                        pk1[0:TC, :]),
                        waits=after(ttr) + (after(tldx) if half == 0 else []))
                sst["pk1_last"] = after(txa[1])
                tstr = prog.emit("gpsimd", lambda e, gt0=gt0: e.dma_start(
                    out=dst[gt0:gt0 + TC, :], in_=xrow[:]),
                    waits=after(txa[1]), incs=[(s_sst, 16)])
                sst["xrow_last"] = after(tstr)
                yield 1500.0
            sst.setdefault("blk_sst", {})[blk] = [(s_sst, s_sst.n)]

        wbT_s = sbuf("wbT_s", [128, 8, N16], BF16)
        wcT_s = sbuf("wcT_s", [128, 8, N16], BF16)
        for (wsrc, wdst) in ((WB, wbT_s), (WC, wcT_s)):
            tl = prog.emit("sync", lambda e, wsrc=wsrc: e.dma_start(
                out=xrow[0:16, :], in_=wsrc[:, :]),
                waits=[(s_ld, s_ld.n), (s_v, s_v.n)], incs=[(s_ld, 16)])
            for cd in range(8):
                tp = prog.emit("tensor", lambda e, cd=cd: e.transpose(
                    pmisc[:, 0:16], xrow[0:16, cd * 128:(cd + 1) * 128],
                    identf[0:16, 0:16]),
                    waits=after(tl) + [(s_v, s_v.n)])
                prog.emit("vector", lambda e, cd=cd, wdst=wdst: e.tensor_copy(
                    out=wdst[:, cd, :], in_=pmisc[:, 0:16]),
                    waits=after(tp))
        bc_const = [(s_v, s_v.n)]
        for _eng in ("tensor", "vector", "scalar", "sync", "gpsimd"):
            prog.emit(_eng, lambda e: None, waits=list(bc_const))

        use_ffn = "ffn1" in phases
        use_ssm = "ssm" in phases
        use_ffn2 = "ffn2" in phases
        src1, dst1 = xin, xr1
        src2 = xr1 if use_ffn else xin
        dst2 = xr2
        src3 = xr2 if use_ssm else src2
        dst3 = out

        def barrier():
            w = [(s_st[0], s_st[0].n), (s_st[1], s_st[1].n),
                 (s_sst, s_sst.n), (s_v, s_v.n), (s_sc, s_sc.n),
                 (s_pe, s_pe.n), (s_bnc, s_bnc.n), (s_sxt, s_sxt.n),
                 (s_wdt, s_wdt.n), (s_wdl, s_wdl.n)]
            w += [(s_ldx[i], s_ldx[i].n) for i in range(2)]
            w += [(s_ldw[i], s_ldw[i].n) for i in range(4)]
            for eng in ENGINES:
                prog.emit(eng, lambda e: None, waits=list(w))

        def stdep():
            return [(s_st[0], s_st[0].n), (s_st[1], s_st[1].n)]

        def sstdep():
            return [(s_sst, s_sst.n)]

        if use_ffn and use_ssm and use_ffn2:
            drain(ffn_block(src1, dst1, 0))
            drain(ffn_block(src1, dst1, 1))
            progress = {"ffn1": NBLK - 1, "ssm": -1}

            def f1_tail():
                for b in range(2, NBLK):
                    for c in ffn_block(src1, dst1, b):
                        yield c

            def ssm0_stream():
                for c in ssm_block(src2, dst2, 0, fst["blk_store"][0]):
                    yield c
                progress["ssm"] = 0

            merge([f1_tail(), ssm0_stream()])

            def ffn_stream():
                for b in range(NBLK):
                    while progress["ssm"] < b:
                        yield ("need", "ssm", b)
                    for c in ffn_block(src3, dst3, b, sst["blk_sst"][b]):
                        yield c

            def ssm_stream():
                for b in range(1, NBLK):
                    for c in ssm_block(src2, dst2, b, fst["blk_store"][b]):
                        yield c
                    progress["ssm"] = b

            g1, g2 = ffn_stream(), ssm_stream()
            clocks = [0.0, 0.0]
            alive = [True, True]
            blocked = [None, None]
            gens = [g1, g2]
            while any(alive):
                cands = []
                for j in range(2):
                    if not alive[j]:
                        continue
                    if blocked[j] is not None:
                        key, bb = blocked[j]
                        if progress[key] < bb:
                            continue
                        blocked[j] = None
                        clocks[j] = max(clocks)
                    cands.append(j)
                if not cands:
                    raise RuntimeError("scheduler deadlock")
                j = min(cands, key=lambda q: clocks[q])
                try:
                    c = next(gens[j])
                    if isinstance(c, tuple):
                        blocked[j] = (c[1], c[2])
                    else:
                        clocks[j] += (c or 0.0)
                except StopIteration:
                    alive[j] = False
        else:
            if use_ffn:
                for b in range(NBLK):
                    drain(ffn_block(src1, dst1, b))
                barrier()
            if use_ssm:
                for b in range(NBLK):
                    drain(ssm_block(src2, dst2, b, stdep()))
                barrier()
            if use_ffn2:
                for b in range(NBLK):
                    drain(ffn_block(src3, dst3, b, sstdep()))
            else:
                cp = sbuf("cpbuf", [128, D], F32)
                for r in range(0, T, 128):
                    tld = prog.emit("gpsimd", lambda e, r=r: e.dma_start(
                        out=cp[:], in_=src3[r:r + 128, :]),
                        waits=[(s_st[0], s_st[0].n)], incs=[(s_ldx[0], 16)])
                    prog.emit("gpsimd", lambda e, r=r: e.dma_start(
                        out=out[r:r + 128, :], in_=cp[:]),
                        waits=after(tld), incs=[(s_st[0], 16)])

        barrier()

        with nc.Block() as block:
            @block.sync
            def _(eng):
                prog.replay("sync", eng)

            @block.gpsimd
            def _(eng):
                prog.replay("gpsimd", eng)

            @block.tensor
            def _(eng):
                prog.replay("tensor", eng)

            @block.vector
            def _(eng):
                prog.replay("vector", eng)

            @block.scalar
            def _(eng):
                prog.replay("scalar", eng)

    return nc


_nc_cache = {}


def _get_nc(phases):
    key = tuple(phases)
    if key not in _nc_cache:
        _nc_cache[key] = build_program(phases)
    return _nc_cache[key]


def kernel(L0_attn_x, L0_attn_Wdt, L0_attn_bdt, L0_attn_WB, L0_attn_WC,
           L0_attn_A_log, L0_attn_Dp, L0_ffn_norm_weight, L0_ffn_router_w,
           L0_ffn_sw1, L0_ffn_sb1, L0_ffn_sw2, L0_ffn_sb2, L0_ffn_ew1,
           L0_ffn_eb1, L0_ffn_ew2, L0_ffn_eb2, _phases=("ffn1", "ssm", "ffn2")):
    x = np.ascontiguousarray(np.asarray(L0_attn_x, dtype=np.float32))
    W1 = np.concatenate([np.asarray(L0_ffn_ew1, dtype=np.float32),
                         np.asarray(L0_ffn_sw1, dtype=np.float32)[None]], axis=0)
    W2 = np.concatenate([np.asarray(L0_ffn_ew2, dtype=np.float32),
                         np.asarray(L0_ffn_sw2, dtype=np.float32)[None]], axis=0)
    b1c = np.concatenate([np.asarray(L0_ffn_eb1, dtype=np.float32),
                          np.asarray(L0_ffn_sb1, dtype=np.float32)[None]], axis=0)
    b2c = np.concatenate([np.asarray(L0_ffn_eb2, dtype=np.float32),
                          np.asarray(L0_ffn_sb2, dtype=np.float32)[None]], axis=0)
    common = {
        "Wdt": np.ascontiguousarray(L0_attn_Wdt, dtype=np.float32),
        "bdt": np.ascontiguousarray(np.asarray(L0_attn_bdt, dtype=np.float32).reshape(1, D)),
        "WB": np.ascontiguousarray(L0_attn_WB, dtype=np.float32),
        "WC": np.ascontiguousarray(L0_attn_WC, dtype=np.float32),
        "A_log": np.ascontiguousarray(L0_attn_A_log, dtype=np.float32),
        "Dp": np.ascontiguousarray(np.asarray(L0_attn_Dp, dtype=np.float32).reshape(1, D)),
        "nw": np.ascontiguousarray(np.asarray(L0_ffn_norm_weight, dtype=np.float32).reshape(1, D)),
        "rw": np.ascontiguousarray(L0_ffn_router_w, dtype=np.float32),
        "W1": np.ascontiguousarray(W1),
        "W2": np.ascontiguousarray(W2),
        "b1": np.ascontiguousarray(b1c),
        "b2": np.ascontiguousarray(b2c),
        "ident": np.eye(128, dtype=np.float32),
    }
    nc = _get_nc(_phases)
    in_maps = []
    for c in range(8):
        m = dict(common)
        m["x"] = np.ascontiguousarray(x[c])
        in_maps.append(m)
    res = run_bass_kernel_spmd(nc, in_maps, core_ids=list(range(8)))
    outs = [np.asarray(res.results[c]["out"]) for c in range(8)]
    return np.stack(outs, axis=0).astype(np.float32)


# version 10
# speedup vs baseline: 1.0928x; 1.0212x over previous
import numpy as np
from contextlib import ExitStack

import concourse.bass as bass
import concourse.mybir as mybir
from concourse.bass_utils import run_bass_kernel_spmd

F32 = mybir.dt.float32
BF16 = mybir.dt.bfloat16
AF = mybir.ActivationFunctionType
ALU = mybir.AluOpType
AX = mybir.AxisListType

T = 4096
D = 1024
H = 4096
NEXP = 9
N16 = 16
BT = 1024
NBLK = T // BT
NTT = BT // 128
NQ = 4
QH = H // NQ
TC = 32
NCH = BT // TC
GRP = 2
RING = 8
ENGINES = ("sync", "gpsimd", "tensor", "vector", "scalar")

K1DT = 0.5 / (0.0495 + 1e-8)
K2DT = 0.0495 / 0.5
CDT = 0.0505


class SemT:
    def __init__(self, h):
        self.h = h
        self.n = 0


class Prog:
    def __init__(self):
        self.events = []

    def set_selfsems(self, m):
        self._selfsems = m

    def emit(self, engine, fn, waits=(), incs=()):
        waits = list(waits)
        incs = list(incs)
        ent = getattr(self, "_selfsems", {}).get(engine)
        if ent is not None:
            gen, do_wait = ent
            assert not incs
            incs = [(gen, 1)]
            if do_wait:
                waits.append((gen, gen.n))
        post = []
        for s, k in incs:
            s.n += k
            post.append((s, s.n))
        assert len(incs) <= 1
        self.events.append((engine, fn, waits, incs))
        return post

    def replay(self, engine, eng):
        for (e, fn, waits, incs) in self.events:
            if e != engine:
                continue
            for (s, v) in waits:
                eng.wait_ge(s.h, v)
            inst = fn(eng)
            for (s, k) in incs:
                if inst is None:
                    eng.sem_inc(s.h, k)
                else:
                    inst.then_inc(s.h, k)


def after(tok):
    return [(s, v) for (s, v) in tok]


def maxwaits(toks):
    best = {}
    for (s, v) in toks:
        if id(s) not in best or best[id(s)][1] < v:
            best[id(s)] = (s, v)
    return list(best.values())


def merge(gens):
    clocks = [0.0] * len(gens)
    alive = [True] * len(gens)
    while any(alive):
        i = min((j for j in range(len(gens)) if alive[j]), key=lambda j: clocks[j])
        try:
            c = next(gens[i])
            clocks[i] += (c or 0.0)
        except StopIteration:
            alive[i] = False


def drain(gen):
    for _ in gen:
        pass


def build_program(phases=("ffn1", "ssm", "ffn2")):
    nc = bass.Bass("TRN2", target_bir_lowering=False)

    xin = nc.dram_tensor("x", [T, D], F32, kind="ExternalInput").ap()
    Wdt = nc.dram_tensor("Wdt", [D, D], F32, kind="ExternalInput").ap()
    bdt = nc.dram_tensor("bdt", [1, D], F32, kind="ExternalInput").ap()
    WB = nc.dram_tensor("WB", [N16, D], F32, kind="ExternalInput").ap()
    WC = nc.dram_tensor("WC", [N16, D], F32, kind="ExternalInput").ap()
    A_log = nc.dram_tensor("A_log", [D, N16], F32, kind="ExternalInput").ap()
    Dp = nc.dram_tensor("Dp", [1, D], F32, kind="ExternalInput").ap()
    nw = nc.dram_tensor("nw", [1, D], F32, kind="ExternalInput").ap()
    rw = nc.dram_tensor("rw", [8, D], F32, kind="ExternalInput").ap()
    W1 = nc.dram_tensor("W1", [NEXP, D, H], F32, kind="ExternalInput").ap()
    W2 = nc.dram_tensor("W2", [NEXP, H, D], F32, kind="ExternalInput").ap()
    b1 = nc.dram_tensor("b1", [NEXP, H], F32, kind="ExternalInput").ap()
    b2 = nc.dram_tensor("b2", [NEXP, D], F32, kind="ExternalInput").ap()
    ident_in = nc.dram_tensor("ident", [128, 128], F32, kind="ExternalInput").ap()

    out = nc.dram_tensor("out", [T, D], F32, kind="ExternalOutput").ap()
    xr1 = nc.dram_tensor("xr1", [T, D], F32, kind="Internal").ap()
    xr2 = nc.dram_tensor("xr2", [T, D], F32, kind="Internal").ap()
    bounce = nc.dram_tensor("bounce", [2, N16 * TC], BF16, kind="Internal").ap()
    w1bf = nc.dram_tensor("w1bf", [NEXP, NQ, 128, 8, QH], BF16, kind="Internal").ap()
    w2bf = nc.dram_tensor("w2bf", [NEXP, NQ, 128, 8, D], BF16, kind="Internal").ap()
    wdtd = nc.dram_tensor("wdtd", [8, 128, 8, 128], BF16, kind="Internal").ap()
    zcd = nc.dram_tensor("zcd", [NCH, 128, TC, 128], BF16, kind="Internal").ap()

    prog = Prog()
    ctx = ExitStack()

    def sbuf(name, shape, dt):
        return ctx.enter_context(nc.sbuf_tensor(name, shape, dt))

    def psum(name, shape, dt=F32):
        return ctx.enter_context(nc.psum_tensor(name, shape, dt))

    def sem(name):
        return SemT(ctx.enter_context(nc.semaphore(name)))

    with ctx:
        identf = sbuf("identf", [128, 128], F32)
        ones_row = sbuf("ones_row", [1, 128], F32)
        ones_bf = sbuf("ones_bf", [1, 128], BF16)
        o400 = sbuf("o400", [128, 128], BF16)
        one_col = sbuf("one_col", [128, 1], F32)
        tanh_bias = sbuf("tanh_bias", [128, 1], F32)
        rwTf = sbuf("rwTf", [128, 8, 8], F32)
        nwT = sbuf("nwT", [128, 8], F32)
        eb1T = sbuf("eb1T", [128, 32, NEXP], F32)
        eb2x = sbuf("eb2x", [NEXP, D], BF16)
        a1t = sbuf("a1t", [128, 128], F32)
        dpT = sbuf("dpT", [128, 8], F32)
        bdtT = sbuf("bdtT", [128, 8], F32)

        s_pe = sem("s_pe")
        s_v = sem("s_v")
        s_sc = sem("s_sc")
        s_ld = sem("s_ld")
        s_ldx = [sem("s_ldx0"), sem("s_ldx1")]
        s_sxt = sem("s_sxt")
        s_ldw = [sem("s_ldw0"), sem("s_ldw1"), sem("s_ldw2"), sem("s_ldw3")]
        s_wdt = sem("s_wdt")
        s_wdl = sem("s_wdl")
        s_zcs = [sem("s_zcs0"), sem("s_zcs1")]
        s_zcl = [sem("s_zcl0"), sem("s_zcl1")]
        s_st = [sem("s_st0"), sem("s_st1")]
        s_sst = sem("s_sst")
        s_bnc = sem("s_bnc")
        s_pw = [sem(f"s_pw{i}") for i in range(4)]
        s_ps = [sem(f"s_ps{i}") for i in range(4)]
        prog.set_selfsems({"vector": (s_v, True), "scalar": (s_sc, True),
                           "tensor": (s_pe, False)})

        ph = [psum("ph0", [128, 512]), psum("ph1", [128, 512])]
        po = [psum("po0", [128, 512]), psum("po1", [128, 512])]
        pns = [psum("pns0", [128, 512]), psum("pns1", [128, 512])]
        pmisc = psum("pmisc", [128, 512])
        pk1 = psum("pk1", [128, 512])
        pr8 = po[0][:, 0:8]
        pt9 = po[1][0:9, 0:128]

        p0 = ExitStack()
        stg = p0.enter_context(nc.sbuf_tensor("p0stage", [128, 1024], F32))
        stgw = p0.enter_context(nc.sbuf_tensor("p0stagew", [16, 1024], F32))
        stgb = p0.enter_context(nc.sbuf_tensor("p0stageb", [128, 1024], BF16))
        stgb1 = p0.enter_context(nc.sbuf_tensor("p0stgb1", [9, 4096], F32))

        t_id = prog.emit("sync", lambda e: e.dma_start(out=identf[:], in_=ident_in[:, :]),
                         waits=[(s_ld, s_ld.n)], incs=[(s_ld, 16)])
        prog.emit("vector", lambda e: e.memset(ones_row[:], 1.0))
        prog.emit("vector", lambda e: e.memset(ones_bf[:], 1.0))
        prog.emit("vector", lambda e: e.memset(o400[:], 1.0 / 400.0))
        prog.emit("vector", lambda e: e.memset(one_col[:], 1.0))
        t_ms = prog.emit("vector", lambda e: e.memset(tanh_bias[:], -CDT * K1DT))

        def row_to_dmajor(src_row, dst):
            tl = prog.emit("sync", lambda e: e.dma_start(
                out=stgw[0:8, 0:128],
                in_=src_row.rearrange("o (c p) -> (o c) p", c=8)),
                waits=[(s_ld, s_ld.n), (s_pe, s_pe.n)], incs=[(s_ld, 16)])
            tp = prog.emit("tensor", lambda e: e.transpose(
                pmisc[:, 0:8], stgw[0:8, 0:128], identf[0:8, 0:8]),
                waits=after(tl) + after(t_id) + [(s_v, s_v.n)])
            return prog.emit("vector", lambda e: e.tensor_copy(out=dst, in_=pmisc[:, 0:8]),
                             waits=after(tp))

        tok_nw = row_to_dmajor(nw, nwT[:, :])
        tok_dp = row_to_dmajor(Dp, dpT[:, :])
        tok_bdt = row_to_dmajor(bdt, bdtT[:, :])

        tl = prog.emit("sync", lambda e: e.dma_start(
            out=a1t[:].rearrange("p (c n) -> p c n", c=8),
            in_=A_log.rearrange("(c p) n -> p c n", p=128)),
            waits=[(s_ld, s_ld.n)], incs=[(s_ld, 16)])
        tsa = prog.emit("scalar", lambda e: e.activation(
            out=a1t[:], in_=a1t[:], func=AF.Tanh, scale=0.1),
            waits=after(tl))
        tsb = prog.emit("scalar", lambda e: e.activation(
            out=a1t[:], in_=a1t[:], func=AF.Exp, scale=10.0),
            waits=after(tsa))
        tok_a1 = prog.emit("vector", lambda e: e.tensor_scalar_mul(a1t[:], a1t[:], -1.0),
                           waits=after(tsb))

        tok_rwT = []
        tl = prog.emit("sync", lambda e: e.dma_start(out=stgw[0:8, :], in_=rw[:, :]),
                       waits=[(s_ld, s_ld.n), (s_pe, s_pe.n)], incs=[(s_ld, 16)])
        for cd in range(8):
            tp = prog.emit("tensor", lambda e, cd=cd: e.transpose(
                pmisc[:, 0:8], stgw[0:8, cd * 128:(cd + 1) * 128], identf[0:8, 0:8]),
                waits=after(tl) + after(tok_nw) + [(s_v, s_v.n)])
            tok_rwT.append(prog.emit("vector", lambda e, cd=cd: e.tensor_scalar(
                out=rwTf[:, cd, :], in0=pmisc[:, 0:8],
                scalar1=nwT[:, cd:cd + 1], scalar2=None, op0=ALU.mult),
                waits=after(tp)))

        tl_b1 = prog.emit("sync", lambda e: e.dma_start(out=stgb1[:, :], in_=b1[:, :]),
                          waits=[(s_ld, s_ld.n), (s_pe, s_pe.n)], incs=[(s_ld, 16)])
        tok_eb1 = []
        for ch in range(32):
            tp = prog.emit("tensor", lambda e, ch=ch: e.transpose(
                pmisc[:, 0:9], stgb1[0:9, ch * 128:(ch + 1) * 128], identf[0:9, 0:9]),
                waits=after(tl_b1) + [(s_v, s_v.n)])
            tok_eb1.append(prog.emit("vector", lambda e, ch=ch: e.tensor_copy(
                out=eb1T[:, ch, :], in_=pmisc[:, 0:9]),
                waits=after(tp)))
        tl_b2 = prog.emit("sync", lambda e: e.dma_start(out=stgw[0:9, :], in_=b2[:, :]),
                          waits=[(s_ld, s_ld.n), (s_pe, s_pe.n), (s_v, s_v.n)],
                          incs=[(s_ld, 16)])
        tok_eb2 = prog.emit("vector", lambda e: e.tensor_copy(out=eb2x[:, :], in_=stgw[0:9, :]),
                            waits=after(tl_b2))

        wdt_st_tok = []
        for co in range(8):
            tl = prog.emit("sync", lambda e, co=co: e.dma_start(
                out=stg[:], in_=Wdt[co * 128:(co + 1) * 128, :]),
                waits=[(s_ld, s_ld.n), (s_v, s_v.n), (s_pe, s_pe.n)],
                incs=[(s_ld, 16)])
            tcps = []
            for cd in range(8):
                tp = prog.emit("tensor", lambda e, cd=cd: e.transpose(
                    pmisc[:, 0:128], stg[:, cd * 128:(cd + 1) * 128], identf[:]),
                    waits=after(tl) + [(s_v, s_v.n)])
                tcps.append(prog.emit("vector", lambda e, cd=cd: e.tensor_copy(
                    out=stgb[:, cd * 128:(cd + 1) * 128], in_=pmisc[:, 0:128]),
                    waits=after(tp) + (wdt_st_tok if cd == 0 else [])))
            wdt_st_tok = after(prog.emit("gpsimd", lambda e, co=co: e.dma_start(
                out=wdtd[co].rearrange("p c d -> p (c d)"), in_=stgb[:]),
                waits=after(tcps[-1]), incs=[(s_wdt, 16)]))
        wdt_done = s_wdt.n

        pstg = [p0.enter_context(nc.sbuf_tensor(f"pstg{i}", [128, 4, 1024], F32)) for i in range(4)]
        pcv = [p0.enter_context(nc.sbuf_tensor(f"pcv{i}", [128, 4, 1024], BF16)) for i in range(4)]
        last_conv = {}
        last_store = {}
        for sl in range(NEXP):
            for q in range(NQ):
                for g4 in range(2):
                    chn = g4 % 2
                    wld = prog.emit("sync", lambda e, sl=sl, q=q, g4=g4, chn=chn: e.dma_start(
                        out=pstg[chn][:],
                        in_=W1[sl][g4 * 512:(g4 + 1) * 512, q * QH:(q + 1) * QH]
                            .rearrange("(a p) h -> p a h", p=128)),
                        waits=last_conv.get(chn, []), incs=[(s_pw[chn], 16)])
                    w_ = after(wld) + last_store.get(chn, [])
                    if sl == 0 and q == 0 and g4 == 0:
                        w_ += after(tok_nw)
                    tcv = None
                    for a in range(4):
                        tcv = prog.emit("vector", lambda e, g4=g4, a=a, chn=chn: e.tensor_scalar(
                            out=pcv[chn][:, a, :], in0=pstg[chn][:, a, :],
                            scalar1=nwT[:, g4 * 4 + a:g4 * 4 + a + 1],
                            scalar2=None, op0=ALU.mult),
                            waits=w_ if a == 0 else ())
                    last_conv[chn] = after(tcv)
                    tst = prog.emit("gpsimd", lambda e, sl=sl, q=q, g4=g4, chn=chn: e.dma_start(
                        out=w1bf[sl, q, :, g4 * 4:(g4 + 1) * 4, :], in_=pcv[chn][:]),
                        waits=after(tcv), incs=[(s_ps[chn], 16)])
                    last_store[chn] = after(tst)
                for g4 in range(2):
                    chn = 2 + g4 % 2
                    wld = prog.emit("sync", lambda e, sl=sl, q=q, g4=g4, chn=chn: e.dma_start(
                        out=pstg[chn][:],
                        in_=W2[sl][q * QH + g4 * 512:q * QH + (g4 + 1) * 512, :]
                            .rearrange("(a p) h -> p a h", p=128)),
                        waits=last_conv.get(chn, []), incs=[(s_pw[chn], 16)])
                    tcv = prog.emit("scalar", lambda e, chn=chn: e.activation(
                        out=pcv[chn][:].rearrange("p a h -> p (a h)"),
                        in_=pstg[chn][:].rearrange("p a h -> p (a h)"), func=AF.Copy),
                        waits=after(wld) + last_store.get(chn, []))
                    last_conv[chn] = after(tcv)
                    tst = prog.emit("gpsimd", lambda e, sl=sl, q=q, g4=g4, chn=chn: e.dma_start(
                        out=w2bf[sl, q, :, g4 * 4:(g4 + 1) * 4, :], in_=pcv[chn][:]),
                        waits=after(tcv), incs=[(s_ps[chn], 16)])
                    last_store[chn] = after(tst)

        allc = (after(tok_a1) + after(tok_dp) + after(tok_bdt) + after(tok_eb2)
                + [(s_wdt, wdt_done)]
                + [(s_ps[i], s_ps[i].n) for i in range(4)]
                + [(s_pw[i], s_pw[i].n) for i in range(4)]
                + [(s_v, s_v.n), (s_sc, s_sc.n), (s_pe, s_pe.n), (s_ld, s_ld.n)])
        for t_ in tok_rwT + tok_eb1:
            allc += after(t_)
        const_waits = maxwaits(allc)
        for _eng in ENGINES:
            prog.emit(_eng, lambda e: None, waits=list(const_waits))
        p0.close()

        nrmT = sbuf("nrmT", [128, 8, BT], BF16)
        nrmTf = sbuf("nrmTf", [128, 8, 128], F32)
        accb = sbuf("accb", [128, NTT, D], BF16)
        g = sbuf("gq", [128, 8, BT], BF16)
        w1q = sbuf("w1q", [128, 8, QH], BF16)
        w2q = sbuf("w2q", [128, 8, D], BF16)
        xt = [sbuf("xt0", [128, D], F32), sbuf("xt1", [128, D], F32)]
        nrm_s = sbuf("nrm_s", [128, D], F32)
        stats = sbuf("stats", [128, 12], F32)
        logit = sbuf("logit", [128, NTT, 8], F32)
        mx8 = sbuf("mx8", [128, NTT, 8], F32)
        s12 = sbuf("s12", [128, NTT, 2], F32)
        wts = sbuf("wts", [128, NTT, 4], F32)
        msk1 = sbuf("msk1", [128, NTT, 8], F32)
        msk2 = sbuf("msk2", [128, NTT, 8], F32)
        comb = sbuf("comb", [128, NTT, NEXP], F32)
        combT = sbuf("combT", [NEXP, BT], BF16)

        fst = {
            "xt_last": {},
            "evh": {},
            "evo": {},
            "pe_q": {},
            "qc": 0,
            "w1tok": None, "w2tok": None,
            "lnv": {},
        }

        def ffn_block(src, dst, blk, dep=()):
            dep = list(dep)
            base = blk * BT

            def flush():
                for fn_ in fst.get("pending", []):
                    fn_()
                fst["pending"] = []

            def defer(fn_):
                fst.setdefault("pending", []).append(fn_)
            flush()
            tok_tcp = None
            tok_tlg = None
            for tt in range(NTT):
                xb = xt[tt % 2]
                row0 = base + tt * 128
                w_ = list(fst["xt_last"].get(tt % 2, [])) + dep
                tld = prog.emit("gpsimd", lambda e, xb=xb, row0=row0: e.dma_start(
                    out=xb[:], in_=src[row0:row0 + 128, :]),
                    waits=w_, incs=[(s_ldx[tt % 2], 16)])
                tsum = prog.emit("vector", lambda e, xb=xb: e.reduce_sum(
                    out=stats[:, 0:1], in_=xb[:], axis=AX.X),
                    waits=after(tld))
                tsq = prog.emit("scalar", lambda e, xb=xb: e.activation(
                    out=nrm_s[:], in_=xb[:], func=AF.Square,
                    accum_out=stats[:, 1:2]),
                    waits=after(tld) + [(s_v, s_v.n)])
                tv = prog.emit("vector", lambda e: e.tensor_scalar_mul(
                    stats[:, 2:3], stats[:, 0:1], 1.0 / D), waits=after(tsum))
                tv = prog.emit("vector", lambda e: e.tensor_scalar_mul(
                    stats[:, 3:4], stats[:, 1:2], 1.0 / D), waits=after(tsq))
                tv = prog.emit("vector", lambda e: e.tensor_mul(
                    stats[:, 4:5], stats[:, 2:3], stats[:, 2:3]))
                tv = prog.emit("vector", lambda e: e.tensor_sub(
                    stats[:, 5:6], stats[:, 3:4], stats[:, 4:5]))
                tv = prog.emit("vector", lambda e: e.tensor_scalar_add(
                    stats[:, 6:7], stats[:, 5:6], 1e-5))
                tv = prog.emit("vector", lambda e: e.reciprocal(
                    out=stats[:, 7:8], in_=stats[:, 6:7]))
                tv = prog.emit("vector", lambda e: e.tensor_mul(
                    stats[:, 8:9], stats[:, 5:6], stats[:, 7:8]))
                tv = prog.emit("vector", lambda e: e.tensor_scalar_add(
                    stats[:, 9:10], stats[:, 8:9], 1e-8))
                tv9 = prog.emit("vector", lambda e: e.tensor_mul(
                    stats[:, 10:11], stats[:, 6:7], stats[:, 9:10]))
                tl1 = prog.emit("scalar", lambda e: e.activation(
                    out=stats[:, 11:12], in_=stats[:, 10:11], func=AF.Ln),
                    waits=after(tv9))
                tl2 = prog.emit("scalar", lambda e: e.activation(
                    out=stats[:, 10:11], in_=stats[:, 11:12], func=AF.Exp,
                    scale=-0.5))
                tnrm = prog.emit("vector", lambda e, xb=xb: e.scalar_tensor_tensor(
                    out=nrm_s[:], in0=xb[:], scalar=stats[:, 2:3],
                    in1=stats[:, 10:11].to_broadcast([128, D]),
                    op0=ALU.subtract, op1=ALU.mult),
                    waits=after(tl2))
                fst["xt_last"][tt % 2] = after(tnrm) + after(tsq)
                ttr = {}
                for cdx in range(8):
                    pb = po[cdx // 4]
                    w_ = after(tnrm)
                    if cdx == 0:
                        for par in (0, 1):
                            if par in fst["evo"]:
                                w_.append((s_v, fst["evo"][par]))
                        if tok_tlg is not None:
                            w_ += after(tok_tlg)
                    ttr[cdx] = prog.emit("tensor", lambda e, cdx=cdx, pb=pb: e.transpose(
                        pb[:, (cdx % 4) * 128:(cdx % 4 + 1) * 128],
                        nrm_s[:, cdx * 128:(cdx + 1) * 128], identf[:]),
                        waits=w_)
                tok_tcp = prog.emit("vector", lambda e, tt=tt: e.tensor_copy(
                    out=nrmT[:, 0:4, tt * 128:(tt + 1) * 128],
                    in_=po[0][:].rearrange("p (c f) -> p c f", c=4)),
                    waits=after(ttr[3]))
                tok_tcp2 = prog.emit("vector", lambda e, tt=tt: e.tensor_copy(
                    out=nrmT[:, 4:8, tt * 128:(tt + 1) * 128],
                    in_=po[1][:].rearrange("p (c f) -> p c f", c=4)),
                    waits=after(ttr[7]))
                tcpf = prog.emit("vector", lambda e: e.tensor_copy(
                    out=nrmTf[:, 0:4, :],
                    in_=po[0][:].rearrange("p (c f) -> p c f", c=4)))
                tcpf2 = prog.emit("vector", lambda e: e.tensor_copy(
                    out=nrmTf[:, 4:8, :],
                    in_=po[1][:].rearrange("p (c f) -> p c f", c=4)))
                trt = None
                for cd in range(8):
                    w_ = after(tcpf2) if cd == 0 else ()
                    trt = prog.emit("tensor", lambda e, cd=cd: e.matmul(
                        pr8, nrmTf[:, cd, :], rwTf[:, cd, :],
                        start=(cd == 0), stop=(cd == 7)),
                        waits=w_)
                tok_tlg = prog.emit("vector", lambda e, tt=tt: e.tensor_copy(
                    out=logit[:, tt, :], in_=pr8),
                    waits=after(trt))
                prog.emit("vector", lambda e, tt=tt: e.max(
                    out=mx8[:, tt, :], in_=logit[:, tt, :]))
                fst["evo"][0] = s_v.n
                fst["evo"][1] = s_v.n
                yield 2600.0

            te = prog.emit("scalar", lambda e: e.activation(
                out=s12[:, :, :], in_=mx8[:, :, 0:2], func=AF.Exp, scale=-1.0),
                waits=[(s_v, s_v.n)])
            tv = prog.emit("vector", lambda e: e.tensor_scalar_add(
                s12[:, :, :], s12[:, :, :], 1.0), waits=after(te))
            tv = prog.emit("vector", lambda e: e.reciprocal(
                out=s12[:, :, :], in_=s12[:, :, :]))
            tv = prog.emit("vector", lambda e: e.tensor_add(
                wts[:, :, 0], s12[:, :, 0], s12[:, :, 1]))
            tv = prog.emit("vector", lambda e: e.tensor_scalar_add(
                wts[:, :, 0], wts[:, :, 0], 1e-6))
            tv = prog.emit("vector", lambda e: e.reciprocal(
                out=wts[:, :, 1], in_=wts[:, :, 0]))
            tv = prog.emit("vector", lambda e: e.tensor_mul(
                wts[:, :, 2], s12[:, :, 0], wts[:, :, 1]))
            tv = prog.emit("vector", lambda e: e.tensor_mul(
                wts[:, :, 3], s12[:, :, 1], wts[:, :, 1]))
            tv = prog.emit("vector", lambda e: e.tensor_tensor(
                out=msk1[:], in0=logit[:],
                in1=mx8[:, :, 0:1].to_broadcast([128, NTT, 8]), op=ALU.is_ge))
            tv = prog.emit("vector", lambda e: e.tensor_tensor(
                out=msk2[:], in0=logit[:],
                in1=mx8[:, :, 1:2].to_broadcast([128, NTT, 8]), op=ALU.is_ge))
            tv = prog.emit("vector", lambda e: e.tensor_sub(
                msk2[:], msk2[:], msk1[:]))
            tv = prog.emit("vector", lambda e: e.tensor_tensor(
                out=msk1[:], in0=msk1[:],
                in1=wts[:, :, 2:3].to_broadcast([128, NTT, 8]), op=ALU.mult))
            tv = prog.emit("vector", lambda e: e.tensor_tensor(
                out=msk2[:], in0=msk2[:],
                in1=wts[:, :, 3:4].to_broadcast([128, NTT, 8]), op=ALU.mult))
            tv = prog.emit("vector", lambda e: e.tensor_add(
                comb[:, :, 0:8], msk1[:], msk2[:]))
            tcomb = prog.emit("vector", lambda e: e.memset(comb[:, :, 8:9], 1.0))
            yield 2000.0

            for tt in range(NTT):
                w_ = [(s_v, s_v.n)]
                tp = prog.emit("tensor", lambda e, tt=tt: e.transpose(
                    pt9, comb[:, tt, :], identf[:]), waits=w_)
                tcc = prog.emit("vector", lambda e, tt=tt: e.tensor_copy(
                    out=combT[:, tt * 128:(tt + 1) * 128], in_=pt9),
                    waits=after(tp))
                tbs = {}
                for hf in range(2):
                    w_ = after(tcc)
                    if hf == 0 and (0 in fst["evo"]):
                        w_.append((s_v, fst["evo"][0]))
                    if hf == 1 and (1 in fst["evo"]):
                        w_.append((s_v, fst["evo"][1]))
                    tbs[hf] = prog.emit("tensor", lambda e, tt=tt, hf=hf: e.matmul(
                        po[hf][:], combT[:, tt * 128:(tt + 1) * 128],
                        eb2x[:, hf * 512:(hf + 1) * 512],
                        start=True, stop=True),
                        waits=w_)
                for hf in range(2):
                    tev = prog.emit("vector", lambda e, tt=tt, hf=hf: e.tensor_copy(
                        out=accb[:, tt, hf * 512:(hf + 1) * 512], in_=po[hf][:]),
                        waits=after(tbs[hf]))
                    fst["evo"][hf] = s_v.n
                yield 1500.0

            for sl in range(NEXP):
                for q in range(NQ):
                    qc = fst["qc"]
                    fst["qc"] += 1
                    wpd = []
                    t_w1 = prog.emit("sync", lambda e, sl=sl, q=q: e.dma_start(
                        out=w1q[:], in_=w1bf[sl, q]),
                        waits=[(s_pe, fst["pe_q"].get(("mm1", qc - 1), 0))] + wpd,
                        incs=[(s_ldw[qc % 2], 16)])
                    t_w2 = prog.emit("sync", lambda e, sl=sl, q=q: e.dma_start(
                        out=w2q[:], in_=w2bf[sl, q]),
                        waits=[(s_pe, fst["pe_q"].get(("mm2", qc - 1), 0))],
                        incs=[(s_ldw[2 + qc % 2], 16)])
                    fst["w1tok"] = after(t_w1)
                    fst["w2tok"] = after(t_w2)

                    for i in range(8):
                        for t2 in range(2):
                            par = (i * 2 + t2) % 2
                            hb = ph[par]
                            flush()
                            mmlast = None
                            for c in range(8):
                                w_ = []
                                if c == 0:
                                    if i == 0 and t2 == 0:
                                        w_ += fst["w1tok"]
                                    if ("h", par) in fst["evh"]:
                                        w_.append((s_sc, fst["evh"][("h", par)]))
                                mmlast = prog.emit("tensor",
                                    lambda e, i=i, c=c, t2=t2, hb=hb: e.matmul(
                                        hb[:],
                                        w1q[:, c, i * 128:(i + 1) * 128],
                                        nrmT[:, c, t2 * 512:(t2 + 1) * 512],
                                        start=(c == 0), stop=(c == 7)),
                                    waits=w_)
                                if c == 3:
                                    yield 900.0

                            def ev1(i=i, t2=t2, hb=hb, q=q, sl=sl, par=par, mmlast=mmlast):
                                prog.emit("scalar", lambda e: e.activation(
                                    out=g[:, i, t2 * 512:(t2 + 1) * 512],
                                    in_=hb[:], func=AF.Relu,
                                    bias=eb1T[:, q * 8 + i, sl:sl + 1]),
                                    waits=after(mmlast))
                                fst["evh"][("h", par)] = s_sc.n
                            defer(ev1)
                            yield 1100.0
                    flush()
                    fst["pe_q"][("mm1", qc)] = s_pe.n
                    g_ready = s_sc.n

                    for tt in range(NTT):
                        for hf in range(2):
                            par = (tt * 2 + hf) % 2
                            ob = po[par]
                            flush()
                            mmlast = None
                            for i in range(8):
                                w_ = []
                                if i == 0:
                                    if tt == 0 and hf == 0:
                                        w_ = [(s_sc, g_ready)] + fst["w2tok"]
                                    if par in fst["evo"]:
                                        w_.append((s_v, fst["evo"][par]))
                                mmlast = prog.emit("tensor",
                                    lambda e, tt=tt, i=i, hf=hf, ob=ob: e.matmul(
                                        ob[:],
                                        g[:, i, tt * 128:(tt + 1) * 128],
                                        w2q[:, i, hf * 512:(hf + 1) * 512],
                                        start=(i == 0), stop=(i == 7)),
                                    waits=w_)
                                if i == 3:
                                    yield 900.0

                            def ev2(tt=tt, hf=hf, ob=ob, sl=sl, par=par, mmlast=mmlast):
                                prog.emit("vector", lambda e: e.scalar_tensor_tensor(
                                    out=accb[:, tt, hf * 512:(hf + 1) * 512], in0=ob[:],
                                    scalar=comb[:, tt, sl:sl + 1], op0=ALU.mult,
                                    in1=accb[:, tt, hf * 512:(hf + 1) * 512], op1=ALU.add),
                                    waits=after(mmlast))
                                fst["evo"][par] = s_v.n
                            defer(ev2)
                            yield 1100.0
                    flush()
                    fst["pe_q"][("mm2", qc)] = s_pe.n

            acc_done = s_v.n
            for tt in range(NTT):
                xb = xt[tt % 2]
                row0 = base + tt * 128
                w_ = list(fst["xt_last"].get(tt % 2, [])) + dep
                tld = prog.emit("gpsimd", lambda e, xb=xb, row0=row0: e.dma_start(
                    out=xb[:], in_=src[row0:row0 + 128, :]),
                    waits=w_, incs=[(s_ldx[tt % 2], 16)])
                tap = prog.emit("vector", lambda e, xb=xb, tt=tt: e.scalar_tensor_tensor(
                    out=xb[:], in0=accb[:, tt, :], scalar=0.5,
                    op0=ALU.mult, in1=xb[:], op1=ALU.add),
                    waits=after(tld) + ([(s_v, acc_done)] if tt == 0 else []))
                tstr = prog.emit("gpsimd", lambda e, xb=xb, row0=row0: e.dma_start(
                    out=dst[row0:row0 + 128, :], in_=xb[:]),
                    waits=after(tap), incs=[(s_st[tt % 2], 16)])
                fst["xt_last"][tt % 2] = after(tap) + after(tstr)
                yield 800.0
            fst.setdefault("blk_store", {})[blk] = [
                (s_st[0], s_st[0].n), (s_st[1], s_st[1].n)]

        xnT = sbuf("xnT", [128, 8, BT], BF16)
        dtb = sbuf("dtb", [128, 8, BT], BF16)
        zcl = [sbuf("zcl0", [128, TC, 128], BF16), sbuf("zcl1", [128, TC, 128], BF16)]
        zs = sbuf("zs", [128, 8, 128], F32)
        dAs = [sbuf("dAs0", [128, 128], F32), sbuf("dAs1", [128, 128], F32)]
        ub = sbuf("ub", [128, TC, 128], BF16)
        hist = sbuf("hist", [128, RING, 128], F32)
        dtx = sbuf("dtx", [128, 8, TC], F32)
        qb = [sbuf("qb0", [128, 8, TC], F32), sbuf("qb1", [128, 8, TC], F32)]
        wred = sbuf("wred", [128, 8, 8], F32)
        yb = sbuf("yb", [128, 8, TC], F32)
        Bn = sbuf("Bn", [16, BT], BF16)
        Cn = sbuf("Cn", [16, BT], BF16)
        Brep = [sbuf("Brep0", [128, N16, TC], F32), sbuf("Brep1", [128, N16, TC], F32)]
        Crep = [sbuf("Crep0", [128, N16, TC], F32), sbuf("Crep1", [128, N16, TC], F32)]
        brow = sbuf("brow", [1, 2, N16 * TC], BF16)
        shist2 = sbuf("shist2", [128, TC + 1], F32)
        sq_s = sbuf("sq_s", [128, 128], F32)
        p2 = sbuf("p2", [128, 1], F32)
        p2b = sbuf("p2b", [128, 1], BF16)
        s0 = sbuf("s0", [128, 2], F32)
        xrow = sbuf("xrow", [TC, D], F32)
        sxt = sbuf("sxt", [128, D], F32)
        snrm = sbuf("snrm", [128, D], F32)
        sstats = sbuf("sstats", [128, 12], F32)
        tmp = sbuf("stmp", [128, 128], F32)
        wdtr = [sbuf("wdtr0", [128, 8, 128], BF16)]

        sst = {
            "init": False,
            "sxt_last": [],
            "xrow_last": [],
            "pdt_last": {},
            "bc_last": {},
            "bnc_ch": [],
            "wdt_last": {},
            "pk1_last": [],
            "pns_last": {},
            "lnp_last": [],
            "dA_tok": {},
        }

        def ssm_init():
            prog.emit("vector", lambda e: e.memset(hist[:, RING - 1, :], 0.0))
            prog.emit("vector", lambda e: e.memset(shist2[:, 0:1], 1.0))
            sst["init"] = True

        def ssm_block(src, dst, blk, dep=()):
            dep = list(dep)
            if not sst["init"]:
                ssm_init()
            base = blk * BT
            for tt in range(NTT):
                row0 = base + tt * 128
                tld = prog.emit("gpsimd", lambda e, row0=row0: e.dma_start(
                    out=sxt[:], in_=src[row0:row0 + 128, :]),
                    waits=list(sst["sxt_last"]) + dep, incs=[(s_sxt, 16)])
                tsum = prog.emit("vector", lambda e: e.reduce_sum(
                    out=sstats[:, 0:1], in_=sxt[:], axis=AX.X),
                    waits=after(tld))
                tsq = prog.emit("scalar", lambda e: e.activation(
                    out=snrm[:], in_=sxt[:], func=AF.Square,
                    accum_out=sstats[:, 1:2]),
                    waits=after(tld) + [(s_v, s_v.n)])
                tv = prog.emit("vector", lambda e: e.tensor_scalar_mul(
                    sstats[:, 2:3], sstats[:, 0:1], 1.0 / D), waits=after(tsum))
                tv = prog.emit("vector", lambda e: e.tensor_scalar_mul(
                    sstats[:, 3:4], sstats[:, 1:2], 1.0 / D), waits=after(tsq))
                tv = prog.emit("vector", lambda e: e.tensor_mul(
                    sstats[:, 4:5], sstats[:, 2:3], sstats[:, 2:3]))
                tv = prog.emit("vector", lambda e: e.tensor_sub(
                    sstats[:, 5:6], sstats[:, 3:4], sstats[:, 4:5]))
                tv5 = prog.emit("vector", lambda e: e.tensor_scalar_add(
                    sstats[:, 6:7], sstats[:, 5:6], 1e-5))
                tl1 = prog.emit("scalar", lambda e: e.activation(
                    out=sstats[:, 11:12], in_=sstats[:, 6:7], func=AF.Ln),
                    waits=after(tv5))
                tl2 = prog.emit("scalar", lambda e: e.activation(
                    out=sstats[:, 10:11], in_=sstats[:, 11:12], func=AF.Exp,
                    scale=-0.5))
                tnrm = prog.emit("vector", lambda e: e.scalar_tensor_tensor(
                    out=snrm[:], in0=sxt[:], scalar=sstats[:, 2:3],
                    in1=sstats[:, 10:11].to_broadcast([128, D]),
                    op0=ALU.subtract, op1=ALU.mult),
                    waits=after(tl2))
                sst["sxt_last"] = after(tnrm) + after(tsq)
                ttr = {}
                for cdx in range(4):
                    w_ = after(tnrm)
                    if cdx == 0:
                        w_ = list(w_) + sst["lnp_last"] + [(s_sc, sst["pdt_last"].get("any", 0))]
                    ttr[cdx] = prog.emit("tensor", lambda e, cdx=cdx: e.transpose(
                        pmisc[:, cdx * 128:(cdx + 1) * 128],
                        snrm[:, cdx * 128:(cdx + 1) * 128], identf[:]),
                        waits=w_)
                tc1 = prog.emit("vector", lambda e, tt=tt: e.tensor_copy(
                    out=xnT[:, 0:4, tt * 128:(tt + 1) * 128],
                    in_=pmisc[:].rearrange("p (c f) -> p c f", c=4)),
                    waits=after(ttr[3]))
                for cdx in range(4, 8):
                    w_ = after(tc1) if cdx == 4 else ()
                    ttr[cdx] = prog.emit("tensor", lambda e, cdx=cdx: e.transpose(
                        pmisc[:, (cdx % 4) * 128:(cdx % 4 + 1) * 128],
                        snrm[:, cdx * 128:(cdx + 1) * 128], identf[:]),
                        waits=w_)
                tc2 = prog.emit("vector", lambda e, tt=tt: e.tensor_copy(
                    out=xnT[:, 4:8, tt * 128:(tt + 1) * 128],
                    in_=pmisc[:].rearrange("p (c f) -> p c f", c=4)),
                    waits=after(ttr[7]))
                sst["lnp_last"] = after(tc2)
                yield 2600.0
            xn_v = s_v.n

            for co in range(8):
                wv = wdtr[0]
                twl = prog.emit("sync", lambda e, co=co, wv=wv: e.dma_start(
                    out=wv[:].rearrange("p c d -> p (c d)"),
                    in_=wdtd[co].rearrange("p c d -> p (c d)")),
                    waits=[(s_pe, sst["wdt_last"].get(0, 0)),
                           (s_wdl, s_wdl.n)],
                    incs=[(s_wdl, 16)])
                for tq in range(2):
                    mmlast = None
                    for cd in range(8):
                        w_ = []
                        if cd == 0:
                            w_ = [(s_v, xn_v)] + after(twl)
                            w_.append((s_sc, sst["pdt_last"].get("any", 0)))
                            w_ += sst["lnp_last"]
                        mmlast = prog.emit("tensor", lambda e, co=co, cd=cd, tq=tq, wv=wv: e.matmul(
                            pmisc[:], wv[:, cd, :],
                            xnT[:, cd, tq * 512:(tq + 1) * 512],
                            start=(cd == 0), stop=(cd == 7)),
                            waits=w_)
                    tso = prog.emit("scalar", lambda e, co=co: e.activation(
                        out=pmisc[:], in_=pmisc[:], func=AF.Exp,
                        bias=bdtT[:, co:co + 1]),
                        waits=after(mmlast))
                    tso2 = prog.emit("scalar", lambda e, co=co, tq=tq: e.activation(
                        out=dtb[:, co, tq * 512:(tq + 1) * 512], in_=pmisc[:],
                        func=AF.Ln, bias=one_col[:]),
                        waits=after(tso))
                    sst["pdt_last"]["any"] = s_sc.n
                    sst["lnp_last"] = []
                    yield 1800.0
                sst["wdt_last"][0] = s_pe.n

            for (wTs, dstN) in ((wbT_s, Bn), (wcT_s, Cn)):
                for tq in range(2):
                    mmlast = None
                    for cd in range(8):
                        w_ = []
                        if cd == 0:
                            w_ = [(s_sc, sst["pdt_last"]["any"]), (s_v, s_v.n)]
                            w_ += [(s_bnc, s_bnc.n)]
                        mmlast = prog.emit("tensor", lambda e, cd=cd, tq=tq, wTs=wTs: e.matmul(
                            pmisc[0:16, :], wTs[:, cd, :],
                            xnT[:, cd, tq * 512:(tq + 1) * 512],
                            start=(cd == 0), stop=(cd == 7)),
                            waits=w_)
                    tbc = prog.emit("vector", lambda e, tq=tq, dstN=dstN: e.tensor_copy(
                        out=dstN[:, tq * 512:(tq + 1) * 512], in_=pmisc[0:16, :]),
                        waits=after(mmlast))
                    sst["bc_last"]["any"] = s_v.n
                    yield 1200.0

            tth = prog.emit("scalar", lambda e: e.activation(
                out=dtb[:].rearrange("p c f -> p (c f)"),
                in_=dtb[:].rearrange("p c f -> p (c f)"),
                func=AF.Tanh, scale=K1DT, bias=tanh_bias[:]),
                waits=[(s_v, s_v.n)])
            tdt = prog.emit("vector", lambda e: e.tensor_scalar(
                out=dtb[:].rearrange("p c f -> p (c f)"),
                in0=dtb[:].rearrange("p c f -> p (c f)"),
                scalar1=K2DT, scalar2=CDT, op0=ALU.mult, op1=ALU.add),
                waits=after(tth))
            yield 8000.0
            ztan_tok = {}
            for c in range(NCH):
                cpar = c % 2
                ttn = None
                for sl4 in range(TC // 4):
                    hpar = sl4 % 2
                    w_ = after(tdt) if (c == 0 and sl4 == 0) else []
                    w_ = w_ + ztan_tok.get(hpar, [])
                    tz = prog.emit("vector", lambda e, c=c, sl4=sl4, hpar=hpar: e.tensor_tensor(
                        out=zs[:, hpar * 4:(hpar + 1) * 4, :]
                            .rearrange("p t (cc n) -> p t cc n", cc=8),
                        in0=dtb[:, :, c * TC + sl4 * 4:c * TC + (sl4 + 1) * 4]
                            .rearrange("p cc t -> p t cc")[:, :, :, None]
                            .to_broadcast([128, 4, 8, 16]),
                        in1=a1t[:].rearrange("p (cc n) -> p cc n", cc=8)[:, None, :, :]
                            .to_broadcast([128, 4, 8, 16]),
                        op=ALU.mult),
                        waits=w_)
                    w2_ = after(tz)
                    if sl4 <= 1:
                        w2_ = w2_ + sst.get("zcs_war", {}).get(cpar, [])
                    ttn = prog.emit("scalar", lambda e, cpar=cpar, sl4=sl4, hpar=hpar: e.activation(
                        out=zcl[cpar][:, sl4 * 4:(sl4 + 1) * 4, :]
                            .rearrange("p t f -> p (t f)"),
                        in_=zs[:, hpar * 4:(hpar + 1) * 4, :]
                            .rearrange("p t f -> p (t f)"),
                        func=AF.Tanh, scale=1.0 / 16.0),
                        waits=w2_)
                    ztan_tok[hpar] = after(ttn)
                tzs = prog.emit("gpsimd", lambda e, c=c, cpar=cpar: e.dma_start(
                    out=zcd[c].rearrange("p t f -> p (t f)"),
                    in_=zcl[cpar][:].rearrange("p t f -> p (t f)")),
                    waits=after(ttn) + [(s_zcs[cpar], s_zcs[cpar].n)],
                    incs=[(s_zcs[cpar], 16)])
                sst.setdefault("zcs_war", {})[cpar] = after(tzs)
                yield 2200.0
            sst["zcl_tok"] = {}
            for c in (0, 1):
                cpar = c % 2
                tzl = prog.emit("sync", lambda e, c=c, cpar=cpar: e.dma_start(
                    out=zcl[cpar][:].rearrange("p t f -> p (t f)"),
                    in_=zcd[c].rearrange("p t f -> p (t f)")),
                    waits=[(s_zcs[cpar], s_zcs[cpar].n),
                           (s_zcl[cpar], s_zcl[cpar].n)],
                    incs=[(s_zcl[cpar], 16)])
                sst["zcl_tok"][cpar] = after(tzl)

            def prep_bounce(c):
                t0c = c * TC
                tb1 = prog.emit("sync", lambda e, t0c=t0c: e.dma_start(
                    out=bounce[0, :].rearrange("(n t) -> n t", n=16),
                    in_=Bn[:, t0c:t0c + TC]),
                    waits=[(s_bnc, s_bnc.n),
                           (s_v, sst["bc_last"].get("any", 0))]
                          + sst.get("brow_rd", []),
                    incs=[(s_bnc, 16)])
                tb2 = prog.emit("sync", lambda e, t0c=t0c: e.dma_start(
                    out=bounce[1, :].rearrange("(n t) -> n t", n=16),
                    in_=Cn[:, t0c:t0c + TC]),
                    waits=(), incs=[(s_bnc, 16)])
                tb3 = prog.emit("sync", lambda e: e.dma_start(
                    out=brow[:, :, :], in_=bounce[None, :, :]),
                    waits=maxwaits(after(tb1) + after(tb2)),
                    incs=[(s_bnc, 16)])
                sst["bounce_tok"] = after(tb3)

            def prep_brep(c):
                cpar = c % 2
                tk1 = prog.emit("tensor", lambda e: e.matmul(
                    pk1[:], ones_bf[:], brow[:, 0, :], start=True, stop=True),
                    waits=sst["bounce_tok"] + sst["pk1_last"])
                tkb = prog.emit("vector", lambda e, cpar=cpar: e.tensor_copy(
                    out=Brep[cpar][:].rearrange("p n t -> p (n t)"), in_=pk1[:]),
                    waits=after(tk1))
                sst["pk1_last"] = after(tkb)
                sst["brep_tok"] = after(tkb)

            def prep_crep(c):
                cpar = c % 2
                tk2 = prog.emit("tensor", lambda e: e.matmul(
                    pk1[:], ones_bf[:], brow[:, 1, :], start=True, stop=True),
                    waits=sst["bounce_tok"] + sst["pk1_last"])
                tkc = prog.emit("vector", lambda e, cpar=cpar: e.tensor_copy(
                    out=Crep[cpar][:].rearrange("p n t -> p (n t)"), in_=pk1[:]),
                    waits=after(tk2))
                sst["pk1_last"] = after(tkc)
                sst["brow_rd"] = after(tk2)

            def prep_dtx_qb(c):
                t0c = c * TC
                cpar = c % 2
                prog.emit("vector", lambda e, t0c=t0c: e.tensor_tensor(
                    out=dtx[:], in0=dtb[:, :, t0c:t0c + TC],
                    in1=xnT[:, :, t0c:t0c + TC], op=ALU.mult))
                prog.emit("vector", lambda e, t0c=t0c, cpar=cpar: e.tensor_tensor(
                    out=qb[cpar][:], in0=xnT[:, :, t0c:t0c + TC],
                    in1=dpT[:][:, :, None].to_broadcast([128, 8, TC]),
                    op=ALU.mult))

            def prep_uslice(c, j):
                cpar = c % 2
                prog.emit("vector", lambda e, j=j, cpar=cpar: e.tensor_tensor(
                    out=ub[:, j * 8:(j + 1) * 8, :]
                        .rearrange("p t (cc n) -> p t cc n", cc=8),
                    in0=dtx[:, :, j * 8:(j + 1) * 8]
                        .rearrange("p cc t -> p t cc")[:, :, :, None]
                        .to_broadcast([128, 8, 8, 16]),
                    in1=Brep[cpar][:, :, j * 8:(j + 1) * 8]
                        .rearrange("p n t -> p t n")[:, :, None, :]
                        .to_broadcast([128, 8, 8, 16]),
                    op=ALU.mult),
                    waits=sst["brep_tok"] if j == 0 else ())

            def prep_zcload(c):
                cpar = c % 2
                tzl = prog.emit("sync", lambda e, c=c, cpar=cpar: e.dma_start(
                    out=zcl[cpar][:].rearrange("p t f -> p (t f)"),
                    in_=zcd[c].rearrange("p t f -> p (t f)")),
                    waits=[(s_zcl[cpar], s_zcl[cpar].n),
                           (s_sc, sst.get("zc_rd", {}).get(cpar, 0))],
                    incs=[(s_zcl[cpar], 16)])
                sst["zcl_tok"][cpar] = after(tzl)

            for ch in range(NCH):
                par_c = ch % 2
                t0 = ch * TC
                gt0 = base + t0
                if ch == 0:
                    prep_bounce(0)
                    prep_brep(0)
                    prep_crep(0)
                    prep_dtx_qb(0)
                    for j in range(4):
                        prep_uslice(0, j)
                    yield 6000.0

                nxt = ch + 1
                extras = {}
                for slot, fn_ in zip((0, 2, 4, 5, 6), sst.pop("bdry_parts", [])):
                    extras.setdefault(slot, []).append(fn_)
                if nxt < NCH:
                    extras[1] = [lambda: prep_bounce(nxt)]
                    extras[3] = [lambda: prep_dtx_qb(nxt)]
                    extras.setdefault(10, []).append(lambda: prep_brep(nxt))
                    extras.setdefault(12, []).append(lambda: prep_crep(nxt))
                    for slot, j in zip((15, 21, 27, 31), range(4)):
                        extras.setdefault(slot, []).append(
                            lambda j=j: prep_uslice(nxt, j))

                if ch == 0:
                    tldx0 = prog.emit("gpsimd", lambda e, gt0=gt0: e.dma_start(
                        out=xrow[:], in_=src[gt0:gt0 + TC, :]),
                        waits=list(sst["xrow_last"]) + dep, incs=[(s_sst, 16)])
                    sst["xrow_tok"] = after(tldx0)

                dA_prev_tok = sst["dA_tok"]
                t_dap = prog.emit("scalar", lambda e, par_c=par_c: e.activation(
                    out=dAs[gt0 % 2][:],
                    in_=zcl[par_c][:, 0, :], func=AF.Exp, scale=16.0),
                    waits=[(s_v, dA_prev_tok.get(("use", gt0 % 2), 0))]
                          + sst["zcl_tok"].get(par_c, []))
                dA_prev_tok[("rdy", gt0 % 2)] = t_dap
                for ti in range(TC):
                    gstep = gt0 + ti
                    par = gstep % 2
                    rs = gstep % RING
                    prs = (gstep - 1) % RING
                    tha = prog.emit("vector", lambda e, prs=prs, par=par: e.tensor_tensor(
                        out=tmp[:], in0=hist[:, prs, :], in1=dAs[par][:],
                        op=ALU.mult),
                        waits=after(dA_prev_tok[("rdy", par)]))
                    dA_prev_tok[("use", par)] = s_v.n
                    if ti + 1 < TC:
                        t_dap = prog.emit("scalar", lambda e, ti=ti, par=par, par_c=par_c: e.activation(
                            out=dAs[1 - par][:],
                            in_=zcl[par_c][:, ti + 1, :], func=AF.Exp, scale=16.0),
                            waits=[(s_v, dA_prev_tok.get(("use", 1 - par), 0))])
                        dA_prev_tok[("rdy", 1 - par)] = t_dap
                    t1 = prog.emit("vector", lambda e, ti=ti, rs=rs: e.scalar_tensor_tensor(
                        out=hist[:, rs, :], in0=tmp[:],
                        scalar=shist2[:, ti:ti + 1],
                        op0=ALU.mult, in1=ub[:, ti, :], op1=ALU.add))
                    t3 = prog.emit("vector", lambda e, rs=rs: e.scalar_tensor_tensor(
                        out=sq_s[:], in0=hist[:, rs, :], scalar=1.0,
                        op0=ALU.mult, in1=hist[:, rs, :], op1=ALU.mult,
                        accum_out=p2[:]))
                    tcpb = prog.emit("vector", lambda e: e.tensor_copy(
                        out=p2b[:], in_=p2[:]))
                    t4 = prog.emit("tensor", lambda e, par=par: e.matmul(
                        pns[par][:, 0:1], o400[:], p2b[:], start=True, stop=True),
                        waits=after(tcpb) + [(s_sc, sst["pns_last"].get(par, 0))])
                    t5 = prog.emit("scalar", lambda e, par=par: e.activation(
                        out=s0[:, par:par + 1], in_=pns[par][:, 0:1], func=AF.Ln),
                        waits=after(t4))
                    t6 = prog.emit("scalar", lambda e, par=par: e.activation(
                        out=s0[:, par:par + 1], in_=s0[:, par:par + 1],
                        func=AF.Exp, scale=-0.5))
                    sst["pns_last"][par] = s_sc.n
                    prog.emit("vector", lambda e, ti=ti, par=par: e.tensor_scalar_min(
                        shist2[:, ti + 1:ti + 2], s0[:, par:par + 1], 1.0),
                        waits=after(t6))
                    if ti % 8 == 7:
                        k8 = ti // 8
                        tw1 = prog.emit("vector", lambda e, k8=k8, par_c=par_c: e.tensor_tensor(
                            out=zs[:].rearrange("p t (cc n) -> p t cc n", cc=8),
                            in0=hist[:].rearrange("p t (cc n) -> p t cc n", cc=8),
                            in1=Crep[par_c][:, :, k8 * 8:(k8 + 1) * 8]
                                .rearrange("p n t -> p t n")[:, :, None, :]
                                .to_broadcast([128, 8, 8, 16]),
                            op=ALU.mult))
                        tw2 = prog.emit("vector", lambda e: e.reduce_sum(
                            out=wred[:], in_=zs[:].rearrange("p t (cc n) -> p t cc n", cc=8),
                            axis=AX.X),
                            waits=after(tw1))
                        ty = prog.emit("vector", lambda e, k8=k8: e.tensor_tensor(
                            out=yb[:, :, k8 * 8:(k8 + 1) * 8].rearrange("p cc t -> p t cc"),
                            in0=wred[:],
                            in1=shist2[:, k8 * 8 + 1:(k8 + 1) * 8 + 1][:, :, None]
                                .to_broadcast([128, 8, 8]),
                            op=ALU.mult),
                            waits=after(tw2))
                        yield 1200.0
                    for fn_ in extras.get(ti, []):
                        fn_()
                    yield 2400.0
                sst.setdefault("zc_rd", {})[par_c] = s_sc.n
                if ch + 2 < NCH:
                    prep_zcload(ch + 2)

                def p_ty2(par_c=par_c):
                    ty2 = prog.emit("vector", lambda e, par_c=par_c: e.tensor_add(
                        yb[:], yb[:], qb[par_c][:]))
                    sst["ty2_tok"] = after(ty2)

                def p_trA():
                    ttr = None
                    for cd4 in range(4):
                        w_ = []
                        if cd4 == 0:
                            w_ = sst["ty2_tok"] + sst["pk1_last"]
                        ttr = prog.emit("tensor", lambda e, cd4=cd4: e.transpose(
                            pk1[0:TC, cd4 * 128:(cd4 + 1) * 128], yb[:, cd4, :],
                            identf[:]),
                            waits=w_)
                    txa = prog.emit("vector", lambda e: e.tensor_add(
                        xrow[:, 0:512], xrow[:, 0:512], pk1[0:TC, :]),
                        waits=after(ttr) + sst["xrow_tok"])
                    sst["txa0"] = after(txa)

                def p_trB():
                    ttr = None
                    for cd4 in range(4):
                        w_ = []
                        if cd4 == 0:
                            w_ = sst["txa0"]
                        ttr = prog.emit("tensor", lambda e, cd4=cd4: e.transpose(
                            pk1[0:TC, cd4 * 128:(cd4 + 1) * 128], yb[:, 4 + cd4, :],
                            identf[:]),
                            waits=w_)
                    txa = prog.emit("vector", lambda e: e.tensor_add(
                        xrow[:, 512:1024], xrow[:, 512:1024], pk1[0:TC, :]),
                        waits=after(ttr))
                    sst["pk1_last"] = after(txa)
                    sst["txa1"] = after(txa)

                def p_store(gt0=gt0):
                    tstr = prog.emit("gpsimd", lambda e, gt0=gt0: e.dma_start(
                        out=dst[gt0:gt0 + TC, :], in_=xrow[:]),
                        waits=sst["txa1"], incs=[(s_sst, 16)])
                    sst["xrow_last"] = after(tstr)

                def p_loadx(nt0=gt0 + TC):
                    tld2 = prog.emit("gpsimd", lambda e, nt0=nt0: e.dma_start(
                        out=xrow[:], in_=src[nt0:nt0 + TC, :]),
                        waits=list(sst["xrow_last"]) + dep, incs=[(s_sst, 16)])
                    sst["xrow_tok"] = after(tld2)

                prog.emit("vector", lambda e: e.tensor_copy(
                    out=shist2[:, 0:1], in_=shist2[:, TC:TC + 1]))
                if ch + 1 < NCH:
                    sst["bdry_parts"] = [p_ty2, p_trA, p_trB, p_store, p_loadx]
                else:
                    for fn_ in (p_ty2, p_trA, p_trB, p_store):
                        fn_()
                yield 1500.0
            sst.setdefault("blk_sst", {})[blk] = [(s_sst, s_sst.n)]

        wbT_s = sbuf("wbT_s", [128, 8, N16], BF16)
        wcT_s = sbuf("wcT_s", [128, 8, N16], BF16)
        for (wsrc, wdst) in ((WB, wbT_s), (WC, wcT_s)):
            tl = prog.emit("sync", lambda e, wsrc=wsrc: e.dma_start(
                out=xrow[0:16, :], in_=wsrc[:, :]),
                waits=[(s_ld, s_ld.n), (s_v, s_v.n)], incs=[(s_ld, 16)])
            for cd in range(8):
                tp = prog.emit("tensor", lambda e, cd=cd: e.transpose(
                    pmisc[:, 0:16], xrow[0:16, cd * 128:(cd + 1) * 128],
                    identf[0:16, 0:16]),
                    waits=after(tl) + [(s_v, s_v.n)])
                prog.emit("vector", lambda e, cd=cd, wdst=wdst: e.tensor_copy(
                    out=wdst[:, cd, :], in_=pmisc[:, 0:16]),
                    waits=after(tp))
        bc_const = [(s_v, s_v.n)]
        for _eng in ("tensor", "vector", "scalar", "sync", "gpsimd"):
            prog.emit(_eng, lambda e: None, waits=list(bc_const))

        use_ffn = "ffn1" in phases
        use_ssm = "ssm" in phases
        use_ffn2 = "ffn2" in phases
        src1, dst1 = xin, xr1
        src2 = xr1 if use_ffn else xin
        dst2 = xr2
        src3 = xr2 if use_ssm else src2
        dst3 = out

        def barrier():
            w = [(s_st[0], s_st[0].n), (s_st[1], s_st[1].n),
                 (s_sst, s_sst.n), (s_v, s_v.n), (s_sc, s_sc.n),
                 (s_pe, s_pe.n), (s_bnc, s_bnc.n), (s_sxt, s_sxt.n),
                 (s_wdt, s_wdt.n), (s_wdl, s_wdl.n)]
            w += [(s_ldx[i], s_ldx[i].n) for i in range(2)]
            w += [(s_ldw[i], s_ldw[i].n) for i in range(4)]
            for eng in ENGINES:
                prog.emit(eng, lambda e: None, waits=list(w))

        def stdep():
            return [(s_st[0], s_st[0].n), (s_st[1], s_st[1].n)]

        def sstdep():
            return [(s_sst, s_sst.n)]

        if use_ffn and use_ssm and use_ffn2:
            drain(ffn_block(src1, dst1, 0))
            drain(ffn_block(src1, dst1, 1))
            progress = {"ffn1": NBLK - 1, "ssm": -1}

            def f1_tail():
                for b in range(2, NBLK):
                    for c in ffn_block(src1, dst1, b):
                        yield c

            def ssm0_stream():
                for c in ssm_block(src2, dst2, 0, fst["blk_store"][0]):
                    yield c
                progress["ssm"] = 0

            merge([f1_tail(), ssm0_stream()])

            def ffn_stream():
                for b in range(NBLK):
                    while progress["ssm"] < b:
                        yield ("need", "ssm", b)
                    for c in ffn_block(src3, dst3, b, sst["blk_sst"][b]):
                        yield c

            def ssm_stream():
                for b in range(1, NBLK):
                    for c in ssm_block(src2, dst2, b, fst["blk_store"][b]):
                        yield c
                    progress["ssm"] = b

            g1, g2 = ffn_stream(), ssm_stream()
            clocks = [0.0, 0.0]
            alive = [True, True]
            blocked = [None, None]
            gens = [g1, g2]
            while any(alive):
                cands = []
                for j in range(2):
                    if not alive[j]:
                        continue
                    if blocked[j] is not None:
                        key, bb = blocked[j]
                        if progress[key] < bb:
                            continue
                        blocked[j] = None
                        clocks[j] = max(clocks)
                    cands.append(j)
                if not cands:
                    raise RuntimeError("scheduler deadlock")
                j = min(cands, key=lambda q: clocks[q])
                try:
                    c = next(gens[j])
                    if isinstance(c, tuple):
                        blocked[j] = (c[1], c[2])
                    else:
                        clocks[j] += (c or 0.0)
                except StopIteration:
                    alive[j] = False
        else:
            if use_ffn:
                for b in range(NBLK):
                    drain(ffn_block(src1, dst1, b))
                barrier()
            if use_ssm:
                for b in range(NBLK):
                    drain(ssm_block(src2, dst2, b, stdep()))
                barrier()
            if use_ffn2:
                for b in range(NBLK):
                    drain(ffn_block(src3, dst3, b, sstdep()))
            else:
                cp = sbuf("cpbuf", [128, D], F32)
                for r in range(0, T, 128):
                    tld = prog.emit("gpsimd", lambda e, r=r: e.dma_start(
                        out=cp[:], in_=src3[r:r + 128, :]),
                        waits=[(s_st[0], s_st[0].n)], incs=[(s_ldx[0], 16)])
                    prog.emit("gpsimd", lambda e, r=r: e.dma_start(
                        out=out[r:r + 128, :], in_=cp[:]),
                        waits=after(tld), incs=[(s_st[0], 16)])

        barrier()

        with nc.Block() as block:
            @block.sync
            def _(eng):
                prog.replay("sync", eng)

            @block.gpsimd
            def _(eng):
                prog.replay("gpsimd", eng)

            @block.tensor
            def _(eng):
                prog.replay("tensor", eng)

            @block.vector
            def _(eng):
                prog.replay("vector", eng)

            @block.scalar
            def _(eng):
                prog.replay("scalar", eng)

    return nc


_nc_cache = {}


def _get_nc(phases):
    key = tuple(phases)
    if key not in _nc_cache:
        _nc_cache[key] = build_program(phases)
    return _nc_cache[key]


def kernel(L0_attn_x, L0_attn_Wdt, L0_attn_bdt, L0_attn_WB, L0_attn_WC,
           L0_attn_A_log, L0_attn_Dp, L0_ffn_norm_weight, L0_ffn_router_w,
           L0_ffn_sw1, L0_ffn_sb1, L0_ffn_sw2, L0_ffn_sb2, L0_ffn_ew1,
           L0_ffn_eb1, L0_ffn_ew2, L0_ffn_eb2, _phases=("ffn1", "ssm", "ffn2")):
    x = np.ascontiguousarray(np.asarray(L0_attn_x, dtype=np.float32))
    W1 = np.concatenate([np.asarray(L0_ffn_ew1, dtype=np.float32),
                         np.asarray(L0_ffn_sw1, dtype=np.float32)[None]], axis=0)
    W2 = np.concatenate([np.asarray(L0_ffn_ew2, dtype=np.float32),
                         np.asarray(L0_ffn_sw2, dtype=np.float32)[None]], axis=0)
    b1c = np.concatenate([np.asarray(L0_ffn_eb1, dtype=np.float32),
                          np.asarray(L0_ffn_sb1, dtype=np.float32)[None]], axis=0)
    b2c = np.concatenate([np.asarray(L0_ffn_eb2, dtype=np.float32),
                          np.asarray(L0_ffn_sb2, dtype=np.float32)[None]], axis=0)
    common = {
        "Wdt": np.ascontiguousarray(L0_attn_Wdt, dtype=np.float32),
        "bdt": np.ascontiguousarray(np.asarray(L0_attn_bdt, dtype=np.float32).reshape(1, D)),
        "WB": np.ascontiguousarray(L0_attn_WB, dtype=np.float32),
        "WC": np.ascontiguousarray(L0_attn_WC, dtype=np.float32),
        "A_log": np.ascontiguousarray(L0_attn_A_log, dtype=np.float32),
        "Dp": np.ascontiguousarray(np.asarray(L0_attn_Dp, dtype=np.float32).reshape(1, D)),
        "nw": np.ascontiguousarray(np.asarray(L0_ffn_norm_weight, dtype=np.float32).reshape(1, D)),
        "rw": np.ascontiguousarray(L0_ffn_router_w, dtype=np.float32),
        "W1": np.ascontiguousarray(W1),
        "W2": np.ascontiguousarray(W2),
        "b1": np.ascontiguousarray(b1c),
        "b2": np.ascontiguousarray(b2c),
        "ident": np.eye(128, dtype=np.float32),
    }
    nc = _get_nc(_phases)
    in_maps = []
    for c in range(8):
        m = dict(common)
        m["x"] = np.ascontiguousarray(x[c])
        in_maps.append(m)
    res = run_bass_kernel_spmd(nc, in_maps, core_ids=list(range(8)))
    outs = [np.asarray(res.results[c]["out"]) for c in range(8)]
    return np.stack(outs, axis=0).astype(np.float32)


# version 11 (final)
# speedup vs baseline: 1.1221x; 1.0268x over previous
import numpy as np
from contextlib import ExitStack

import concourse.bass as bass
import concourse.mybir as mybir
from concourse.bass_utils import run_bass_kernel_spmd

F32 = mybir.dt.float32
BF16 = mybir.dt.bfloat16
AF = mybir.ActivationFunctionType
ALU = mybir.AluOpType
AX = mybir.AxisListType

T = 4096
D = 1024
H = 4096
NEXP = 9
N16 = 16
BT = 1024
NBLK = T // BT
NTT = BT // 128
NQ = 4
QH = H // NQ
TC = 32
NCH = BT // TC
GRP = 2
RING = 8
ENGINES = ("sync", "gpsimd", "tensor", "vector", "scalar")

K1DT = 0.5 / (0.0495 + 1e-8)
K2DT = 0.0495 / 0.5
CDT = 0.0505


class SemT:
    def __init__(self, h):
        self.h = h
        self.n = 0


class Prog:
    def __init__(self):
        self.events = []

    def set_selfsems(self, m):
        self._selfsems = m

    def emit(self, engine, fn, waits=(), incs=()):
        waits = list(waits)
        incs = list(incs)
        ent = getattr(self, "_selfsems", {}).get(engine)
        if ent is not None:
            gen, do_wait = ent
            assert not incs
            incs = [(gen, 1)]
            if do_wait:
                waits.append((gen, gen.n))
        post = []
        for s, k in incs:
            s.n += k
            post.append((s, s.n))
        assert len(incs) <= 1
        self.events.append((engine, fn, waits, incs))
        return post

    def replay(self, engine, eng):
        for (e, fn, waits, incs) in self.events:
            if e != engine:
                continue
            for (s, v) in waits:
                eng.wait_ge(s.h, v)
            inst = fn(eng)
            for (s, k) in incs:
                if inst is None:
                    eng.sem_inc(s.h, k)
                else:
                    inst.then_inc(s.h, k)


def after(tok):
    return [(s, v) for (s, v) in tok]


def maxwaits(toks):
    best = {}
    for (s, v) in toks:
        if id(s) not in best or best[id(s)][1] < v:
            best[id(s)] = (s, v)
    return list(best.values())


def merge(gens):
    clocks = [0.0] * len(gens)
    alive = [True] * len(gens)
    while any(alive):
        i = min((j for j in range(len(gens)) if alive[j]), key=lambda j: clocks[j])
        try:
            c = next(gens[i])
            clocks[i] += (c or 0.0)
        except StopIteration:
            alive[i] = False


def drain(gen):
    for _ in gen:
        pass


def build_program(phases=("ffn1", "ssm", "ffn2")):
    nc = bass.Bass("TRN2", target_bir_lowering=False)

    xin = nc.dram_tensor("x", [T, D], F32, kind="ExternalInput").ap()
    Wdt = nc.dram_tensor("Wdt", [D, D], F32, kind="ExternalInput").ap()
    bdt = nc.dram_tensor("bdt", [1, D], F32, kind="ExternalInput").ap()
    WB = nc.dram_tensor("WB", [N16, D], F32, kind="ExternalInput").ap()
    WC = nc.dram_tensor("WC", [N16, D], F32, kind="ExternalInput").ap()
    A_log = nc.dram_tensor("A_log", [D, N16], F32, kind="ExternalInput").ap()
    Dp = nc.dram_tensor("Dp", [1, D], F32, kind="ExternalInput").ap()
    nw = nc.dram_tensor("nw", [1, D], F32, kind="ExternalInput").ap()
    rw = nc.dram_tensor("rw", [8, D], F32, kind="ExternalInput").ap()
    W1 = nc.dram_tensor("W1", [NEXP, D, H], F32, kind="ExternalInput").ap()
    W2 = nc.dram_tensor("W2", [NEXP, H, D], F32, kind="ExternalInput").ap()
    b1 = nc.dram_tensor("b1", [NEXP, H], F32, kind="ExternalInput").ap()
    b2 = nc.dram_tensor("b2", [NEXP, D], F32, kind="ExternalInput").ap()
    ident_in = nc.dram_tensor("ident", [128, 128], F32, kind="ExternalInput").ap()

    out = nc.dram_tensor("out", [T, D], F32, kind="ExternalOutput").ap()
    xr1 = nc.dram_tensor("xr1", [T, D], F32, kind="Internal").ap()
    xr2 = nc.dram_tensor("xr2", [T, D], F32, kind="Internal").ap()
    bounce = nc.dram_tensor("bounce", [2, N16 * TC], BF16, kind="Internal").ap()
    w1bf = nc.dram_tensor("w1bf", [NEXP, NQ, 128, 8, QH], BF16, kind="Internal").ap()
    w2bf = nc.dram_tensor("w2bf", [NEXP, NQ, 128, 8, D], BF16, kind="Internal").ap()
    wdtd = nc.dram_tensor("wdtd", [8, 128, 8, 128], BF16, kind="Internal").ap()
    zcd = nc.dram_tensor("zcd", [NCH, 128, TC, 128], BF16, kind="Internal").ap()

    prog = Prog()
    ctx = ExitStack()

    def sbuf(name, shape, dt):
        return ctx.enter_context(nc.sbuf_tensor(name, shape, dt))

    def psum(name, shape, dt=F32):
        return ctx.enter_context(nc.psum_tensor(name, shape, dt))

    def sem(name):
        return SemT(ctx.enter_context(nc.semaphore(name)))

    with ctx:
        identf = sbuf("identf", [128, 128], F32)
        ones_row = sbuf("ones_row", [1, 128], F32)
        ones_bf = sbuf("ones_bf", [1, 128], BF16)
        o400 = sbuf("o400", [128, 128], BF16)
        one_col = sbuf("one_col", [128, 1], F32)
        tanh_bias = sbuf("tanh_bias", [128, 1], F32)
        rwTf = sbuf("rwTf", [128, 8, 8], F32)
        nwT = sbuf("nwT", [128, 8], F32)
        eb1T = sbuf("eb1T", [128, 32, NEXP], F32)
        eb2x = sbuf("eb2x", [NEXP, D], BF16)
        a1t = sbuf("a1t", [128, 128], F32)
        dpT = sbuf("dpT", [128, 8], F32)
        bdtT = sbuf("bdtT", [128, 8], F32)

        s_pe = sem("s_pe")
        s_v = sem("s_v")
        s_sc = sem("s_sc")
        s_ld = sem("s_ld")
        s_ldx = [sem("s_ldx0"), sem("s_ldx1")]
        s_sxt = sem("s_sxt")
        s_ldw = [sem("s_ldw0"), sem("s_ldw1"), sem("s_ldw2"), sem("s_ldw3")]
        s_wdt = sem("s_wdt")
        s_wdl = sem("s_wdl")
        s_zcs = [sem("s_zcs0"), sem("s_zcs1")]
        s_zcl = [sem("s_zcl0"), sem("s_zcl1")]
        s_st = [sem("s_st0"), sem("s_st1")]
        s_sst = sem("s_sst")
        s_bnc = sem("s_bnc")
        s_pw = [sem(f"s_pw{i}") for i in range(4)]
        s_ps = [sem(f"s_ps{i}") for i in range(4)]
        prog.set_selfsems({"vector": (s_v, True), "scalar": (s_sc, True),
                           "tensor": (s_pe, False)})

        ph = [psum("ph0", [128, 512]), psum("ph1", [128, 512])]
        po = [psum("po0", [128, 512]), psum("po1", [128, 512])]
        pns = [psum("pns0", [128, 512]), psum("pns1", [128, 512])]
        pmisc = psum("pmisc", [128, 512])
        pk1 = psum("pk1", [128, 512])
        pr8 = po[0][:, 0:8]
        pt9 = po[1][0:9, 0:128]

        p0 = ExitStack()
        stg = p0.enter_context(nc.sbuf_tensor("p0stage", [128, 1024], F32))
        stgw = p0.enter_context(nc.sbuf_tensor("p0stagew", [16, 1024], F32))
        stgb = p0.enter_context(nc.sbuf_tensor("p0stageb", [128, 1024], BF16))
        stgb1 = p0.enter_context(nc.sbuf_tensor("p0stgb1", [9, 4096], F32))

        t_id = prog.emit("sync", lambda e: e.dma_start(out=identf[:], in_=ident_in[:, :]),
                         waits=[(s_ld, s_ld.n)], incs=[(s_ld, 16)])
        prog.emit("vector", lambda e: e.memset(ones_row[:], 1.0))
        prog.emit("vector", lambda e: e.memset(ones_bf[:], 1.0))
        prog.emit("vector", lambda e: e.memset(o400[:], 1.0 / 400.0))
        prog.emit("vector", lambda e: e.memset(one_col[:], 1.0))
        t_ms = prog.emit("vector", lambda e: e.memset(tanh_bias[:], -CDT * K1DT))

        def row_to_dmajor(src_row, dst):
            tl = prog.emit("sync", lambda e: e.dma_start(
                out=stgw[0:8, 0:128],
                in_=src_row.rearrange("o (c p) -> (o c) p", c=8)),
                waits=[(s_ld, s_ld.n), (s_pe, s_pe.n)], incs=[(s_ld, 16)])
            tp = prog.emit("tensor", lambda e: e.transpose(
                pmisc[:, 0:8], stgw[0:8, 0:128], identf[0:8, 0:8]),
                waits=after(tl) + after(t_id) + [(s_v, s_v.n)])
            return prog.emit("vector", lambda e: e.tensor_copy(out=dst, in_=pmisc[:, 0:8]),
                             waits=after(tp))

        tok_nw = row_to_dmajor(nw, nwT[:, :])
        tok_dp = row_to_dmajor(Dp, dpT[:, :])
        tok_bdt = row_to_dmajor(bdt, bdtT[:, :])

        tl = prog.emit("sync", lambda e: e.dma_start(
            out=a1t[:].rearrange("p (c n) -> p c n", c=8),
            in_=A_log.rearrange("(c p) n -> p c n", p=128)),
            waits=[(s_ld, s_ld.n)], incs=[(s_ld, 16)])
        tsa = prog.emit("scalar", lambda e: e.activation(
            out=a1t[:], in_=a1t[:], func=AF.Tanh, scale=0.1),
            waits=after(tl))
        tsb = prog.emit("scalar", lambda e: e.activation(
            out=a1t[:], in_=a1t[:], func=AF.Exp, scale=10.0),
            waits=after(tsa))
        tok_a1 = prog.emit("vector", lambda e: e.tensor_scalar_mul(a1t[:], a1t[:], -1.0),
                           waits=after(tsb))

        tok_rwT = []
        tl = prog.emit("sync", lambda e: e.dma_start(out=stgw[0:8, :], in_=rw[:, :]),
                       waits=[(s_ld, s_ld.n), (s_pe, s_pe.n)], incs=[(s_ld, 16)])
        for cd in range(8):
            tp = prog.emit("tensor", lambda e, cd=cd: e.transpose(
                pmisc[:, 0:8], stgw[0:8, cd * 128:(cd + 1) * 128], identf[0:8, 0:8]),
                waits=after(tl) + after(tok_nw) + [(s_v, s_v.n)])
            tok_rwT.append(prog.emit("vector", lambda e, cd=cd: e.tensor_scalar(
                out=rwTf[:, cd, :], in0=pmisc[:, 0:8],
                scalar1=nwT[:, cd:cd + 1], scalar2=None, op0=ALU.mult),
                waits=after(tp)))

        tl_b1 = prog.emit("sync", lambda e: e.dma_start(out=stgb1[:, :], in_=b1[:, :]),
                          waits=[(s_ld, s_ld.n), (s_pe, s_pe.n)], incs=[(s_ld, 16)])
        tok_eb1 = []
        for ch in range(32):
            tp = prog.emit("tensor", lambda e, ch=ch: e.transpose(
                pmisc[:, 0:9], stgb1[0:9, ch * 128:(ch + 1) * 128], identf[0:9, 0:9]),
                waits=after(tl_b1) + [(s_v, s_v.n)])
            tok_eb1.append(prog.emit("vector", lambda e, ch=ch: e.tensor_copy(
                out=eb1T[:, ch, :], in_=pmisc[:, 0:9]),
                waits=after(tp)))
        tl_b2 = prog.emit("sync", lambda e: e.dma_start(out=stgw[0:9, :], in_=b2[:, :]),
                          waits=[(s_ld, s_ld.n), (s_pe, s_pe.n), (s_v, s_v.n)],
                          incs=[(s_ld, 16)])
        tok_eb2 = prog.emit("vector", lambda e: e.tensor_copy(out=eb2x[:, :], in_=stgw[0:9, :]),
                            waits=after(tl_b2))

        wdt_st_tok = []
        for co in range(8):
            tl = prog.emit("sync", lambda e, co=co: e.dma_start(
                out=stg[:], in_=Wdt[co * 128:(co + 1) * 128, :]),
                waits=[(s_ld, s_ld.n), (s_v, s_v.n), (s_pe, s_pe.n)],
                incs=[(s_ld, 16)])
            tcps = []
            for cd in range(8):
                tp = prog.emit("tensor", lambda e, cd=cd: e.transpose(
                    pmisc[:, 0:128], stg[:, cd * 128:(cd + 1) * 128], identf[:]),
                    waits=after(tl) + [(s_v, s_v.n)])
                tcps.append(prog.emit("vector", lambda e, cd=cd: e.tensor_copy(
                    out=stgb[:, cd * 128:(cd + 1) * 128], in_=pmisc[:, 0:128]),
                    waits=after(tp) + (wdt_st_tok if cd == 0 else [])))
            wdt_st_tok = after(prog.emit("gpsimd", lambda e, co=co: e.dma_start(
                out=wdtd[co].rearrange("p c d -> p (c d)"), in_=stgb[:]),
                waits=after(tcps[-1]), incs=[(s_wdt, 16)]))
        wdt_done = s_wdt.n

        pstg = [p0.enter_context(nc.sbuf_tensor(f"pstg{i}", [128, 4, 1024], F32)) for i in range(4)]
        pcv = [p0.enter_context(nc.sbuf_tensor(f"pcv{i}", [128, 4, 1024], BF16)) for i in range(4)]
        last_conv = {}
        last_store = {}
        for sl in range(NEXP):
            for q in range(NQ):
                for g4 in range(2):
                    chn = g4 % 2
                    wld = prog.emit("sync", lambda e, sl=sl, q=q, g4=g4, chn=chn: e.dma_start(
                        out=pstg[chn][:],
                        in_=W1[sl][g4 * 512:(g4 + 1) * 512, q * QH:(q + 1) * QH]
                            .rearrange("(a p) h -> p a h", p=128)),
                        waits=last_conv.get(chn, []), incs=[(s_pw[chn], 16)])
                    w_ = after(wld) + last_store.get(chn, [])
                    if sl == 0 and q == 0 and g4 == 0:
                        w_ += after(tok_nw)
                    tcv = None
                    for a in range(4):
                        tcv = prog.emit("vector", lambda e, g4=g4, a=a, chn=chn: e.tensor_scalar(
                            out=pcv[chn][:, a, :], in0=pstg[chn][:, a, :],
                            scalar1=nwT[:, g4 * 4 + a:g4 * 4 + a + 1],
                            scalar2=None, op0=ALU.mult),
                            waits=w_ if a == 0 else ())
                    last_conv[chn] = after(tcv)
                    tst = prog.emit("gpsimd", lambda e, sl=sl, q=q, g4=g4, chn=chn: e.dma_start(
                        out=w1bf[sl, q, :, g4 * 4:(g4 + 1) * 4, :], in_=pcv[chn][:]),
                        waits=after(tcv), incs=[(s_ps[chn], 16)])
                    last_store[chn] = after(tst)
                for g4 in range(2):
                    chn = 2 + g4 % 2
                    wld = prog.emit("sync", lambda e, sl=sl, q=q, g4=g4, chn=chn: e.dma_start(
                        out=pstg[chn][:],
                        in_=W2[sl][q * QH + g4 * 512:q * QH + (g4 + 1) * 512, :]
                            .rearrange("(a p) h -> p a h", p=128)),
                        waits=last_conv.get(chn, []), incs=[(s_pw[chn], 16)])
                    tcv = prog.emit("scalar", lambda e, chn=chn: e.activation(
                        out=pcv[chn][:].rearrange("p a h -> p (a h)"),
                        in_=pstg[chn][:].rearrange("p a h -> p (a h)"), func=AF.Copy),
                        waits=after(wld) + last_store.get(chn, []))
                    last_conv[chn] = after(tcv)
                    tst = prog.emit("gpsimd", lambda e, sl=sl, q=q, g4=g4, chn=chn: e.dma_start(
                        out=w2bf[sl, q, :, g4 * 4:(g4 + 1) * 4, :], in_=pcv[chn][:]),
                        waits=after(tcv), incs=[(s_ps[chn], 16)])
                    last_store[chn] = after(tst)

        allc = (after(tok_a1) + after(tok_dp) + after(tok_bdt) + after(tok_eb2)
                + [(s_wdt, wdt_done)]
                + [(s_ps[i], s_ps[i].n) for i in range(4)]
                + [(s_pw[i], s_pw[i].n) for i in range(4)]
                + [(s_v, s_v.n), (s_sc, s_sc.n), (s_pe, s_pe.n), (s_ld, s_ld.n)])
        for t_ in tok_rwT + tok_eb1:
            allc += after(t_)
        const_waits = maxwaits(allc)
        for _eng in ENGINES:
            prog.emit(_eng, lambda e: None, waits=list(const_waits))
        p0.close()

        nrmT = sbuf("nrmT", [128, 8, BT], BF16)
        nrmTf = sbuf("nrmTf", [128, 8, 128], F32)
        accb = sbuf("accb", [128, NTT, D], BF16)
        g = sbuf("gq", [128, 8, BT], BF16)
        w1q = sbuf("w1q", [128, 8, QH], BF16)
        w2q = sbuf("w2q", [128, 8, D], BF16)
        xt = [sbuf("xt0", [128, D], F32), sbuf("xt1", [128, D], F32)]
        nrm_s = sbuf("nrm_s", [128, D], F32)
        stats = sbuf("stats", [128, 12], F32)
        logit = sbuf("logit", [128, NTT, 8], F32)
        mx8 = sbuf("mx8", [128, NTT, 8], F32)
        s12 = sbuf("s12", [128, NTT, 2], F32)
        wts = sbuf("wts", [128, NTT, 4], F32)
        msk1 = sbuf("msk1", [128, NTT, 8], F32)
        msk2 = sbuf("msk2", [128, NTT, 8], F32)
        comb = sbuf("comb", [128, NTT, NEXP], F32)
        combT = sbuf("combT", [NEXP, BT], BF16)

        fst = {
            "xt_last": {},
            "evh": {},
            "evo": {},
            "pe_q": {},
            "qc": 0,
            "w1tok": None, "w2tok": None,
            "lnv": {},
        }

        def ffn_block(src, dst, blk, dep=()):
            dep = list(dep)
            base = blk * BT

            def flush():
                for fn_ in fst.get("pending", []):
                    fn_()
                fst["pending"] = []

            def defer(fn_):
                fst.setdefault("pending", []).append(fn_)
            flush()
            tok_tcp = None
            tok_tlg = None
            for tt in range(NTT):
                xb = xt[tt % 2]
                row0 = base + tt * 128
                w_ = list(fst["xt_last"].get(tt % 2, [])) + dep
                tld = prog.emit("gpsimd", lambda e, xb=xb, row0=row0: e.dma_start(
                    out=xb[:], in_=src[row0:row0 + 128, :]),
                    waits=w_, incs=[(s_ldx[tt % 2], 16)])
                tsum = prog.emit("vector", lambda e, xb=xb: e.reduce_sum(
                    out=stats[:, 0:1], in_=xb[:], axis=AX.X),
                    waits=after(tld))
                tsq = prog.emit("scalar", lambda e, xb=xb: e.activation(
                    out=nrm_s[:], in_=xb[:], func=AF.Square,
                    accum_out=stats[:, 1:2]),
                    waits=after(tld) + [(s_v, s_v.n)])
                tv = prog.emit("vector", lambda e: e.tensor_scalar_mul(
                    stats[:, 2:3], stats[:, 0:1], 1.0 / D), waits=after(tsum))
                tv = prog.emit("vector", lambda e: e.tensor_scalar_mul(
                    stats[:, 3:4], stats[:, 1:2], 1.0 / D), waits=after(tsq))
                tv = prog.emit("vector", lambda e: e.tensor_mul(
                    stats[:, 4:5], stats[:, 2:3], stats[:, 2:3]))
                tv = prog.emit("vector", lambda e: e.tensor_sub(
                    stats[:, 5:6], stats[:, 3:4], stats[:, 4:5]))
                tv = prog.emit("vector", lambda e: e.tensor_scalar_add(
                    stats[:, 6:7], stats[:, 5:6], 1e-5))
                tv = prog.emit("vector", lambda e: e.reciprocal(
                    out=stats[:, 7:8], in_=stats[:, 6:7]))
                tv = prog.emit("vector", lambda e: e.tensor_mul(
                    stats[:, 8:9], stats[:, 5:6], stats[:, 7:8]))
                tv = prog.emit("vector", lambda e: e.tensor_scalar_add(
                    stats[:, 9:10], stats[:, 8:9], 1e-8))
                tv9 = prog.emit("vector", lambda e: e.tensor_mul(
                    stats[:, 10:11], stats[:, 6:7], stats[:, 9:10]))
                tl1 = prog.emit("scalar", lambda e: e.activation(
                    out=stats[:, 11:12], in_=stats[:, 10:11], func=AF.Ln),
                    waits=after(tv9))
                tl2 = prog.emit("scalar", lambda e: e.activation(
                    out=stats[:, 10:11], in_=stats[:, 11:12], func=AF.Exp,
                    scale=-0.5))
                tnrm = prog.emit("vector", lambda e, xb=xb: e.scalar_tensor_tensor(
                    out=nrm_s[:], in0=xb[:], scalar=stats[:, 2:3],
                    in1=stats[:, 10:11].to_broadcast([128, D]),
                    op0=ALU.subtract, op1=ALU.mult),
                    waits=after(tl2))
                fst["xt_last"][tt % 2] = after(tnrm) + after(tsq)
                ttr = {}
                for cdx in range(8):
                    pb = po[cdx // 4]
                    w_ = after(tnrm)
                    if cdx == 0:
                        for par in (0, 1):
                            if par in fst["evo"]:
                                w_.append((s_v, fst["evo"][par]))
                        if tok_tlg is not None:
                            w_ += after(tok_tlg)
                    ttr[cdx] = prog.emit("tensor", lambda e, cdx=cdx, pb=pb: e.transpose(
                        pb[:, (cdx % 4) * 128:(cdx % 4 + 1) * 128],
                        nrm_s[:, cdx * 128:(cdx + 1) * 128], identf[:]),
                        waits=w_)
                tok_tcp = prog.emit("vector", lambda e, tt=tt: e.tensor_copy(
                    out=nrmT[:, 0:4, tt * 128:(tt + 1) * 128],
                    in_=po[0][:].rearrange("p (c f) -> p c f", c=4)),
                    waits=after(ttr[3]))
                tok_tcp2 = prog.emit("vector", lambda e, tt=tt: e.tensor_copy(
                    out=nrmT[:, 4:8, tt * 128:(tt + 1) * 128],
                    in_=po[1][:].rearrange("p (c f) -> p c f", c=4)),
                    waits=after(ttr[7]))
                tcpf = prog.emit("vector", lambda e: e.tensor_copy(
                    out=nrmTf[:, 0:4, :],
                    in_=po[0][:].rearrange("p (c f) -> p c f", c=4)))
                tcpf2 = prog.emit("vector", lambda e: e.tensor_copy(
                    out=nrmTf[:, 4:8, :],
                    in_=po[1][:].rearrange("p (c f) -> p c f", c=4)))
                trt = None
                for cd in range(8):
                    w_ = after(tcpf2) if cd == 0 else ()
                    trt = prog.emit("tensor", lambda e, cd=cd: e.matmul(
                        pr8, nrmTf[:, cd, :], rwTf[:, cd, :],
                        start=(cd == 0), stop=(cd == 7)),
                        waits=w_)
                tok_tlg = prog.emit("vector", lambda e, tt=tt: e.tensor_copy(
                    out=logit[:, tt, :], in_=pr8),
                    waits=after(trt))
                prog.emit("vector", lambda e, tt=tt: e.max(
                    out=mx8[:, tt, :], in_=logit[:, tt, :]))
                fst["evo"][0] = s_v.n
                fst["evo"][1] = s_v.n
                yield 2600.0

            te = prog.emit("scalar", lambda e: e.activation(
                out=s12[:, :, :], in_=mx8[:, :, 0:2], func=AF.Exp, scale=-1.0),
                waits=[(s_v, s_v.n)])
            tv = prog.emit("vector", lambda e: e.tensor_scalar_add(
                s12[:, :, :], s12[:, :, :], 1.0), waits=after(te))
            tv = prog.emit("vector", lambda e: e.reciprocal(
                out=s12[:, :, :], in_=s12[:, :, :]))
            tv = prog.emit("vector", lambda e: e.tensor_add(
                wts[:, :, 0], s12[:, :, 0], s12[:, :, 1]))
            tv = prog.emit("vector", lambda e: e.tensor_scalar_add(
                wts[:, :, 0], wts[:, :, 0], 1e-6))
            tv = prog.emit("vector", lambda e: e.reciprocal(
                out=wts[:, :, 1], in_=wts[:, :, 0]))
            tv = prog.emit("vector", lambda e: e.tensor_mul(
                wts[:, :, 2], s12[:, :, 0], wts[:, :, 1]))
            tv = prog.emit("vector", lambda e: e.tensor_mul(
                wts[:, :, 3], s12[:, :, 1], wts[:, :, 1]))
            tv = prog.emit("vector", lambda e: e.tensor_tensor(
                out=msk1[:], in0=logit[:],
                in1=mx8[:, :, 0:1].to_broadcast([128, NTT, 8]), op=ALU.is_ge))
            tv = prog.emit("vector", lambda e: e.tensor_tensor(
                out=msk2[:], in0=logit[:],
                in1=mx8[:, :, 1:2].to_broadcast([128, NTT, 8]), op=ALU.is_ge))
            tv = prog.emit("vector", lambda e: e.tensor_sub(
                msk2[:], msk2[:], msk1[:]))
            tv = prog.emit("vector", lambda e: e.tensor_tensor(
                out=msk1[:], in0=msk1[:],
                in1=wts[:, :, 2:3].to_broadcast([128, NTT, 8]), op=ALU.mult))
            tv = prog.emit("vector", lambda e: e.tensor_tensor(
                out=msk2[:], in0=msk2[:],
                in1=wts[:, :, 3:4].to_broadcast([128, NTT, 8]), op=ALU.mult))
            tv = prog.emit("vector", lambda e: e.tensor_add(
                comb[:, :, 0:8], msk1[:], msk2[:]))
            tcomb = prog.emit("vector", lambda e: e.memset(comb[:, :, 8:9], 1.0))
            yield 2000.0

            for tt in range(NTT):
                w_ = [(s_v, s_v.n)]
                tp = prog.emit("tensor", lambda e, tt=tt: e.transpose(
                    pt9, comb[:, tt, :], identf[:]), waits=w_)
                tcc = prog.emit("vector", lambda e, tt=tt: e.tensor_copy(
                    out=combT[:, tt * 128:(tt + 1) * 128], in_=pt9),
                    waits=after(tp))
                tbs = {}
                for hf in range(2):
                    w_ = after(tcc)
                    if hf == 0 and (0 in fst["evo"]):
                        w_.append((s_v, fst["evo"][0]))
                    if hf == 1 and (1 in fst["evo"]):
                        w_.append((s_v, fst["evo"][1]))
                    tbs[hf] = prog.emit("tensor", lambda e, tt=tt, hf=hf: e.matmul(
                        po[hf][:], combT[:, tt * 128:(tt + 1) * 128],
                        eb2x[:, hf * 512:(hf + 1) * 512],
                        start=True, stop=True),
                        waits=w_)
                for hf in range(2):
                    tev = prog.emit("vector", lambda e, tt=tt, hf=hf: e.tensor_copy(
                        out=accb[:, tt, hf * 512:(hf + 1) * 512], in_=po[hf][:]),
                        waits=after(tbs[hf]))
                    fst["evo"][hf] = s_v.n
                yield 1500.0

            for sl in range(NEXP):
                for q in range(NQ):
                    qc = fst["qc"]
                    fst["qc"] += 1
                    wpd = []
                    t_w1 = prog.emit("sync", lambda e, sl=sl, q=q: e.dma_start(
                        out=w1q[:], in_=w1bf[sl, q]),
                        waits=[(s_pe, fst["pe_q"].get(("mm1", qc - 1), 0))] + wpd,
                        incs=[(s_ldw[qc % 2], 16)])
                    t_w2 = prog.emit("sync", lambda e, sl=sl, q=q: e.dma_start(
                        out=w2q[:], in_=w2bf[sl, q]),
                        waits=[(s_pe, fst["pe_q"].get(("mm2", qc - 1), 0))],
                        incs=[(s_ldw[2 + qc % 2], 16)])
                    fst["w1tok"] = after(t_w1)
                    fst["w2tok"] = after(t_w2)

                    for i in range(8):
                        for t2 in range(2):
                            par = (i * 2 + t2) % 2
                            hb = ph[par]
                            flush()
                            mmlast = None
                            for c in range(8):
                                w_ = []
                                if c == 0:
                                    if i == 0 and t2 == 0:
                                        w_ += fst["w1tok"]
                                    if ("h", par) in fst["evh"]:
                                        w_.append((s_sc, fst["evh"][("h", par)]))
                                mmlast = prog.emit("tensor",
                                    lambda e, i=i, c=c, t2=t2, hb=hb: e.matmul(
                                        hb[:],
                                        w1q[:, c, i * 128:(i + 1) * 128],
                                        nrmT[:, c, t2 * 512:(t2 + 1) * 512],
                                        start=(c == 0), stop=(c == 7)),
                                    waits=w_)
                                if c == 3:
                                    yield 900.0

                            def ev1(i=i, t2=t2, hb=hb, q=q, sl=sl, par=par, mmlast=mmlast):
                                prog.emit("scalar", lambda e: e.activation(
                                    out=g[:, i, t2 * 512:(t2 + 1) * 512],
                                    in_=hb[:], func=AF.Relu,
                                    bias=eb1T[:, q * 8 + i, sl:sl + 1]),
                                    waits=after(mmlast))
                                fst["evh"][("h", par)] = s_sc.n
                            defer(ev1)
                            yield 1100.0
                    flush()
                    fst["pe_q"][("mm1", qc)] = s_pe.n
                    g_ready = s_sc.n

                    for tt in range(NTT):
                        for hf in range(2):
                            par = (tt * 2 + hf) % 2
                            ob = po[par]
                            flush()
                            mmlast = None
                            for i in range(8):
                                w_ = []
                                if i == 0:
                                    if tt == 0 and hf == 0:
                                        w_ = [(s_sc, g_ready)] + fst["w2tok"]
                                    if par in fst["evo"]:
                                        w_.append((s_v, fst["evo"][par]))
                                mmlast = prog.emit("tensor",
                                    lambda e, tt=tt, i=i, hf=hf, ob=ob: e.matmul(
                                        ob[:],
                                        g[:, i, tt * 128:(tt + 1) * 128],
                                        w2q[:, i, hf * 512:(hf + 1) * 512],
                                        start=(i == 0), stop=(i == 7)),
                                    waits=w_)
                                if i == 3:
                                    yield 900.0

                            def ev2(tt=tt, hf=hf, ob=ob, sl=sl, par=par, mmlast=mmlast):
                                prog.emit("vector", lambda e: e.scalar_tensor_tensor(
                                    out=accb[:, tt, hf * 512:(hf + 1) * 512], in0=ob[:],
                                    scalar=comb[:, tt, sl:sl + 1], op0=ALU.mult,
                                    in1=accb[:, tt, hf * 512:(hf + 1) * 512], op1=ALU.add),
                                    waits=after(mmlast))
                                fst["evo"][par] = s_v.n
                            defer(ev2)
                            yield 1100.0
                    flush()
                    fst["pe_q"][("mm2", qc)] = s_pe.n

            acc_done = s_v.n
            for tt in range(NTT):
                xb = xt[tt % 2]
                row0 = base + tt * 128
                w_ = list(fst["xt_last"].get(tt % 2, [])) + dep
                tld = prog.emit("gpsimd", lambda e, xb=xb, row0=row0: e.dma_start(
                    out=xb[:], in_=src[row0:row0 + 128, :]),
                    waits=w_, incs=[(s_ldx[tt % 2], 16)])
                tap = prog.emit("vector", lambda e, xb=xb, tt=tt: e.scalar_tensor_tensor(
                    out=xb[:], in0=accb[:, tt, :], scalar=0.5,
                    op0=ALU.mult, in1=xb[:], op1=ALU.add),
                    waits=after(tld) + ([(s_v, acc_done)] if tt == 0 else []))
                tstr = prog.emit("gpsimd", lambda e, xb=xb, row0=row0: e.dma_start(
                    out=dst[row0:row0 + 128, :], in_=xb[:]),
                    waits=after(tap), incs=[(s_st[tt % 2], 16)])
                fst["xt_last"][tt % 2] = after(tap) + after(tstr)
                yield 800.0
            fst.setdefault("blk_store", {})[blk] = [
                (s_st[0], s_st[0].n), (s_st[1], s_st[1].n)]

        xnT = sbuf("xnT", [128, 8, BT], BF16)
        dtb = sbuf("dtb", [128, 8, BT], BF16)
        zcl = [sbuf("zcl0", [128, TC, 128], BF16), sbuf("zcl1", [128, TC, 128], BF16)]
        zs = sbuf("zs", [128, 8, 128], F32)
        dAs = [sbuf("dAs0", [128, 128], F32), sbuf("dAs1", [128, 128], F32)]
        ub = sbuf("ub", [128, TC, 128], BF16)
        hist = sbuf("hist", [128, RING, 128], F32)
        dtx = sbuf("dtx", [128, 8, TC], F32)
        qb = [sbuf("qb0", [128, 8, TC], F32), sbuf("qb1", [128, 8, TC], F32)]
        wred = sbuf("wred", [128, 8, 8], F32)
        yb = sbuf("yb", [128, 8, TC], F32)
        Bn = sbuf("Bn", [16, BT], BF16)
        Cn = sbuf("Cn", [16, BT], BF16)
        Brep = [sbuf("Brep0", [128, N16, TC], F32), sbuf("Brep1", [128, N16, TC], F32)]
        Crep = [sbuf("Crep0", [128, N16, TC], F32), sbuf("Crep1", [128, N16, TC], F32)]
        brow = sbuf("brow", [1, 2, N16 * TC], BF16)
        shist2 = sbuf("shist2", [128, TC + 1], F32)
        sq_s = sbuf("sq_s", [128, 128], F32)
        p2 = sbuf("p2", [128, 1], F32)
        p2b = sbuf("p2b", [128, 1], BF16)
        s0 = sbuf("s0", [128, 2], F32)
        xrow = sbuf("xrow", [TC, D], F32)
        sxt = sbuf("sxt", [128, D], F32)
        snrm = sbuf("snrm", [128, D], F32)
        sstats = sbuf("sstats", [128, 12], F32)
        tmp = sbuf("stmp", [128, 128], F32)
        wdtr = [sbuf("wdtr0", [128, 8, 128], BF16)]

        progress_ssmp = [0]
        sst = {
            "init": False,
            "sxt_last": [],
            "xrow_last": [],
            "pdt_last": {},
            "bc_last": {},
            "bnc_ch": [],
            "wdt_last": {},
            "pk1_last": [],
            "pns_last": {},
            "lnp_last": [],
            "dA_tok": {},
        }

        def ssm_init():
            prog.emit("vector", lambda e: e.memset(hist[:, RING - 1, :], 0.0))
            prog.emit("vector", lambda e: e.memset(shist2[:, 0:1], 1.0))
            sst["init"] = True

        def ssm_block(src, dst, blk, dep=()):
            dep = list(dep)
            if not sst["init"]:
                ssm_init()
            base = blk * BT
            for tt in range(NTT):
                row0 = base + tt * 128
                tld = prog.emit("gpsimd", lambda e, row0=row0: e.dma_start(
                    out=sxt[:], in_=src[row0:row0 + 128, :]),
                    waits=list(sst["sxt_last"]) + dep, incs=[(s_sxt, 16)])
                tsum = prog.emit("vector", lambda e: e.reduce_sum(
                    out=sstats[:, 0:1], in_=sxt[:], axis=AX.X),
                    waits=after(tld))
                tsq = prog.emit("scalar", lambda e: e.activation(
                    out=snrm[:], in_=sxt[:], func=AF.Square,
                    accum_out=sstats[:, 1:2]),
                    waits=after(tld) + [(s_v, s_v.n)])
                tv = prog.emit("vector", lambda e: e.tensor_scalar_mul(
                    sstats[:, 2:3], sstats[:, 0:1], 1.0 / D), waits=after(tsum))
                tv = prog.emit("vector", lambda e: e.tensor_scalar_mul(
                    sstats[:, 3:4], sstats[:, 1:2], 1.0 / D), waits=after(tsq))
                tv = prog.emit("vector", lambda e: e.tensor_mul(
                    sstats[:, 4:5], sstats[:, 2:3], sstats[:, 2:3]))
                tv = prog.emit("vector", lambda e: e.tensor_sub(
                    sstats[:, 5:6], sstats[:, 3:4], sstats[:, 4:5]))
                tv5 = prog.emit("vector", lambda e: e.tensor_scalar_add(
                    sstats[:, 6:7], sstats[:, 5:6], 1e-5))
                tl1 = prog.emit("scalar", lambda e: e.activation(
                    out=sstats[:, 11:12], in_=sstats[:, 6:7], func=AF.Ln),
                    waits=after(tv5))
                tl2 = prog.emit("scalar", lambda e: e.activation(
                    out=sstats[:, 10:11], in_=sstats[:, 11:12], func=AF.Exp,
                    scale=-0.5))
                tnrm = prog.emit("vector", lambda e: e.scalar_tensor_tensor(
                    out=snrm[:], in0=sxt[:], scalar=sstats[:, 2:3],
                    in1=sstats[:, 10:11].to_broadcast([128, D]),
                    op0=ALU.subtract, op1=ALU.mult),
                    waits=after(tl2))
                sst["sxt_last"] = after(tnrm) + after(tsq)
                ttr = {}
                for cdx in range(4):
                    w_ = after(tnrm)
                    if cdx == 0:
                        w_ = list(w_) + sst["lnp_last"] + [(s_sc, sst["pdt_last"].get("any", 0))]
                    ttr[cdx] = prog.emit("tensor", lambda e, cdx=cdx: e.transpose(
                        pmisc[:, cdx * 128:(cdx + 1) * 128],
                        snrm[:, cdx * 128:(cdx + 1) * 128], identf[:]),
                        waits=w_)
                tc1 = prog.emit("vector", lambda e, tt=tt: e.tensor_copy(
                    out=xnT[:, 0:4, tt * 128:(tt + 1) * 128],
                    in_=pmisc[:].rearrange("p (c f) -> p c f", c=4)),
                    waits=after(ttr[3]))
                for cdx in range(4, 8):
                    w_ = after(tc1) if cdx == 4 else ()
                    ttr[cdx] = prog.emit("tensor", lambda e, cdx=cdx: e.transpose(
                        pmisc[:, (cdx % 4) * 128:(cdx % 4 + 1) * 128],
                        snrm[:, cdx * 128:(cdx + 1) * 128], identf[:]),
                        waits=w_)
                tc2 = prog.emit("vector", lambda e, tt=tt: e.tensor_copy(
                    out=xnT[:, 4:8, tt * 128:(tt + 1) * 128],
                    in_=pmisc[:].rearrange("p (c f) -> p c f", c=4)),
                    waits=after(ttr[7]))
                sst["lnp_last"] = after(tc2)
                yield 2600.0
            xn_v = s_v.n

            for co in range(8):
                wv = wdtr[0]
                twl = prog.emit("sync", lambda e, co=co, wv=wv: e.dma_start(
                    out=wv[:].rearrange("p c d -> p (c d)"),
                    in_=wdtd[co].rearrange("p c d -> p (c d)")),
                    waits=[(s_pe, sst["wdt_last"].get(0, 0)),
                           (s_wdl, s_wdl.n)],
                    incs=[(s_wdl, 16)])
                for tq in range(2):
                    mmlast = None
                    for cd in range(8):
                        w_ = []
                        if cd == 0:
                            w_ = [(s_v, xn_v)] + after(twl)
                            w_.append((s_sc, sst["pdt_last"].get("any", 0)))
                            w_ += sst["lnp_last"]
                        mmlast = prog.emit("tensor", lambda e, co=co, cd=cd, tq=tq, wv=wv: e.matmul(
                            pmisc[:], wv[:, cd, :],
                            xnT[:, cd, tq * 512:(tq + 1) * 512],
                            start=(cd == 0), stop=(cd == 7)),
                            waits=w_)
                    tso = prog.emit("scalar", lambda e, co=co: e.activation(
                        out=pmisc[:], in_=pmisc[:], func=AF.Exp,
                        bias=bdtT[:, co:co + 1]),
                        waits=after(mmlast))
                    tso2 = prog.emit("scalar", lambda e, co=co, tq=tq: e.activation(
                        out=dtb[:, co, tq * 512:(tq + 1) * 512], in_=pmisc[:],
                        func=AF.Ln, bias=one_col[:]),
                        waits=after(tso))
                    sst["pdt_last"]["any"] = s_sc.n
                    sst["lnp_last"] = []
                    yield 1800.0
                sst["wdt_last"][0] = s_pe.n

            for (wTs, dstN) in ((wbT_s, Bn), (wcT_s, Cn)):
                for tq in range(2):
                    mmlast = None
                    for cd in range(8):
                        w_ = []
                        if cd == 0:
                            w_ = [(s_sc, sst["pdt_last"]["any"]), (s_v, s_v.n)]
                            w_ += [(s_bnc, s_bnc.n)]
                        mmlast = prog.emit("tensor", lambda e, cd=cd, tq=tq, wTs=wTs: e.matmul(
                            pmisc[0:16, :], wTs[:, cd, :],
                            xnT[:, cd, tq * 512:(tq + 1) * 512],
                            start=(cd == 0), stop=(cd == 7)),
                            waits=w_)
                    tbc = prog.emit("vector", lambda e, tq=tq, dstN=dstN: e.tensor_copy(
                        out=dstN[:, tq * 512:(tq + 1) * 512], in_=pmisc[0:16, :]),
                        waits=after(mmlast))
                    sst["bc_last"]["any"] = s_v.n
                    yield 1200.0

            tth = prog.emit("scalar", lambda e: e.activation(
                out=dtb[:].rearrange("p c f -> p (c f)"),
                in_=dtb[:].rearrange("p c f -> p (c f)"),
                func=AF.Tanh, scale=K1DT, bias=tanh_bias[:]),
                waits=[(s_v, s_v.n)])
            tdt = prog.emit("vector", lambda e: e.tensor_scalar(
                out=dtb[:].rearrange("p c f -> p (c f)"),
                in0=dtb[:].rearrange("p c f -> p (c f)"),
                scalar1=K2DT, scalar2=CDT, op0=ALU.mult, op1=ALU.add),
                waits=after(tth))
            yield 8000.0
            ztan_tok = {}
            for c in range(NCH):
                cpar = c % 2
                ttn = None
                for sl4 in range(TC // 4):
                    hpar = sl4 % 2
                    w_ = after(tdt) if (c == 0 and sl4 == 0) else []
                    w_ = w_ + ztan_tok.get(hpar, [])
                    tz = prog.emit("vector", lambda e, c=c, sl4=sl4, hpar=hpar: e.tensor_tensor(
                        out=zs[:, hpar * 4:(hpar + 1) * 4, :]
                            .rearrange("p t (cc n) -> p t cc n", cc=8),
                        in0=dtb[:, :, c * TC + sl4 * 4:c * TC + (sl4 + 1) * 4]
                            .rearrange("p cc t -> p t cc")[:, :, :, None]
                            .to_broadcast([128, 4, 8, 16]),
                        in1=a1t[:].rearrange("p (cc n) -> p cc n", cc=8)[:, None, :, :]
                            .to_broadcast([128, 4, 8, 16]),
                        op=ALU.mult),
                        waits=w_)
                    w2_ = after(tz)
                    if sl4 <= 1:
                        w2_ = w2_ + sst.get("zcs_war", {}).get(cpar, [])
                    ttn = prog.emit("scalar", lambda e, cpar=cpar, sl4=sl4, hpar=hpar: e.activation(
                        out=zcl[cpar][:, sl4 * 4:(sl4 + 1) * 4, :]
                            .rearrange("p t f -> p (t f)"),
                        in_=zs[:, hpar * 4:(hpar + 1) * 4, :]
                            .rearrange("p t f -> p (t f)"),
                        func=AF.Tanh, scale=1.0 / 16.0),
                        waits=w2_)
                    ztan_tok[hpar] = after(ttn)
                tzs = prog.emit("gpsimd", lambda e, c=c, cpar=cpar: e.dma_start(
                    out=zcd[c].rearrange("p t f -> p (t f)"),
                    in_=zcl[cpar][:].rearrange("p t f -> p (t f)")),
                    waits=after(ttn) + [(s_zcs[cpar], s_zcs[cpar].n)],
                    incs=[(s_zcs[cpar], 16)])
                sst.setdefault("zcs_war", {})[cpar] = after(tzs)
                yield 2200.0
            sst["zcl_tok"] = {}
            for c in (0, 1):
                cpar = c % 2
                tzl = prog.emit("sync", lambda e, c=c, cpar=cpar: e.dma_start(
                    out=zcl[cpar][:].rearrange("p t f -> p (t f)"),
                    in_=zcd[c].rearrange("p t f -> p (t f)")),
                    waits=[(s_zcs[cpar], s_zcs[cpar].n),
                           (s_zcl[cpar], s_zcl[cpar].n)],
                    incs=[(s_zcl[cpar], 16)])
                sst["zcl_tok"][cpar] = after(tzl)

            def prep_bounce(c):
                t0c = c * TC
                tb1 = prog.emit("sync", lambda e, t0c=t0c: e.dma_start(
                    out=bounce[0, :].rearrange("(n t) -> n t", n=16),
                    in_=Bn[:, t0c:t0c + TC]),
                    waits=[(s_bnc, s_bnc.n),
                           (s_v, sst["bc_last"].get("any", 0))]
                          + sst.get("brow_rd", []),
                    incs=[(s_bnc, 16)])
                tb2 = prog.emit("sync", lambda e, t0c=t0c: e.dma_start(
                    out=bounce[1, :].rearrange("(n t) -> n t", n=16),
                    in_=Cn[:, t0c:t0c + TC]),
                    waits=(), incs=[(s_bnc, 16)])
                tb3 = prog.emit("sync", lambda e: e.dma_start(
                    out=brow[:, :, :], in_=bounce[None, :, :]),
                    waits=maxwaits(after(tb1) + after(tb2)),
                    incs=[(s_bnc, 16)])
                sst["bounce_tok"] = after(tb3)

            def prep_brep(c):
                cpar = c % 2
                tk1 = prog.emit("tensor", lambda e: e.matmul(
                    pk1[:], ones_bf[:], brow[:, 0, :], start=True, stop=True),
                    waits=sst["bounce_tok"] + sst["pk1_last"])
                tkb = prog.emit("vector", lambda e, cpar=cpar: e.tensor_copy(
                    out=Brep[cpar][:].rearrange("p n t -> p (n t)"), in_=pk1[:]),
                    waits=after(tk1))
                sst["pk1_last"] = after(tkb)
                sst["brep_tok"] = after(tkb)

            def prep_crep(c):
                cpar = c % 2
                tk2 = prog.emit("tensor", lambda e: e.matmul(
                    pk1[:], ones_bf[:], brow[:, 1, :], start=True, stop=True),
                    waits=sst["bounce_tok"] + sst["pk1_last"])
                tkc = prog.emit("vector", lambda e, cpar=cpar: e.tensor_copy(
                    out=Crep[cpar][:].rearrange("p n t -> p (n t)"), in_=pk1[:]),
                    waits=after(tk2))
                sst["pk1_last"] = after(tkc)
                sst["brow_rd"] = after(tk2)

            def prep_dtx_qb(c):
                t0c = c * TC
                cpar = c % 2
                prog.emit("vector", lambda e, t0c=t0c: e.tensor_tensor(
                    out=dtx[:], in0=dtb[:, :, t0c:t0c + TC],
                    in1=xnT[:, :, t0c:t0c + TC], op=ALU.mult))
                prog.emit("vector", lambda e, t0c=t0c, cpar=cpar: e.tensor_tensor(
                    out=qb[cpar][:], in0=xnT[:, :, t0c:t0c + TC],
                    in1=dpT[:][:, :, None].to_broadcast([128, 8, TC]),
                    op=ALU.mult))

            def prep_uslice(c, j):
                cpar = c % 2
                prog.emit("vector", lambda e, j=j, cpar=cpar: e.tensor_tensor(
                    out=ub[:, j * 8:(j + 1) * 8, :]
                        .rearrange("p t (cc n) -> p t cc n", cc=8),
                    in0=dtx[:, :, j * 8:(j + 1) * 8]
                        .rearrange("p cc t -> p t cc")[:, :, :, None]
                        .to_broadcast([128, 8, 8, 16]),
                    in1=Brep[cpar][:, :, j * 8:(j + 1) * 8]
                        .rearrange("p n t -> p t n")[:, :, None, :]
                        .to_broadcast([128, 8, 8, 16]),
                    op=ALU.mult),
                    waits=sst["brep_tok"] if j == 0 else ())

            def prep_zcload(c):
                cpar = c % 2
                tzl = prog.emit("sync", lambda e, c=c, cpar=cpar: e.dma_start(
                    out=zcl[cpar][:].rearrange("p t f -> p (t f)"),
                    in_=zcd[c].rearrange("p t f -> p (t f)")),
                    waits=[(s_zcl[cpar], s_zcl[cpar].n),
                           (s_sc, sst.get("zc_rd", {}).get(cpar, 0))],
                    incs=[(s_zcl[cpar], 16)])
                sst["zcl_tok"][cpar] = after(tzl)

            progress_ssmp[0] = blk + 1
            for ch in range(NCH):
                par_c = ch % 2
                t0 = ch * TC
                gt0 = base + t0
                if ch == 0:
                    prep_bounce(0)
                    prep_brep(0)
                    prep_crep(0)
                    prep_dtx_qb(0)
                    for j in range(4):
                        prep_uslice(0, j)
                    yield 6000.0

                nxt = ch + 1
                extras = {}
                for slot, fn_ in zip((0, 2, 4, 5, 6), sst.pop("bdry_parts", [])):
                    extras.setdefault(slot, []).append(fn_)
                if nxt < NCH:
                    extras[1] = [lambda: prep_bounce(nxt)]
                    extras[3] = [lambda: prep_dtx_qb(nxt)]
                    extras.setdefault(10, []).append(lambda: prep_brep(nxt))
                    extras.setdefault(12, []).append(lambda: prep_crep(nxt))
                    for slot, j in zip((15, 21, 27, 31), range(4)):
                        extras.setdefault(slot, []).append(
                            lambda j=j: prep_uslice(nxt, j))

                if ch == 0:
                    tldx0 = prog.emit("gpsimd", lambda e, gt0=gt0: e.dma_start(
                        out=xrow[:], in_=src[gt0:gt0 + TC, :]),
                        waits=list(sst["xrow_last"]) + dep, incs=[(s_sst, 16)])
                    sst["xrow_tok"] = after(tldx0)

                dA_prev_tok = sst["dA_tok"]
                t_dap = prog.emit("scalar", lambda e, par_c=par_c: e.activation(
                    out=dAs[gt0 % 2][:],
                    in_=zcl[par_c][:, 0, :], func=AF.Exp, scale=16.0),
                    waits=[(s_v, dA_prev_tok.get(("use", gt0 % 2), 0))]
                          + sst["zcl_tok"].get(par_c, []))
                dA_prev_tok[("rdy", gt0 % 2)] = t_dap
                for ti in range(TC):
                    gstep = gt0 + ti
                    par = gstep % 2
                    rs = gstep % RING
                    prs = (gstep - 1) % RING
                    tha = prog.emit("vector", lambda e, prs=prs, par=par: e.tensor_tensor(
                        out=tmp[:], in0=hist[:, prs, :], in1=dAs[par][:],
                        op=ALU.mult),
                        waits=after(dA_prev_tok[("rdy", par)]))
                    dA_prev_tok[("use", par)] = s_v.n
                    if ti + 1 < TC:
                        t_dap = prog.emit("scalar", lambda e, ti=ti, par=par, par_c=par_c: e.activation(
                            out=dAs[1 - par][:],
                            in_=zcl[par_c][:, ti + 1, :], func=AF.Exp, scale=16.0),
                            waits=[(s_v, dA_prev_tok.get(("use", 1 - par), 0))])
                        dA_prev_tok[("rdy", 1 - par)] = t_dap
                    t1 = prog.emit("vector", lambda e, ti=ti, rs=rs: e.scalar_tensor_tensor(
                        out=hist[:, rs, :], in0=tmp[:],
                        scalar=shist2[:, ti:ti + 1],
                        op0=ALU.mult, in1=ub[:, ti, :], op1=ALU.add))
                    t3 = prog.emit("vector", lambda e, rs=rs: e.scalar_tensor_tensor(
                        out=sq_s[:], in0=hist[:, rs, :], scalar=1.0,
                        op0=ALU.mult, in1=hist[:, rs, :], op1=ALU.mult,
                        accum_out=p2[:]))
                    tcpb = prog.emit("vector", lambda e: e.tensor_copy(
                        out=p2b[:], in_=p2[:]))
                    t4 = prog.emit("tensor", lambda e, par=par: e.matmul(
                        pns[par][:, 0:1], o400[:], p2b[:], start=True, stop=True),
                        waits=after(tcpb) + [(s_sc, sst["pns_last"].get(par, 0))])
                    t5 = prog.emit("scalar", lambda e, par=par: e.activation(
                        out=s0[:, par:par + 1], in_=pns[par][:, 0:1], func=AF.Ln),
                        waits=after(t4))
                    t6 = prog.emit("scalar", lambda e, par=par: e.activation(
                        out=s0[:, par:par + 1], in_=s0[:, par:par + 1],
                        func=AF.Exp, scale=-0.5))
                    sst["pns_last"][par] = s_sc.n
                    prog.emit("vector", lambda e, ti=ti, par=par: e.tensor_scalar_min(
                        shist2[:, ti + 1:ti + 2], s0[:, par:par + 1], 1.0),
                        waits=after(t6))
                    if ti % 8 == 7:
                        k8 = ti // 8
                        tw1 = prog.emit("vector", lambda e, k8=k8, par_c=par_c: e.tensor_tensor(
                            out=zs[:].rearrange("p t (cc n) -> p t cc n", cc=8),
                            in0=hist[:].rearrange("p t (cc n) -> p t cc n", cc=8),
                            in1=Crep[par_c][:, :, k8 * 8:(k8 + 1) * 8]
                                .rearrange("p n t -> p t n")[:, :, None, :]
                                .to_broadcast([128, 8, 8, 16]),
                            op=ALU.mult))
                        tw2 = prog.emit("vector", lambda e: e.reduce_sum(
                            out=wred[:], in_=zs[:].rearrange("p t (cc n) -> p t cc n", cc=8),
                            axis=AX.X),
                            waits=after(tw1))
                        ty = prog.emit("vector", lambda e, k8=k8: e.tensor_tensor(
                            out=yb[:, :, k8 * 8:(k8 + 1) * 8].rearrange("p cc t -> p t cc"),
                            in0=wred[:],
                            in1=shist2[:, k8 * 8 + 1:(k8 + 1) * 8 + 1][:, :, None]
                                .to_broadcast([128, 8, 8]),
                            op=ALU.mult),
                            waits=after(tw2))
                        yield 1200.0
                    for fn_ in extras.get(ti, []):
                        fn_()
                    yield 2400.0
                sst.setdefault("zc_rd", {})[par_c] = s_sc.n
                if ch + 2 < NCH:
                    prep_zcload(ch + 2)

                def p_ty2(par_c=par_c):
                    ty2 = prog.emit("vector", lambda e, par_c=par_c: e.tensor_add(
                        yb[:], yb[:], qb[par_c][:]))
                    sst["ty2_tok"] = after(ty2)

                def p_trA():
                    ttr = None
                    for cd4 in range(4):
                        w_ = []
                        if cd4 == 0:
                            w_ = sst["ty2_tok"] + sst["pk1_last"]
                        ttr = prog.emit("tensor", lambda e, cd4=cd4: e.transpose(
                            pk1[0:TC, cd4 * 128:(cd4 + 1) * 128], yb[:, cd4, :],
                            identf[:]),
                            waits=w_)
                    txa = prog.emit("vector", lambda e: e.tensor_add(
                        xrow[:, 0:512], xrow[:, 0:512], pk1[0:TC, :]),
                        waits=after(ttr) + sst["xrow_tok"])
                    sst["txa0"] = after(txa)

                def p_trB():
                    ttr = None
                    for cd4 in range(4):
                        w_ = []
                        if cd4 == 0:
                            w_ = sst["txa0"]
                        ttr = prog.emit("tensor", lambda e, cd4=cd4: e.transpose(
                            pk1[0:TC, cd4 * 128:(cd4 + 1) * 128], yb[:, 4 + cd4, :],
                            identf[:]),
                            waits=w_)
                    txa = prog.emit("vector", lambda e: e.tensor_add(
                        xrow[:, 512:1024], xrow[:, 512:1024], pk1[0:TC, :]),
                        waits=after(ttr))
                    sst["pk1_last"] = after(txa)
                    sst["txa1"] = after(txa)

                def p_store(gt0=gt0):
                    tstr = prog.emit("gpsimd", lambda e, gt0=gt0: e.dma_start(
                        out=dst[gt0:gt0 + TC, :], in_=xrow[:]),
                        waits=sst["txa1"], incs=[(s_sst, 16)])
                    sst["xrow_last"] = after(tstr)

                def p_loadx(nt0=gt0 + TC):
                    tld2 = prog.emit("gpsimd", lambda e, nt0=nt0: e.dma_start(
                        out=xrow[:], in_=src[nt0:nt0 + TC, :]),
                        waits=list(sst["xrow_last"]) + dep, incs=[(s_sst, 16)])
                    sst["xrow_tok"] = after(tld2)

                prog.emit("vector", lambda e: e.tensor_copy(
                    out=shist2[:, 0:1], in_=shist2[:, TC:TC + 1]))
                if ch + 1 < NCH:
                    sst["bdry_parts"] = [p_ty2, p_trA, p_trB, p_store, p_loadx]
                else:
                    for fn_ in (p_ty2, p_trA, p_trB, p_store):
                        fn_()
                yield 1500.0
            sst.setdefault("blk_sst", {})[blk] = [(s_sst, s_sst.n)]

        wbT_s = sbuf("wbT_s", [128, 8, N16], BF16)
        wcT_s = sbuf("wcT_s", [128, 8, N16], BF16)
        for (wsrc, wdst) in ((WB, wbT_s), (WC, wcT_s)):
            tl = prog.emit("sync", lambda e, wsrc=wsrc: e.dma_start(
                out=xrow[0:16, :], in_=wsrc[:, :]),
                waits=[(s_ld, s_ld.n), (s_v, s_v.n)], incs=[(s_ld, 16)])
            for cd in range(8):
                tp = prog.emit("tensor", lambda e, cd=cd: e.transpose(
                    pmisc[:, 0:16], xrow[0:16, cd * 128:(cd + 1) * 128],
                    identf[0:16, 0:16]),
                    waits=after(tl) + [(s_v, s_v.n)])
                prog.emit("vector", lambda e, cd=cd, wdst=wdst: e.tensor_copy(
                    out=wdst[:, cd, :], in_=pmisc[:, 0:16]),
                    waits=after(tp))
        bc_const = [(s_v, s_v.n)]
        for _eng in ("tensor", "vector", "scalar", "sync", "gpsimd"):
            prog.emit(_eng, lambda e: None, waits=list(bc_const))

        use_ffn = "ffn1" in phases
        use_ssm = "ssm" in phases
        use_ffn2 = "ffn2" in phases
        src1, dst1 = xin, xr1
        src2 = xr1 if use_ffn else xin
        dst2 = xr2
        src3 = xr2 if use_ssm else src2
        dst3 = out

        def barrier():
            w = [(s_st[0], s_st[0].n), (s_st[1], s_st[1].n),
                 (s_sst, s_sst.n), (s_v, s_v.n), (s_sc, s_sc.n),
                 (s_pe, s_pe.n), (s_bnc, s_bnc.n), (s_sxt, s_sxt.n),
                 (s_wdt, s_wdt.n), (s_wdl, s_wdl.n)]
            w += [(s_ldx[i], s_ldx[i].n) for i in range(2)]
            w += [(s_ldw[i], s_ldw[i].n) for i in range(4)]
            for eng in ENGINES:
                prog.emit(eng, lambda e: None, waits=list(w))

        def stdep():
            return [(s_st[0], s_st[0].n), (s_st[1], s_st[1].n)]

        def sstdep():
            return [(s_sst, s_sst.n)]

        if use_ffn and use_ssm and use_ffn2:
            drain(ffn_block(src1, dst1, 0))
            drain(ffn_block(src1, dst1, 1))
            progress = {"ffn1": NBLK - 1, "ssm": -1, "ssmp": 0}

            def f1_tail():
                for b in range(2, NBLK):
                    for c in ffn_block(src1, dst1, b):
                        yield c

            def ssm0_stream():
                for c in ssm_block(src2, dst2, 0, fst["blk_store"][0]):
                    yield c
                progress["ssm"] = 0

            merge([f1_tail(), ssm0_stream()])

            def ffn_stream():
                for b in range(NBLK):
                    while progress["ssmp"] < min(b + 2, NBLK):
                        yield ("need", "ssmp", min(b + 2, NBLK))
                    while progress["ssm"] < b:
                        yield ("need", "ssm", b)
                    for c in ffn_block(src3, dst3, b, sst["blk_sst"][b]):
                        yield c

            def ssm_stream():
                for b in range(1, NBLK):
                    for c in ssm_block(src2, dst2, b, fst["blk_store"][b]):
                        yield c
                    progress["ssm"] = b

            class _P(dict):
                def __setitem__(self, k, v):
                    super().__setitem__(k, v)
            progress["ssmp"] = progress_ssmp[0]

            def refresh():
                progress["ssmp"] = progress_ssmp[0]

            g1, g2 = ffn_stream(), ssm_stream()
            clocks = [0.0, 0.0]
            alive = [True, True]
            blocked = [None, None]
            gens = [g1, g2]
            while any(alive):
                cands = []
                refresh()
                for j in range(2):
                    if not alive[j]:
                        continue
                    if blocked[j] is not None:
                        key, bb = blocked[j]
                        if progress[key] < bb:
                            continue
                        blocked[j] = None
                        clocks[j] = max(clocks)
                    cands.append(j)
                if not cands:
                    raise RuntimeError("scheduler deadlock")
                j = min(cands, key=lambda q: clocks[q])
                try:
                    c = next(gens[j])
                    if isinstance(c, tuple):
                        blocked[j] = (c[1], c[2])
                    else:
                        clocks[j] += (c or 0.0)
                except StopIteration:
                    alive[j] = False
        else:
            if use_ffn:
                for b in range(NBLK):
                    drain(ffn_block(src1, dst1, b))
                barrier()
            if use_ssm:
                for b in range(NBLK):
                    drain(ssm_block(src2, dst2, b, stdep()))
                barrier()
            if use_ffn2:
                for b in range(NBLK):
                    drain(ffn_block(src3, dst3, b, sstdep()))
            else:
                cp = sbuf("cpbuf", [128, D], F32)
                for r in range(0, T, 128):
                    tld = prog.emit("gpsimd", lambda e, r=r: e.dma_start(
                        out=cp[:], in_=src3[r:r + 128, :]),
                        waits=[(s_st[0], s_st[0].n)], incs=[(s_ldx[0], 16)])
                    prog.emit("gpsimd", lambda e, r=r: e.dma_start(
                        out=out[r:r + 128, :], in_=cp[:]),
                        waits=after(tld), incs=[(s_st[0], 16)])

        barrier()

        with nc.Block() as block:
            @block.sync
            def _(eng):
                prog.replay("sync", eng)

            @block.gpsimd
            def _(eng):
                prog.replay("gpsimd", eng)

            @block.tensor
            def _(eng):
                prog.replay("tensor", eng)

            @block.vector
            def _(eng):
                prog.replay("vector", eng)

            @block.scalar
            def _(eng):
                prog.replay("scalar", eng)

    return nc


_nc_cache = {}


def _get_nc(phases):
    key = tuple(phases)
    if key not in _nc_cache:
        _nc_cache[key] = build_program(phases)
    return _nc_cache[key]


def kernel(L0_attn_x, L0_attn_Wdt, L0_attn_bdt, L0_attn_WB, L0_attn_WC,
           L0_attn_A_log, L0_attn_Dp, L0_ffn_norm_weight, L0_ffn_router_w,
           L0_ffn_sw1, L0_ffn_sb1, L0_ffn_sw2, L0_ffn_sb2, L0_ffn_ew1,
           L0_ffn_eb1, L0_ffn_ew2, L0_ffn_eb2, _phases=("ffn1", "ssm", "ffn2")):
    x = np.ascontiguousarray(np.asarray(L0_attn_x, dtype=np.float32))
    W1 = np.concatenate([np.asarray(L0_ffn_ew1, dtype=np.float32),
                         np.asarray(L0_ffn_sw1, dtype=np.float32)[None]], axis=0)
    W2 = np.concatenate([np.asarray(L0_ffn_ew2, dtype=np.float32),
                         np.asarray(L0_ffn_sw2, dtype=np.float32)[None]], axis=0)
    b1c = np.concatenate([np.asarray(L0_ffn_eb1, dtype=np.float32),
                          np.asarray(L0_ffn_sb1, dtype=np.float32)[None]], axis=0)
    b2c = np.concatenate([np.asarray(L0_ffn_eb2, dtype=np.float32),
                          np.asarray(L0_ffn_sb2, dtype=np.float32)[None]], axis=0)
    common = {
        "Wdt": np.ascontiguousarray(L0_attn_Wdt, dtype=np.float32),
        "bdt": np.ascontiguousarray(np.asarray(L0_attn_bdt, dtype=np.float32).reshape(1, D)),
        "WB": np.ascontiguousarray(L0_attn_WB, dtype=np.float32),
        "WC": np.ascontiguousarray(L0_attn_WC, dtype=np.float32),
        "A_log": np.ascontiguousarray(L0_attn_A_log, dtype=np.float32),
        "Dp": np.ascontiguousarray(np.asarray(L0_attn_Dp, dtype=np.float32).reshape(1, D)),
        "nw": np.ascontiguousarray(np.asarray(L0_ffn_norm_weight, dtype=np.float32).reshape(1, D)),
        "rw": np.ascontiguousarray(L0_ffn_router_w, dtype=np.float32),
        "W1": np.ascontiguousarray(W1),
        "W2": np.ascontiguousarray(W2),
        "b1": np.ascontiguousarray(b1c),
        "b2": np.ascontiguousarray(b2c),
        "ident": np.eye(128, dtype=np.float32),
    }
    nc = _get_nc(_phases)
    in_maps = []
    for c in range(8):
        m = dict(common)
        m["x"] = np.ascontiguousarray(x[c])
        in_maps.append(m)
    res = run_bass_kernel_spmd(nc, in_maps, core_ids=list(range(8)))
    outs = [np.asarray(res.results[c]["out"]) for c in range(8)]
    return np.stack(outs, axis=0).astype(np.float32)
